# version 47
# baseline (speedup 1.0000x reference)
import numpy as np
import ml_dtypes
from math import comb

import concourse.bass as bass
import concourse.mybir as mybir
from concourse.bass_utils import run_bass_kernel_spmd
from concourse.bass_core_rust import IndirectOffsetOnAxis

F32 = mybir.dt.float32
BF16 = mybir.dt.bfloat16
I32 = mybir.dt.int32
AF = mybir.ActivationFunctionType
ALU = mybir.AluOpType
AXX = mybir.AxisListType.X
BF = ml_dtypes.bfloat16


class Cfg:
    def __init__(self, B=2, S=8192, D=2048, H=16, INTER=8192):
        self.B, self.S, self.D, self.H, self.INTER = B, S, D, H, INTER
        self.HD = D // H
        assert self.HD == 128 and D % 512 == 0 and INTER % 512 == 0
        self.E3 = 3 * D
        self.G = S // 16
        assert self.G % 128 == 0 and self.G <= 512
        self.NK = self.G // 128
        self.TOWN = 4 * self.G
        self.TT = self.TOWN // 128
        self.DC = D // 128
        self.PADS = [4, 8, 12, 16]
        self.C = max(16, int(INTER * D / 128.9))
        self.NS = INTER * D
        self.A = self.NS // 128
        self.NCH = self.A // 128
        self.NG = (self.NCH + 127) // 128
        self.WINW = 136
        self.WINR = 6
        self.PAD_LO, self.PAD_HI = 2, 16
        self.CPP = self.C + self.PAD_LO + self.PAD_HI
        self.dt = float(np.float64(self.C - 1.0) / np.float64(self.NS - 1))
        self.SG = D // 128
        self.SD = INTER // 128
        self.TB = min(2048, self.TOWN)
        self.NTB = self.TOWN // self.TB
        self.IC = 512
        self.NIC = INTER // self.IC
        self.HC = 256
        self.NHC = D // self.HC
        self.SPT = min(4, self.TT)
        self.NSP = self.TT // self.SPT
        self.KT_GRP = min(2, self.NK) if self.NK > 1 else 1
        while self.NK % self.KT_GRP:
            self.KT_GRP -= 1
        self.ESH = 10.0
        self.EC = self.E3 // 512
        self.MC = D // 512
        self.NPASS = 4 + sum(self.PADS)

    def zig_chunks(self, r):
        return [r, 7 - r, 8 + r, 15 - r]

    def pos_row(self):
        pos = {}
        for r in range(4):
            for s, cid in enumerate(self.zig_chunks(r)):
                pos[cid] = r * self.TOWN + s * self.G
        return pos


CRB = 0.5 * np.array([
    [0., 2., 0., 0.], [-1., 0., 1., 0.], [2., -5., 4., -1.], [-1., 3., -3., 1.],
], dtype=np.float64)

_TABLE_CACHE = {}

def recon_tables(cfg):
    key = (cfg.C, cfg.NS)
    if key in _TABLE_CACHE:
        return _TABLE_CACHE[key]
    C, n, A = cfg.C, cfg.NS, cfg.A
    t = np.linspace(0.0, C - 1.0, n, dtype=np.float64)
    i_raw = np.clip(np.floor(t).astype(np.int64), 0, C - 2)
    b_a = i_raw[::128].copy()
    jj = np.arange(n, dtype=np.int64)
    delta = (i_raw - b_a[jj // 128])
    assert delta.min() >= 0 and delta.max() <= 1
    mask = np.zeros((128, A), dtype=np.float32)
    mask[jj % 128, jj // 128] = delta
    phi = t[::128] - b_a
    off = b_a[::128].copy()
    assert off.max() + cfg.WINW <= cfg.CPP

    aa = np.arange(A)
    gch = aa // 128
    al = aa % 128
    atab = np.zeros((8, cfg.WINR, A), dtype=np.float64)
    for side in (0, 1):
        bb = b_a + side
        shift = phi - side
        for l in range(4):
            for r in range(4):
                padidx = (bb - 1 + r) + cfg.PAD_LO
                rp = padidx - off[gch] - al
                assert rp.min() >= 0 and rp.max() < cfg.WINR
                coef = np.zeros(A)
                for k in range(l, 4):
                    coef += comb(k, l) * (shift ** (k - l)) * CRB[k, r]
                np.add.at(atab, (4 * side + l, rp, aa), coef)
    atab[4:8] -= atab[0:4]

    basis = np.zeros((4, 128), dtype=np.float64)
    p = np.arange(128, dtype=np.float64)
    for l in range(4):
        basis[l] = (p * cfg.dt) ** l

    atab_f = np.zeros((8, cfg.WINR, 128, cfg.NG, 128), dtype=np.float32)
    a3 = atab.reshape(8, cfg.WINR, cfg.NCH, 128)
    for g in range(cfg.NCH):
        atab_f[:, :, g % 128, g // 128, :] = a3[:, :, g, :]
    off_f = np.zeros((128, cfg.NG), dtype=np.int32)
    for g in range(cfg.NCH):
        off_f[g % 128, g // 128] = off[g]

    out = dict(off=off_f, atab=atab_f, mask=mask.astype(BF),
               basis=basis.astype(BF))
    _TABLE_CACHE[key] = out
    return out


def attn_bias_table(cfg, r):
    cols = []
    own = cfg.zig_chunks(r)
    for s in range(4):
        cols.append(-cfg.ESH)
        for j in range(cfg.PADS[s]):
            cols.append(-cfg.ESH if j < own[s] else -1e30)
    v = np.array(cols, dtype=np.float32)
    return np.ascontiguousarray(np.broadcast_to(v[None, :], (128, len(cols))))


def tri_masks(cfg):
    m = np.zeros((128, cfg.NK * cfg.G), dtype=np.float32)
    k = np.arange(128)[:, None]
    q = np.arange(cfg.G)[None, :]
    for kt in range(cfg.NK):
        m[:, kt * cfg.G:(kt + 1) * cfg.G] = (kt * 128 + k <= q)
    return m.astype(BF)


ENGINES = ("tensor", "vector", "scalar", "sync", "gpsimd")

class Prog:
    def __init__(self):
        self.ops = {e: [] for e in ENGINES}
        self.cnt = {}
        self.marks = {}

    def wait(self, eng, sem, val):
        if val > 0:
            self.ops[eng].append(("wait", sem, val))

    def waitm(self, eng, label):
        if label in self.marks:
            sem, val = self.marks[label]
            self.wait(eng, sem, val)

    def op(self, eng, fn, sem=None, by=1, mark=None):
        if sem is not None:
            self.cnt[sem] = self.cnt.get(sem, 0) + (1 if by == "dflt" else by)
        self.ops[eng].append(("op", fn, sem, by))
        if mark is not None:
            assert sem is not None
            self.marks[mark] = (sem, self.cnt[sem])

    def setmark(self, mark, sem):
        self.marks[mark] = (sem, self.cnt.get(sem, 0))

    def n(self, sem):
        return self.cnt.get(sem, 0)


def build_nc(cfg):
    nc = bass.Bass("TRN2", target_bir_lowering=False)
    TOWN, D, G, NK, H, DC, TT = cfg.TOWN, cfg.D, cfg.G, cfg.NK, cfg.H, cfg.DC, cfg.TT
    E3, EC, MC, INTER = cfg.E3, cfg.EC, cfg.MC, cfg.INTER
    KTG = cfg.KT_GRP
    NGR = NK // KTG
    SCL = float(1.0 / np.sqrt(128.0))
    pos = cfg.pos_row()
    TB, NTB, IC, NIC, HC, NHC = cfg.TB, cfg.NTB, cfg.IC, cfg.NIC, cfg.HC, cfg.NHC
    SG, SD, SPT, NSP = cfg.SG, cfg.SD, cfg.SPT, cfg.NSP
    NII = INTER // 128
    NTTB = TB // 128

    DT = nc.dram_tensor
    x_own = DT("x_own", [TOWN, D], F32, kind="ExternalInput").ap()
    w_qkv = DT("w_qkv", [E3, D], F32, kind="ExternalInput").ap()
    b_qkv = DT("b_qkv", [1, E3], F32, kind="ExternalInput").ap()
    w_out = DT("w_out", [D, D], F32, kind="ExternalInput").ap()
    b_out = DT("b_out", [1, D], F32, kind="ExternalInput").ap()
    cpp = [DT(f"cpp{i}", [1, cfg.CPP], F32, kind="ExternalInput").ap() for i in range(3)]
    t_off = DT("t_off", [128, cfg.NG], I32, kind="ExternalInput").ap()
    t_atab = DT("t_atab", [8, cfg.WINR, 128, cfg.NG, 128], F32, kind="ExternalInput").ap()
    t_mask = DT("t_mask", [128, cfg.A], BF16, kind="ExternalInput").ap()
    t_basis = DT("t_basis", [4, 128], BF16, kind="ExternalInput").ap()
    t_bias = DT("t_bias", [128, cfg.NPASS], F32, kind="ExternalInput").ap()
    t_tri = DT("t_tri", [128, NK * G], BF16, kind="ExternalInput").ap()
    t_ones = DT("t_ones", [128, 1], BF16, kind="ExternalInput").ap()
    t_ones4 = DT("t_ones4", [128, 1], BF16, kind="ExternalInput").ap()
    t_eps = DT("t_eps", [128, 1], F32, kind="ExternalInput").ap()
    t_onesr = DT("t_onesr", [1, 128], BF16, kind="ExternalInput").ap()
    y_own = DT("y_own", [TOWN, D], F32, kind="ExternalOutput").ap()

    x_bf = DT("x_bf", [TOWN, D], BF16).ap()
    wq_bf = DT("wq_bf", [E3, D], BF16).ap()
    wo_bf = DT("wo_bf", [D, D], BF16).ap()
    woT_dr = DT("woT_dr", [D, D], BF16).ap()
    q_dr = DT("q_dr", [TOWN, D], BF16).ap()
    k_own = DT("k_own", [TOWN, D], BF16).ap()
    v_own = DT("v_own", [TOWN, D], BF16).ap()
    k_all = DT("k_all", [4 * TOWN, D], BF16).ap()
    v_all = DT("v_all", [4 * TOWN, D], BF16).ap()
    h32_dr = DT("h32_dr", [TOWN, D], F32).ap()
    hbf_dr = DT("hbf_dr", [TOWN, D], BF16).ap()
    inter_dr = DT("inter_dr", [TOWN, INTER], BF16).ap()
    r_dr = [DT(f"r_dr{i}", [8, cfg.A], F32).ap() for i in range(3)]
    w_dr = [DT(f"w_dr{i}", [128, cfg.A], BF16).ap() for i in range(3)]

    import contextlib
    stack = contextlib.ExitStack()

    SBn = lambda name, shape, dt=BF16: stack.enter_context(nc.sbuf_tensor(name, shape, dt))
    ones_sb = SBn("ones_sb", [128, 1])
    ones4_sb = SBn("ones4_sb", [128, 1])
    onesr_sb = SBn("onesr_sb", [1, 128])
    basis_sb = SBn("basis_sb", [4, 128])
    bias_sb = SBn("bias_sb", [128, cfg.NPASS], F32)

    off_sb = SBn("off_sb", [128, cfg.NG], I32)
    mean_sb = SBn("mean_sb", [128, 1], F32)
    var_sb = SBn("var_sb", [128, 1], F32)
    rstd_sb = SBn("rstd_sb", [128, 1], F32)
    eps_sb = SBn("eps_sb", [128, 1], F32)
    rec_sb = SBn("rec_sb", [1, G])
    bq_sl = [SBn(f"bq_sl{i}", [1, 512]) for i in range(2)]
    ev_sb = [SBn(f"ev_sb{i}", [128, 512]) for i in range(2)]
    msk_sb = [SBn(f"msk_sb{i}", [128, 512]) for i in range(2)]

    wsel_sb = [SBn(f"wsel_sb{i}", [128, 512]) for i in range(2)]
    xr_sb = [SBn(f"xr_sb{i}", [128, 512], F32) for i in range(2)]

    h32t_sb = [SBn(f"h32t_sb{i}", [128, HC], F32) for i in range(2)]

    watermark = nc.sbuf_base
    W0 = (watermark + 31) // 32 * 32

    SZ = 229376
    assert nc.SBUF_PARTITION_SIZE_BYTES >= SZ - 1

    def manual(name, shape, dtype, off):
        t = nc.alloc_sbuf_tensor_at(name, shape, dtype, offset=off)
        itemsz = {BF16: 2, F32: 4, I32: 4}[dtype]
        end = off + int(np.prod(shape[1:])) * itemsz
        assert end <= nc.SBUF_PARTITION_SIZE_BYTES, (name, end)
        return t

    xT_bytes = DC * TOWN * 2
    attno_bytes = H * TOWN * 2
    hT_bytes = DC * TB * 2
    wg_bytes = IC * SG * 2
    iT_bytes = SPT * NII * 128 * 2
    R0 = W0
    R0_size = max(xT_bytes, attno_bytes, hT_bytes, iT_bytes)
    xT_sb = manual("xT_sb", [128, DC * TOWN], BF16, R0)
    attno_sb = manual("attno_sb", [128, H * TOWN], BF16, R0)
    hT_sb = manual("hT_sb", [128, DC * TB], BF16, R0)
    iT_sb = [manual("iT_sb", [128, SPT * NII * 128], BF16, R0)] * 2

    R1 = R0 + ((R0_size + 31) // 32 * 32)
    wq_bytes = DC * 512 * 2
    wo_bytes = DC * 512 * 2
    slab_bytes = HC * SD * 2
    attn_small = (2 * (TOWN + G + NK * 128) + 4 * KTG * G) * 2
    R1_size = max(max(2 * wq_bytes, 2 * 4 * TOWN * 2) + attn_small,
                  DC * D * 2, 2 * slab_bytes, 2 * wg_bytes)
    wq_col = [manual(f"wq_col{i}", [128, DC * 512], BF16, R1 + i * wq_bytes)
              for i in range(2)]
    wg_blk = [manual(f"wg_blk{i}", [128, IC * SG], BF16, R1 + i * wg_bytes)
              for i in range(2)]
    wu_blk = [manual(f"wu_blk{i}", [128, IC * SG], BF16, R1 + (2 + i) * wg_bytes)
              for i in range(2)]
    kf_sb = [manual(f"kf_sb{i}", [128, 4 * TOWN], BF16, R1 + i * 4 * TOWN * 2)
             for i in range(2)]
    _ao = R1 + max(2 * wq_bytes, 2 * 4 * TOWN * 2)
    q_sb = [manual(f"q_sb{i}", [128, TOWN], BF16, _ao + i * TOWN * 2) for i in range(2)]
    _ao += 2 * TOWN * 2
    k_sb = [manual(f"k_sb{i}", [128, G], BF16, _ao + i * G * 2) for i in range(2)]
    _ao += 2 * G * 2
    v_sb = [manual(f"v_sb{i}", [128, NK * 128], BF16, _ao + i * NK * 128 * 2)
            for i in range(2)]
    _ao += 2 * NK * 128 * 2
    p_sb = [manual(f"p_sb{i}", [128, KTG * G], BF16, _ao + i * KTG * G * 2)
            for i in range(4)]
    _ao += 4 * KTG * G * 2
    wo_full = manual("wo_full", [128, DC * D], BF16, R1)
    slab_sb = [manual(f"slab_sb{i}", [128, HC * SD], BF16, R1 + i * slab_bytes)
               for i in range(2)]

    R2 = R1 + ((R1_size + 31) // 32 * 32)
    cpw_bytes = (cfg.NG * cfg.WINW * 4 + 31) // 32 * 32
    at_bytes = cfg.NG * 128 * 4
    r2_bytes = SPT * D * 4
    scratch_end = 3 * cpw_bytes + 3 * at_bytes + 4 * at_bytes
    bo_off = max(scratch_end, 4 * D * 4)
    y_off = max(r2_bytes + D * 4, bo_off)
    R2_core = max(bo_off + D * 2, r2_bytes + D * 4, y_off + D * 4)
    R2_size = R2_core
    cpw_sb = [manual(f"cpw{i}", [128, cfg.NG * cfg.WINW], F32, R2 + i * cpw_bytes)
              for i in range(3)]
    at_sb = [manual(f"at_sb{i}", [128, cfg.NG * 128], F32,
                    R2 + 3 * cpw_bytes + i * at_bytes) for i in range(3)]
    racc3_sb = [manual(f"racc3_{i}", [128, cfg.NG * 128], F32,
                       R2 + 3 * cpw_bytes + (3 + i) * at_bytes) for i in range(3)]
    rtmp_sb = manual("rtmp_sb", [128, cfg.NG * 128], F32,
                     R2 + 3 * cpw_bytes + 6 * at_bytes)
    rldl_sb = [manual(f"rldl_sb{i}", [4, 1024], BF16,
                      R2 + 3 * cpw_bytes + i * 2048) for i in range(4)]
    _rb = R2 + 3 * cpw_bytes + 3 * at_bytes
    wout_sb = [manual(f"wout_sb{i}", [128, 512], BF16, _rb + i * 1024)
               for i in range(6)]
    wpi_sb = [manual(f"wpi_sb{i}", [128, 512], BF16, _rb + 6144 + i * 1024)
              for i in range(6)]
    bias_q_bc = manual("bias_q_bc", [128, E3], BF16,
                       R1 + max(2 * wq_bytes, 2 * 4 * TOWN * 2))
    bias_o_bc = manual("bias_o_bc", [128, D], BF16, R2 + bo_off)
    r1_sb = manual("r1_sb", [128, D], F32, R2)
    sums4_sb = manual("sums4_sb", [128, G], BF16, R2)
    av_sb = manual("av_sb", [128, G], F32, R2 + G * 4)
    tri_sb = manual("tri_sb", [128, NK * G], BF16, R2 + 2 * G * 4)
    _p5o = R2 + 2 * G * 4 + NK * G * 2
    sg_sb = manual("sg_sb", [128, IC], F32, _p5o)
    t1_sb = manual("t1_sb", [128, IC], F32, _p5o + IC * 4)
    iv_sb = [manual(f"iv_sb{i}", [128, IC], BF16, _p5o + 2 * IC * 4 + i * IC * 2)
             for i in range(2)]
    sq_sb = manual("sq_sb", [128, D], F32, R2 + D * 4)
    hf_sb = [manual(f"hf_sb{i}", [128, D], F32, R2 + 2 * D * 4 + i * D * 4)
             for i in range(2)]
    r2_sb = manual("r2_sb", [128, SPT * D], F32, R2)
    sq2_sb = manual("sq2_sb", [128, D], F32, R2 + r2_bytes)

    y_off = max(r2_bytes + D * 4, bo_off)
    assert y_off + D * 4 <= R2_size
    y_sb = [manual("y_sb", [128, D], F32, R2 + y_off)]
    assert R2 + R2_size <= nc.SBUF_PARTITION_SIZE_BYTES, (R2 + R2_size)

    PSn = lambda name, shape: stack.enter_context(nc.psum_tensor(name, shape, F32))
    S2 = [PSn(f"S2_{i}", [128, 1024]) for i in range(2)]
    P2 = [PSn(f"P2_{i}", [128, 512]) for i in range(2)]
    Q2 = [PSn(f"Q2_{i}", [128, 512]) for i in range(2)]

    P = Prog()

    def gq(dst, src, mark=None):
        P.op("gpsimd", lambda s, e, dst=dst, src=src: e.dma_start(out=dst, in_=src),
             sem="gq", by=16, mark=mark)
    def sq(dst, src, transpose=False, mark=None):
        P.op("sync", lambda s, e, dst=dst, src=src, tr=transpose:
             e.dma_start(out=dst, in_=src, transpose=tr), sem="sq", by=16, mark=mark)

    gq(off_sb[:, :], t_off[:, :], mark="off_ld")
    P.waitm("gpsimd", "off_ld")
    for i in range(3):
        for grp in range(cfg.NG):
            P.op("gpsimd", lambda s, e, i=i, grp=grp: e.indirect_dma_start(
                out=cpw_sb[i][:, grp * cfg.WINW:(grp + 1) * cfg.WINW],
                out_offset=None,
                in_=cpp[i][0:1, :],
                in_offset=IndirectOffsetOnAxis(off_sb[:, grp:grp + 1], 1)),
                sem="cpq", by=16)
    P.setmark("cpw_done", "cpq")
    for _c in range(4):
        if _c:
            P.wait("gpsimd", "xcq", 16 * _c)
        P.op("gpsimd", lambda s, e, _c=_c: e.dma_start(
            out=x_bf[_c * 512:(_c + 1) * 512, :],
            in_=x_own[_c * 512:(_c + 1) * 512, :]),
            sem="xcq", by=16, mark=("x_cast", _c))
    P.op("gpsimd", lambda s, e: e.dma_start(
        out=wq_bf[0:512, :], in_=w_qkv[0:512, :]),
        sem="wcq", by=16, mark=("wq_cast", 0))
    sq(ones_sb[:, :], t_ones[:, :])
    sq(ones4_sb[:, :], t_ones4[:, :])
    sq(eps_sb[:, :], t_eps[:, :])
    sq(onesr_sb[:, :], t_onesr[:, :])
    sq(basis_sb[:, :], t_basis[:, :])
    sq(bias_sb[:, :], t_bias[:, :], mark="p0_sq")

    AT_SEMS = ("rla", "rlb", "rlc")

    def rprep_use_count(k):
        return (k // cfg.WINR) * (6 * cfg.WINR - 3) + 3 + 6 * (k % cfg.WINR)

    def emit_at_row(row):
        for rp in range(cfg.WINR):
            k = row * cfg.WINR + rp
            if k >= 3:
                P.wait("sync", "vc", rprep_use_count(k - 3))
            P.op("sync", lambda s, e, row=row, rp=rp, k=k: e.dma_start(
                out=at_sb[k % 3][:, :], in_=t_atab[row, rp, :, :, :]),
                sem=AT_SEMS[k % 3], by=16, mark=("at_ld", k))

    XT_SEMS = ("xtq0", "xtq1", "xtq2", "xtq3")
    xt_jobs = [(dc, t0) for t0 in range(0, TOWN, 512) for dc in range(DC)]
    for ji, (dc, t0) in enumerate(xt_jobs):
        c = ji // DC
        if ji % DC == 0:
            P.waitm("sync", ("x_cast", c))
        P.op("sync", lambda s, e, dc=dc, t0=t0: e.dma_start(
            out=xT_sb[:, dc * TOWN + t0: dc * TOWN + t0 + 512],
            in_=x_bf[t0:t0 + 512, dc * 128:(dc + 1) * 128], transpose=True),
            sem=XT_SEMS[c], by=16,
            mark=("xt_chunk", c) if ji % DC == DC - 1 else None)
        if ji == DC - 1:
            P.waitm("sync", ("wq_cast", 0))
            for dcw in range(DC):
                P.op("sync", lambda s, en, dcw=dcw: en.dma_start(
                    out=wq_col[0][:, dcw * 512:(dcw + 1) * 512],
                    in_=wq_bf[0:512, dcw * 128:(dcw + 1) * 128],
                    transpose=True), sem="wqa", by=16)
            P.setmark(("wq_loaded", 0), "wqa")
        if (ji + 1) % 8 == 0 and (ji + 1) // 8 <= 8:
            emit_at_row((ji + 1) // 8 - 1)


    nq = E3 // 512
    P.waitm("tensor", "p0_sq")
    for i in range(nq + D // 512):
        src = b_qkv[:, i * 512:(i + 1) * 512] if i < nq else \
              b_out[:, (i - nq) * 512:(i - nq + 1) * 512]
        P.waitm("gpsimd", ("bc_mm", i - 2))
        P.op("gpsimd", lambda s, e, src=src, i=i: e.dma_start(
            out=bq_sl[i % 2][:, :], in_=src),
            sem=("bqa" if i % 2 == 0 else "bqb"), by=16, mark=("bq_sl", i))
        P.waitm("tensor", ("bq_sl", i))
        P.waitm("tensor", ("bc_copy", i - 1))
        P.op("tensor", lambda s, e, i=i: e.matmul(
            P2[0][:, :], onesr_sb[:, :], bq_sl[i % 2][:, :], start=True, stop=True),
            sem="tc", mark=("bc_mm", i))
        P.waitm("scalar", ("bc_mm", i))
        dst = bias_q_bc[:, i * 512:(i + 1) * 512] if i < nq else \
              bias_o_bc[:, (i - nq) * 512:(i - nq + 1) * 512]
        P.op("scalar", lambda s, e, dst=dst: e.copy(dst, P2[0][:, :]),
             sem="ac", mark=("bc_copy", i))
    P.setmark("bias_bc_done", "ac")

    for _e in range(1, EC):
        P.wait("gpsimd", "wcq", 16 * _e)
        P.op("gpsimd", lambda s, e, _e=_e: e.dma_start(
            out=wq_bf[_e * 512:(_e + 1) * 512, :],
            in_=w_qkv[_e * 512:(_e + 1) * 512, :]),
            sem="wcq", by=16, mark=("wq_cast", _e))
    P.op("gpsimd", lambda s, e: e.dma_start(out=wo_bf[:, :], in_=w_out[:, :]),
         sem="wcq", by=16, mark="wo_cast")

    for row in range(8):
        for rp in range(cfg.WINR):
            k = row * cfg.WINR + rp
            P.waitm("vector", ("at_ld", k))
            if k == 0:
                P.waitm("vector", "cpw_done")
            atv = at_sb[k % 3].rearrange("p (g w) -> p g w", w=128)[:, :, :]
            for cpi in range(3):
                cpws = cpw_sb[cpi].rearrange(
                    "p (g w) -> p g w", w=cfg.WINW)[:, :, rp:rp + 128]
                if rp == 0:
                    P.waitm("vector", ("rrow_st", (row - 1) * 3 + 2))
                    P.op("vector", lambda s, e, cpws=cpws, atv=atv, cpi=cpi:
                         e.tensor_tensor(
                             racc3_sb[cpi].rearrange(
                                 "p (g w) -> p g w", w=128)[:, :, :],
                             atv, cpws, ALU.mult), sem="vc",
                         mark=("rprep_use", k) if cpi == 2 else None)
                else:
                    P.op("vector", lambda s, e, cpws=cpws, atv=atv: e.tensor_tensor(
                        rtmp_sb.rearrange("p (g w) -> p g w", w=128)[:, :, :],
                        atv, cpws, ALU.mult), sem="vc")
                    P.op("vector", lambda s, e, cpi=cpi: e.tensor_tensor(
                        racc3_sb[cpi][:, :], racc3_sb[cpi][:, :], rtmp_sb[:, :],
                        ALU.add), sem="vc",
                        mark=("rprep_use", k) if cpi == 2 else None)
        P.setmark(("rrow_vec", row), "vc")
        PPC = min(cfg.NCH, 128)
        P.waitm("scalar", ("rrow_vec", row))
        for cpi in range(3):
            dst = r_dr[cpi][row:row + 1, :].rearrange("o (g pp w) -> (o pp) g w",
                                                      pp=PPC, w=128)
            P.op("scalar", lambda s, e, dst=dst, cpi=cpi: e.dma_start(
                out=dst,
                in_=racc3_sb[cpi][0:PPC, :].rearrange(
                    "p (g w) -> p g w", w=128)[:, :, :]),
                sem="rwq", by=16, mark=("rrow_st", row * 3 + cpi))
    P.setmark("rprep_all", "vc")
    P.setmark("rprep_out_all", "rwq")

    kq = 0
    for e in range(EC):
        nm = "wqa" if e % 2 == 0 else "wqb"
        if e > 0:
            P.waitm("sync", ("qkv_mm_blk", e - 2))
            P.waitm("sync", ("wq_cast", e))
            for dc in range(DC):
                P.op("sync", lambda s, en, e=e, dc=dc: en.dma_start(
                    out=wq_col[e % 2][:, dc * 512:(dc + 1) * 512],
                    in_=wq_bf[e * 512:(e + 1) * 512, dc * 128:(dc + 1) * 128],
                    transpose=True), sem=nm, by=16)
            P.setmark(("wq_loaded", e), nm)
        P.waitm("tensor", ("wq_loaded", e))
        for tt in range(TT):
            if e == 0:
                P.waitm("tensor", ("xt_chunk", tt // 4))
            pb = Q2[kq % 2]
            P.waitm("tensor", ("qkv_ev", kq - 2))
            for dc in range(DC):
                last = dc == DC - 1
                P.op("tensor", lambda s, en, e=e, tt=tt, dc=dc, pb=pb:
                     en.matmul(pb[:, :],
                               xT_sb[:, dc * TOWN + tt * 128: dc * TOWN + (tt + 1) * 128],
                               wq_col[e % 2][:, dc * 512:(dc + 1) * 512],
                               start=(dc == 0), stop=(dc == DC - 1)),
                     sem=("tc" if last else None),
                     mark=(("qkv_mm", kq) if last else None))
            P.waitm("vector", ("qkv_mm", kq))
            P.waitm("vector", "bias_bc_done")
            P.waitm("vector", ("qkv_out", kq - 2))
            P.op("vector", lambda s, en, e=e, kq=kq, pb=pb: en.tensor_tensor(
                ev_sb[kq % 2][:, :], pb[:, :],
                bias_q_bc[:, e * 512:(e + 1) * 512], ALU.add),
                sem="vc", mark=("qkv_ev", kq))
            P.waitm("gpsimd", ("qkv_ev", kq))
            if e < D // 512:
                dst = q_dr[tt * 128:(tt + 1) * 128, e * 512:(e + 1) * 512]
            else:
                ek = e - D // 512
                dst = (k_own if ek < D // 512 else v_own)[
                    tt * 128:(tt + 1) * 128,
                    (ek % (D // 512)) * 512:(ek % (D // 512) + 1) * 512]
            is_k = (D // 512) <= e < 2 * (D // 512)
            nmo = ("eka" if kq % 2 == 0 else "ekb") if is_k else \
                  ("eva" if kq % 2 == 0 else "evb")
            P.op("gpsimd", lambda s, en, dst=dst, kq=kq: en.dma_start(
                out=dst, in_=ev_sb[kq % 2][:, :]), sem=nmo, by=16,
                mark=("qkv_out", kq))
            kq += 1
        P.setmark(("qkv_mm_blk", e), "tc")
    P.setmark("qkv_done", "tc")
    P.setmark("qkv_ev_all", "vc")

    P.wait("gpsimd", "eka", P.n("eka"))
    P.wait("gpsimd", "ekb", P.n("ekb"))
    P.op("gpsimd", lambda s, e: e.collective_compute(
        "AllGather", ALU.bypass,
        replica_groups=[[0, 1, 2, 3], [4, 5, 6, 7]],
        ins=[k_own[:, :].opt()], outs=[k_all[:, :].opt()]),
        sem="cc", by="dflt", mark="kv_all")
    P.wait("gpsimd", "eva", P.n("eva"))
    P.wait("gpsimd", "evb", P.n("evb"))
    P.op("gpsimd", lambda s, e: e.collective_compute(
        "AllGather", ALU.bypass,
        replica_groups=[[0, 1, 2, 3], [4, 5, 6, 7]],
        ins=[v_own[:, :].opt()], outs=[v_all[:, :].opt()]),
        sem="cc", by="dflt", mark="kv_allv")

    NBLK = cfg.A // 512
    P.waitm("gpsimd", "rprep_out_all")
    PSLOT = [(S2[0][:, 0:512], S2[0][:, 512:1024]),
             (S2[1][:, 0:512], S2[1][:, 512:1024]),
             (P2[0][:, :], P2[1][:, :]),
             (Q2[0][:, :], Q2[1][:, :])]
    WS_SEMS = ("ws0", "ws1", "ws2", "ws3", "ws4", "ws5")
    NK_RC = NBLK * 3

    def rc_emit_stage(j):
        mc, dc = divmod(j, DC)
        if j == 0:
            P.wait("sync", "eva", P.n("eva"))
            P.wait("sync", "evb", P.n("evb"))
            P.wait("sync", "eka", P.n("eka"))
            P.wait("sync", "ekb", P.n("ekb"))
            P.waitm("sync", "wo_cast")
        npar = "wta" if j % 2 == 0 else "wtb"
        P.waitm("sync", ("wo_st", j - 2))
        P.op("sync", lambda s, e, mc=mc, dc=dc, j=j: e.dma_start(
            out=ev_sb[j % 2][:, :],
            in_=wo_bf[mc * 512:(mc + 1) * 512, dc * 128:(dc + 1) * 128],
            transpose=True), sem=npar, by=16, mark=("wo_tld", j))
        P.waitm("gpsimd", ("wo_tld", j))
        P.op("gpsimd", lambda s, e, mc=mc, dc=dc, j=j: e.dma_start(
            out=woT_dr[dc * 128:(dc + 1) * 128, mc * 512:(mc + 1) * 512],
            in_=ev_sb[j % 2][:, :]), sem=("wqoa" if j % 2 == 0 else "wqob"),
            by=16, mark=("wo_st", j))

    def rc_emit_loads(k):
        blk, cpi = divmod(k, 3)
        if cpi == 0:
            nmm = "mka" if blk % 2 == 0 else "mkb"
            P.waitm("sync", ("rc_mult", (blk - 2) * 3 + 2))
            P.op("sync", lambda s, e, blk=blk: e.dma_start(
                out=msk_sb[blk % 2][:, :],
                in_=t_mask[:, blk * 512:(blk + 1) * 512]),
                sem=nmm, by=16, mark=("mk_ld", blk))
        nm = ("rwa", "rwb", "rwc", "rwd")[k % 4]
        P.waitm("gpsimd", ("rc_mm", k - 4))
        P.op("gpsimd", lambda s, e, cpi=cpi, blk=blk, k=k: e.dma_start(
            out=rldl_sb[k % 4].rearrange("p (h c) -> p h c", h=2)[:, :, :],
            in_=r_dr[cpi][:, blk * 512:(blk + 1) * 512]
            .rearrange("(h p) c -> p h c", p=4)), sem=nm, by=16,
            mark=("rdl_ld", k))

    def rc_emit_mm(k):
        lo_ps, hi_ps = PSLOT[k % 4]
        P.waitm("tensor", ("rdl_ld", k))
        if k == 0:
            P.waitm("tensor", "p0_sq")
            P.waitm("tensor", "qkv_ev_all")
            P.waitm("tensor", "bias_bc_done")
        P.waitm("tensor", ("rc_add", k - 4))
        P.op("tensor", lambda s, e, lo_ps=lo_ps, k=k: e.matmul(
            lo_ps, basis_sb[:, :], rldl_sb[k % 4][:, 0:512],
            start=True, stop=True), sem="tc")
        P.op("tensor", lambda s, e, hi_ps=hi_ps, k=k: e.matmul(
            hi_ps, basis_sb[:, :], rldl_sb[k % 4][:, 512:1024],
            start=True, stop=True), sem="tc", mark=("rc_mm", k))

    def rc_emit_vec(k):
        blk = k // 3
        lo_ps, hi_ps = PSLOT[k % 4]
        P.waitm("vector", ("rc_mm", k))
        P.waitm("vector", ("mk_ld", blk))
        P.op("vector", lambda s, e, k=k, blk=blk, hi_ps=hi_ps: e.tensor_tensor(
            wsel_sb[k % 2][:, :], hi_ps, msk_sb[blk % 2][:, :],
            ALU.mult), sem="vc", mark=("rc_mult", k))
        P.waitm("vector", ("rc_cpy", k - 6))
        P.op("vector", lambda s, e, k=k, lo_ps=lo_ps: e.tensor_tensor(
            wout_sb[k % 6][:, :], wsel_sb[k % 2][:, :], lo_ps,
            ALU.add), sem="vc", mark=("rc_add", k))

    def rc_emit_copy(k):
        cpi = k % 3
        stride = SG if cpi < 2 else SD
        il = 512 // stride
        pview = wpi_sb[k % 6].rearrange("p (c il) -> p il c", il=il)[:, :, :]
        eng = "gpsimd" if (k % 5) in (1, 3) else "scalar"
        P.waitm(eng, ("rc_add", k))
        P.waitm(eng, ("w_out", k - 6))
        if eng == "scalar":
            P.op("scalar", lambda s, e, k=k, pview=pview: e.copy(
                pview, wout_sb[k % 6][:, :]), sem="ac", mark=("rc_cpy", k))
        else:
            P.op("gpsimd", lambda s, e, k=k, pview=pview: e.tensor_copy(
                pview, wout_sb[k % 6][:, :]), sem="pc", mark=("rc_cpy", k))
        blk, cpi = divmod(k, 3)
        P.waitm(eng, ("rc_cpy", k))
        P.op(eng, lambda s, e, cpi=cpi, blk=blk, k=k: e.dma_start(
            out=w_dr[cpi][:, blk * 512:(blk + 1) * 512],
            in_=wpi_sb[k % 6][:, :]), sem=WS_SEMS[k % 6], by=16,
            mark=("w_out", k))

    rc_emit_loads(0)
    rc_emit_loads(1)
    for k in range(NK_RC + 2):
        if k < MC * DC:
            rc_emit_stage(k)
        if k + 2 < NK_RC:
            rc_emit_loads(k + 2)
        if k < NK_RC:
            rc_emit_mm(k)
            rc_emit_vec(k)
        if k >= 2:
            rc_emit_copy(k - 2)
    P.setmark("recon_sel_all", "vc")
    P.setmark("recon_mm_all", "tc")
    WS_TOT = [P.n(nm) for nm in WS_SEMS]

    def pass_list(sl):
        out = [("diag", (k_own, v_own), sl * G, sum(cfg.PADS[:sl]) + sl)]
        for j in range(cfg.PADS[sl]):
            out.append(("rect", (k_all, v_all), pos[j],
                        sum(cfg.PADS[:sl]) + sl + 1 + j))
        return out


    P.waitm("sync", "rprep_all")
    sq(tri_sb[:, :], t_tri[:, :], mark="tri_ld")
    P.waitm("vector", "tri_ld")
    gidx = 0
    kidx = 0
    for h in range(H):
        nmq = "qla" if h % 2 == 0 else "qlb"
        nmk = "kfa" if h % 2 == 0 else "kfb"
        P.waitm("sync", ("attn_h_smm", h - 2))
        for t0 in range(0, TOWN, 512):
            P.op("sync", lambda s, e, h=h, t0=t0: e.dma_start(
                out=q_sb[h % 2][:, t0:t0 + 512],
                in_=q_dr[t0:t0 + 512, h * 128:(h + 1) * 128], transpose=True),
                sem=nmq, by=16)
        P.setmark(("q_ld", h), nmq)
        if h == 0:
            P.waitm("sync", "qkv_done")
            P.waitm("sync", "kv_all")
        for t0 in range(0, 4 * TOWN, 2048):
            P.op("sync", lambda s, e, h=h, t0=t0: e.dma_start(
                out=kf_sb[h % 2][:, t0:t0 + 2048],
                in_=k_all[t0:t0 + 2048, h * 128:(h + 1) * 128], transpose=True),
                sem=nmk, by=16)
        P.setmark(("kf_ld", h), nmk)
        P.waitm("tensor", ("q_ld", h))
        P.waitm("tensor", ("kf_ld", h))
        for sl in range(4):
            pend_av = [None]
            pl = pass_list(sl)
            npass = len(pl)
            for pi, (kind, src, row0, bidx) in enumerate(pl):
                nm = "kta" if kidx % 2 == 0 else "ktb"
                nmv = "vta" if kidx % 2 == 0 else "vtb"
                P.waitm("sync", ("kv_use", kidx - 2))
                if kind == "rect" and h == 0 and sl == 0 and pi == 1:
                    P.waitm("sync", "kv_allv")
                if kind == "diag":
                    P.op("sync", lambda s, e, src=src, row0=row0, h=h, kidx=kidx:
                         e.dma_start(out=k_sb[kidx % 2][:, :],
                                     in_=src[0][row0:row0 + G, h * 128:(h + 1) * 128],
                                     transpose=True), sem=nm, by=16, mark=("k_ld", kidx))
                P.op("sync", lambda s, e, src=src, row0=row0, h=h, kidx=kidx:
                     e.dma_start(
                         out=v_sb[kidx % 2].rearrange("p (kt d) -> p kt d", d=128)[:, :, :],
                         in_=src[1][row0:row0 + G, h * 128:(h + 1) * 128]
                         .rearrange("(kt p) d -> p kt d", p=128)),
                     sem=nmv, by=16, mark=("v_ld", kidx))
                if kind == "diag":
                    P.waitm("tensor", ("k_ld", kidx))
                for grp in range(NGR):
                    gpar = gidx % 2
                    ppar = gidx % 4
                    if gidx == 0:
                        P.waitm("tensor", "recon_sel_all")
                    P.waitm("tensor", ("exp", gidx - 2))
                    for ktl in range(KTG):
                        kt = grp * KTG + ktl
                        last = ktl == KTG - 1
                        if kind == "diag":
                            klhs = k_sb[kidx % 2][:, kt * 128:(kt + 1) * 128]
                        else:
                            klhs = kf_sb[h % 2][:, row0 + kt * 128: row0 + (kt + 1) * 128]
                        P.op("tensor", lambda s, e, klhs=klhs, gpar=gpar, h=h, sl=sl,
                             ktl=ktl:
                             e.matmul(S2[gpar][:, ktl * G:(ktl + 1) * G],
                                      klhs,
                                      q_sb[h % 2][:, sl * G:(sl + 1) * G],
                                      start=True, stop=True),
                             sem=("tc" if last else None),
                             mark=(("s_mm", gidx) if last else None))
                    P.waitm("scalar", ("s_mm", gidx))
                    P.waitm("scalar", ("av_grp", gidx - 4))
                    P.op("scalar", lambda s, e, gpar=gpar, ppar=ppar, bidx=bidx:
                         e.activation(
                             p_sb[ppar][:, :], S2[gpar][:, 0:KTG * G], AF.Exp,
                             bias=bias_sb[:, bidx:bidx + 1], scale=SCL),
                         sem="ac", mark=("exp", gidx))
                    if kind == "diag":
                        P.waitm("vector", ("exp", gidx))
                        P.op("vector", lambda s, e, ppar=ppar, grp=grp: e.tensor_tensor(
                            p_sb[ppar][:, :], p_sb[ppar][:, :],
                            tri_sb[:, grp * KTG * G:(grp + 1) * KTG * G], ALU.mult),
                            sem="vc", mark=("tri", gidx))
                    gidx += 1
                def emit_av(kind=kind, kidx=kidx, g_base=gidx - NGR, pi=pi,
                           npass=npass, sl=sl, h=h):
                    P.waitm("tensor", ("v_ld", kidx))
                    first = pi == 0
                    last_pass = pi == npass - 1
                    if first:
                        P.waitm("tensor", ("attno_mul", h * 4 + sl - 2))
                        P.waitm("tensor", ("recip", h * 4 + sl - 1))
                        if h == 0 and sl == 0:
                            P.waitm("tensor", "bias_bc_done")
                            P.waitm("tensor", "qkv_ev_all")
                    for grp_i in range(NGR):
                        g2 = g_base + grp_i
                        ppar = g2 % 4
                        P.waitm("tensor", ("exp", g2))
                        if kind == "diag":
                            P.waitm("tensor", ("tri", g2))
                        for ktl in range(KTG):
                            kt = grp_i * KTG + ktl
                            av_last = last_pass and kt == NK - 1
                            grp_last = ktl == KTG - 1
                            P.op("tensor", lambda s, e, sl=sl, kidx=kidx, ppar=ppar,
                                 kt=kt, ktl=ktl, first=first, av_last=av_last:
                                 e.matmul(P2[sl % 2][:, 0:G],
                                          v_sb[kidx % 2][:, kt * 128:(kt + 1) * 128],
                                          p_sb[ppar][:, ktl * G:(ktl + 1) * G],
                                          start=(first and kt == 0), stop=av_last))
                            mk2 = ("avs_mm", h * 4 + sl) if av_last else \
                                  (("av_grp", g2) if grp_last else None)
                            srow = 32 * (kt % 4)
                            P.op("tensor", lambda s, e, ppar=ppar, kt=kt, ktl=ktl,
                                 first=first, last_pass=last_pass, srow=srow:
                                 e.matmul(Q2[0][srow:srow + 1, 0:G], ones_sb[:, :],
                                          p_sb[ppar][:, ktl * G:(ktl + 1) * G],
                                          start=first, stop=last_pass,
                                          tile_position=(0, srow)),
                                 sem=("tc" if (av_last or grp_last) else None),
                                 mark=mk2)
                        if last_pass and grp_i == NGR - 1:
                            P.setmark(("av_grp", g2), "tc")
                    P.setmark(("kv_use", kidx), "tc")
                if pend_av[0] is not None:
                    pend_av[0]()
                pend_av[0] = emit_av
                kidx += 1
            if pend_av[0] is not None:
                pend_av[0]()
                pend_av[0] = None
            P.waitm("scalar", ("avs_mm", h * 4 + sl))
            P.op("scalar", lambda s, e: e.copy(sums4_sb[:, :], Q2[0][:, 0:G]),
                 sem="ac", mark=("sums_cp", h * 4 + sl))
            P.waitm("tensor", ("sums_cp", h * 4 + sl))
            P.op("tensor", lambda s, e: e.matmul(
                Q2[0][0:1, 0:G], ones4_sb[:, :], sums4_sb[:, :],
                start=True, stop=True), sem="tc", mark=("sums_red", h * 4 + sl))
            P.waitm("vector", ("sums_red", h * 4 + sl))
            def _recip(s, e):
                with nc.allow_low_precision(reason="recip feeds bf16 matmul"):
                    return e.reciprocal(rec_sb[:, :], Q2[0][0:1, 0:G])
            P.op("vector", _recip, sem="vc", mark=("recip", h * 4 + sl))
            P.waitm("tensor", ("recip", h * 4 + sl))
            P.waitm("tensor", ("attno_mul", h * 4 + sl - 1))
            P.op("tensor", lambda s, e: e.matmul(
                Q2[1][:, 0:G], onesr_sb[:, :], rec_sb[:, :], start=True, stop=True),
                sem="tc", mark=("rbc", h * 4 + sl))
            P.waitm("scalar", ("avs_mm", h * 4 + sl))
            P.waitm("scalar", ("attno_mul", h * 4 + sl - 1))
            P.op("scalar", lambda s, e, sl=sl: e.copy(av_sb[:, :], P2[sl % 2][:, 0:G]),
                 sem="ac", mark=("av_cp", h * 4 + sl))
            P.waitm("vector", ("av_cp", h * 4 + sl))
            P.waitm("vector", ("rbc", h * 4 + sl))
            P.op("vector", lambda s, e, h=h, sl=sl: e.tensor_tensor(
                attno_sb[:, h * TOWN + sl * G: h * TOWN + (sl + 1) * G],
                av_sb[:, :], Q2[1][:, 0:G], ALU.mult),
                sem="vc", mark=("attno_mul", h * 4 + sl))
        P.setmark(("attn_h_smm", h), "tc")
    P.setmark("attn_done", "vc")
    P.setmark("attn_ac_all", "ac")
    P.setmark("attn_tc_all", "tc")

    P.waitm("sync", "rprep_all")
    P.waitm("sync", "qkv_done")
    P.waitm("tensor", "attn_done")
    P.waitm("sync", "attn_tc_all")
    P.wait("sync", "wqoa", P.n("wqoa"))
    P.wait("sync", "wqob", P.n("wqob"))
    wo_fv = wo_full.rearrange("p (dc mcm) -> p dc mcm", mcm=MC * 512)
    for mc in range(MC):
        P.op("sync", lambda s, e, mc=mc: e.dma_start(
            out=wo_fv[:, :, mc * 512:(mc + 1) * 512],
            in_=woT_dr[:, mc * 512:(mc + 1) * 512]
            .rearrange("(dc p) m -> p dc m", p=128)), sem="wfq", by=16,
            mark="wo_full_ld" if mc == MC - 1 else None)
    for tt in range(TT):
        for mc in range(MC):
            k4 = tt * MC + mc
            pb = Q2[k4 % 2]
            if k4 == 0:
                P.waitm("tensor", "wo_full_ld")
            P.waitm("tensor", ("p4_ev", k4 - 2))
            for dc in range(DC):
                last = dc == DC - 1
                P.op("tensor", lambda s, e, tt=tt, mc=mc, dc=dc, pb=pb, k4=k4:
                     e.matmul(pb[:, :],
                              attno_sb[:, dc * TOWN + tt * 128: dc * TOWN + (tt + 1) * 128],
                              wo_full[:, dc * MC * 512 + mc * 512:
                                      dc * MC * 512 + (mc + 1) * 512],
                              start=(dc == 0), stop=(dc == DC - 1)),
                     sem=("tc" if last else None),
                     mark=(("p4_mm", k4) if last else None))
            nx = "xca" if k4 % 2 == 0 else "xcb"
            P.waitm("gpsimd", ("p4_ev", k4 - 2))
            P.op("gpsimd", lambda s, e, tt=tt, mc=mc, k4=k4: e.dma_start(
                out=xr_sb[k4 % 2][:, :],
                in_=x_own[tt * 128:(tt + 1) * 128, mc * 512:(mc + 1) * 512]),
                sem=nx, by=16, mark=("x_ld", k4))
            P.waitm("vector", ("p4_mm", k4))
            P.waitm("vector", ("x_ld", k4))
            P.op("vector", lambda s, e, mc=mc, pb=pb: e.tensor_tensor(
                r1_sb[:, mc * 512:(mc + 1) * 512], pb[:, :],
                bias_o_bc[:, mc * 512:(mc + 1) * 512], ALU.add), sem="vc")
            P.op("vector", lambda s, e, mc=mc, k4=k4: e.tensor_tensor(
                r1_sb[:, mc * 512:(mc + 1) * 512],
                r1_sb[:, mc * 512:(mc + 1) * 512], xr_sb[k4 % 2][:, :], ALU.add),
                sem="vc", mark=("p4_ev", k4))
        P.op("vector", lambda s, e: e.reduce_sum(mean_sb[:, :], r1_sb[:, :], axis=AXX),
             sem="vc")
        P.op("vector", lambda s, e: e.tensor_scalar(
            mean_sb[:, :], mean_sb[:, :], 1.0 / D, None, ALU.mult), sem="vc")
        P.op("vector", lambda s, e: e.tensor_scalar(
            r1_sb[:, :], r1_sb[:, :], mean_sb[:, 0:1], None, ALU.subtract), sem="vc")
        P.op("vector", lambda s, e: e.tensor_tensor(
            sq_sb[:, :], r1_sb[:, :], r1_sb[:, :], ALU.mult), sem="vc")
        P.op("vector", lambda s, e: e.reduce_sum(var_sb[:, :], sq_sb[:, :], axis=AXX),
             sem="vc", mark=("ln1_var", tt))
        P.waitm("scalar", ("ln1_var", tt))
        P.op("scalar", lambda s, e: e.activation(
            rstd_sb[:, :], var_sb[:, :], AF.Ln, bias=eps_sb[:, 0:1], scale=1.0 / D), sem="ac")
        P.op("scalar", lambda s, e: e.activation(
            rstd_sb[:, :], rstd_sb[:, :], AF.Exp, scale=-0.5),
            sem="ac", mark=("ln1_rstd", tt))
        P.waitm("vector", ("ln1_rstd", tt))
        P.waitm("vector", ("h_out", tt - 2))
        P.op("vector", lambda s, e, tt=tt: e.tensor_scalar(
            hf_sb[tt % 2][:, :], r1_sb[:, :], rstd_sb[:, 0:1], None, ALU.mult),
            sem="vc", mark=("ln1_h", tt))
        P.waitm("gpsimd", ("ln1_h", tt))
        nhq = "hqa" if tt % 2 == 0 else "hqb"
        P.op("gpsimd", lambda s, e, tt=tt: e.dma_start(
            out=h32_dr[tt * 128:(tt + 1) * 128, :], in_=hf_sb[tt % 2][:, :]),
            sem=nhq, by=16)
        P.op("gpsimd", lambda s, e, tt=tt: e.dma_start(
            out=hbf_dr[tt * 128:(tt + 1) * 128, :], in_=hf_sb[tt % 2][:, :]),
            sem=nhq, by=16, mark=("h_out", tt))
    P.setmark("p4_mm_all", "tc")
    P.setmark("p4_vec_all", "vc")

    P.wait("sync", "hqa", P.n("hqa"))
    P.wait("sync", "hqb", P.n("hqb"))
    P.waitm("sync", "p4_vec_all")
    kg = 0
    for tb in range(NTB):
        t_base = tb * TB
        P.waitm("sync", ("gu_blk", tb - 1))
        for dc in range(DC):
            for t0 in range(0, TB, 512):
                P.op("sync", lambda s, e, dc=dc, t0=t0, t_base=t_base: e.dma_start(
                    out=hT_sb[:, dc * TB + t0: dc * TB + t0 + 512],
                    in_=hbf_dr[t_base + t0: t_base + t0 + 512, dc * 128:(dc + 1) * 128],
                    transpose=True), sem="htq", by=16)
        P.setmark(("hT_ld", tb), "htq")
        P.waitm("tensor", ("hT_ld", tb))
        for ic in range(NIC):
            nm = "wga" if ic % 2 == 0 else "wgb"
            P.waitm("gpsimd", ("gu_mm", (tb * NIC + ic - 2 + 1) * NTTB - 1))
            if tb == 0 and ic == 0:
                for _nm, _v in zip(WS_SEMS, WS_TOT):
                    P.wait("gpsimd", _nm, _v)
                P.waitm("gpsimd", "p4_mm_all")
            P.op("gpsimd", lambda s, e, ic=ic: e.dma_start(
                out=wg_blk[ic % 2][:, :],
                in_=w_dr[0][:, ic * IC * SG:(ic + 1) * IC * SG]), sem=nm, by=16)
            P.op("gpsimd", lambda s, e, ic=ic: e.dma_start(
                out=wu_blk[ic % 2][:, :],
                in_=w_dr[1][:, ic * IC * SG:(ic + 1) * IC * SG]), sem=nm, by=16,
                mark=("wg_ld", tb * NIC + ic))
            P.waitm("tensor", ("wg_ld", tb * NIC + ic))
            for tt in range(NTTB):
                gpar = kg % 2
                gq4 = kg % 4
                gps = (S2[0][:, 0:512], S2[1][:, 0:512], P2[0][:, :], Q2[0][:, :])[gq4]
                ups = (S2[0][:, 512:1024], S2[1][:, 512:1024], P2[1][:, :],
                       Q2[1][:, :])[gq4]
                if kg == 0:
                    P.waitm("tensor", "attn_ac_all")
                    P.waitm("tensor", "attn_done")
                    P.waitm("tensor", ("p4_ev", TT * MC - 1))
                P.waitm("tensor", ("gu_ev", kg - 4))
                BIL = 512 // SG
                wgv = wg_blk[ic % 2].rearrange("p (blk x) -> p blk x", x=512)
                wuv = wu_blk[ic % 2].rearrange("p (blk x) -> p blk x", x=512)
                for dc in range(DC):
                    last = dc == DC - 1
                    P.op("tensor", lambda s, e, dc=dc, tt=tt, gps=gps, wgv=wgv, BIL=BIL:
                         e.matmul(gps,
                                  hT_sb[:, dc * TB + tt * 128: dc * TB + (tt + 1) * 128],
                                  wgv[:, :, dc * BIL:(dc + 1) * BIL],
                                  start=(dc == 0), stop=(dc == DC - 1)))
                    P.op("tensor", lambda s, e, dc=dc, tt=tt, ups=ups, wuv=wuv, BIL=BIL:
                         e.matmul(ups,
                                  hT_sb[:, dc * TB + tt * 128: dc * TB + (tt + 1) * 128],
                                  wuv[:, :, dc * BIL:(dc + 1) * BIL],
                                  start=(dc == 0), stop=(dc == DC - 1)),
                         sem=("tc" if last else None),
                         mark=(("gu_mm", kg) if last else None))
                P.waitm("scalar", ("gu_mm", kg))
                P.waitm("scalar", ("gu_t1", kg - 1))
                P.op("scalar", lambda s, e, gps=gps: e.activation(
                    sg_sb[:, :], gps, AF.Sigmoid),
                    sem="ac", mark=("gu_sig", kg))
                P.waitm("vector", ("gu_sig", kg))
                P.op("vector", lambda s, e, gps=gps: e.tensor_tensor(
                    t1_sb[:, :], gps, sg_sb[:, :], ALU.mult),
                    sem="vc", mark=("gu_t1", kg))
                P.waitm("vector", ("iv_out", kg - 2))
                P.op("vector", lambda s, e, ups=ups, kg=kg: e.tensor_tensor(
                    iv_sb[kg % 2][:, :], t1_sb[:, :], ups,
                    ALU.mult), sem="vc", mark=("gu_ev", kg))
                P.waitm("gpsimd", ("gu_ev", kg))
                nmi = "iva" if kg % 2 == 0 else "ivb"
                P.op("gpsimd", lambda s, e, ic=ic, tt=tt, t_base=t_base, kg=kg:
                     e.dma_start(out=inter_dr[t_base + tt * 128: t_base + (tt + 1) * 128,
                                              ic * IC:(ic + 1) * IC],
                                 in_=iv_sb[kg % 2][:, :]), sem=nmi, by=16,
                     mark=("iv_out", kg))
                kg += 1
        P.setmark(("gu_blk", tb), "tc")
    P.setmark("gu_all_tc", "tc")
    P.setmark("gu_all_ac", "ac")
    P.wait("gpsimd", "iva", P.n("iva"))
    P.wait("gpsimd", "ivb", P.n("ivb"))
    P.setmark("inter_all", "gq")

    IVA_ALL, IVB_ALL = P.n("iva"), P.n("ivb")
    kd = 0
    for sp in range(NSP):
        t0 = sp * SPT * 128
        P.waitm("sync", "p4_mm_all")
        P.wait("sync", "iva", IVA_ALL)
        P.wait("sync", "ivb", IVB_ALL)
        P.waitm("sync", ("dn_sp", sp - 1))
        nit = "itqa" if sp % 2 == 0 else "itqb"
        for ii in range(NII):
            P.op("sync", lambda s, e, ii=ii, t0=t0, sp=sp: e.dma_start(
                out=iT_sb[sp % 2][:, ii * SPT * 128:(ii + 1) * SPT * 128],
                in_=inter_dr[t0: t0 + SPT * 128, ii * 128:(ii + 1) * 128],
                transpose=True), sem=nit, by=16)
        P.setmark(("iT_ld", sp), nit)
        P.waitm("tensor", ("iT_ld", sp))
        for hc in range(NHC):
            nms = "sba" if hc % 2 == 0 else "sbb"
            P.waitm("gpsimd", ("dn_mm", kd - SPT - 1))
            if sp == 0 and hc == 0:
                P.waitm("gpsimd", "gu_all_tc")
            P.op("gpsimd", lambda s, e, hc=hc: e.dma_start(
                out=slab_sb[hc % 2][:, :],
                in_=w_dr[2][:, hc * HC * SD:(hc + 1) * HC * SD]), sem=nms, by=16,
                mark=("slab_ld", sp * NHC + hc))
            P.waitm("tensor", ("slab_ld", sp * NHC + hc))
            BHL = 512 // SD
            slv = slab_sb[hc % 2].rearrange("p (blk x) -> p blk x", x=512)
            for ttl in range(SPT):
                pdp = (P2[0], P2[1], Q2[0], Q2[1])[kd % 4]
                if kd == 0:
                    P.waitm("tensor", ("gu_ev", NTB * NIC * NTTB - 1))
                P.waitm("tensor", ("dn_ev", kd - 4))
                for ii in range(NII):
                    last = ii == NII - 1
                    P.op("tensor", lambda s, e, ii=ii, ttl=ttl, pdp=pdp, slv=slv, BHL=BHL,
                         sp=sp:
                         e.matmul(pdp[:, 0:HC],
                                  iT_sb[sp % 2][:, (ii * SPT + ttl) * 128:(ii * SPT + ttl + 1) * 128],
                                  slv[:, :, ii * BHL:(ii + 1) * BHL],
                                  start=(ii == 0), stop=(ii == NII - 1)),
                         sem=("tc" if last else None),
                         mark=(("dn_mm", kd) if last else None))
                nh3 = "h3a" if kd % 2 == 0 else "h3b"
                P.waitm("sync", ("dn_ev", kd - 2))
                P.op("sync", lambda s, e, ttl=ttl, hc=hc, t0=t0, kd=kd: e.dma_start(
                    out=h32t_sb[kd % 2][:, :],
                    in_=h32_dr[t0 + ttl * 128: t0 + (ttl + 1) * 128,
                               hc * HC:(hc + 1) * HC]), sem=nh3, by=16,
                    mark=("h3_ld", kd))
                P.waitm("vector", ("dn_mm", kd))
                P.waitm("vector", ("h3_ld", kd))
                P.op("vector", lambda s, e, ttl=ttl, hc=hc, pdp=pdp, kd=kd:
                     e.tensor_tensor(
                         r2_sb[:, ttl * D + hc * HC: ttl * D + (hc + 1) * HC],
                         pdp[:, 0:HC], h32t_sb[kd % 2][:, :], ALU.add),
                     sem="vc", mark=("dn_ev", kd))
                kd += 1
        for ttl in range(SPT):
            r2v = r2_sb[:, ttl * D:(ttl + 1) * D]
            P.op("vector", lambda s, e, r2v=r2v: e.reduce_sum(
                mean_sb[:, :], r2v, axis=AXX), sem="vc")
            P.op("vector", lambda s, e: e.tensor_scalar(
                mean_sb[:, :], mean_sb[:, :], 1.0 / D, None, ALU.mult), sem="vc")
            P.op("vector", lambda s, e, r2v=r2v: e.tensor_scalar(
                r2v, r2v, mean_sb[:, 0:1], None, ALU.subtract), sem="vc")
            P.op("vector", lambda s, e, r2v=r2v: e.tensor_tensor(
                sq2_sb[:, :], r2v, r2v, ALU.mult), sem="vc")
            P.op("vector", lambda s, e: e.reduce_sum(
                var_sb[:, :], sq2_sb[:, :], axis=AXX), sem="vc",
                mark=("ln2_var", sp * SPT + ttl))
            P.waitm("scalar", ("ln2_var", sp * SPT + ttl))
            P.op("scalar", lambda s, e: e.activation(
                rstd_sb[:, :], var_sb[:, :], AF.Ln, bias=eps_sb[:, 0:1], scale=1.0 / D), sem="ac")
            P.op("scalar", lambda s, e: e.activation(
                rstd_sb[:, :], rstd_sb[:, :], AF.Exp, scale=-0.5),
                sem="ac", mark=("ln2_rstd", sp * SPT + ttl))
            gy = sp * SPT + ttl
            P.waitm("vector", ("ln2_rstd", gy))
            P.waitm("vector", ("y_out", gy - 1))
            P.op("vector", lambda s, e, r2v=r2v, gy=gy: e.tensor_scalar(
                y_sb[0][:, :], r2v, rstd_sb[:, 0:1], None, ALU.mult),
                sem="vc", mark=("ln2_y", gy))
            P.waitm("gpsimd", ("ln2_y", gy))
            nmy = "yqa" if gy % 2 == 0 else "yqb"
            P.op("gpsimd", lambda s, e, gy=gy, ttl=ttl, t0=t0: e.dma_start(
                out=y_own[t0 + ttl * 128: t0 + (ttl + 1) * 128, :],
                in_=y_sb[0][:, :]), sem=nmy, by=16, mark=("y_out", gy))
        P.setmark(("dn_sp", sp), "tc")
    P.wait("gpsimd", "yqa", P.n("yqa"))
    P.wait("gpsimd", "yqb", P.n("yqb"))

    semnames = set(P.cnt.keys())
    for e in ENGINES:
        for item in P.ops[e]:
            if item[0] == "wait":
                semnames.add(item[1])
    sems = {name: stack.enter_context(nc.semaphore(name)) for name in sorted(semnames)}

    with nc.Block() as block:
        selfsem = {"vector": "vc", "scalar": "ac"}
        def mk(eng_name):
            def go(eng):
                own = selfsem.get(eng_name)
                nown = 0
                for item in P.ops[eng_name]:
                    if item[0] == "wait":
                        _, s, v = item
                        eng.wait_ge(sems[s], v)
                    else:
                        _, fn, s, by = item
                        if own is not None and nown > 0:
                            eng.wait_ge(sems[own], nown)
                        inst = fn(sems, eng)
                        if s is not None:
                            if by == "dflt":
                                inst.then_inc(sems[s])
                            else:
                                inst.then_inc(sems[s], by)
                                if s == own:
                                    nown += by
            return go
        block.tensor(mk("tensor"))
        block.vector(mk("vector"))
        block.scalar(mk("scalar"))
        block.sync(mk("sync"))
        block.gpsimd(mk("gpsimd"))

    stack.close()
    return nc


def shard_inputs(cfg, inputs):
    x = np.asarray(inputs["L0_attn_x"])
    wq = np.ascontiguousarray(np.asarray(inputs["L0_attn_w_qkv"], dtype=np.float32))
    bq = np.ascontiguousarray(np.asarray(inputs["L0_attn_b_qkv"], dtype=np.float32)[None, :])
    wo = np.ascontiguousarray(np.asarray(inputs["L0_attn_w_out"], dtype=np.float32))
    bo = np.ascontiguousarray(np.asarray(inputs["L0_attn_b_out"], dtype=np.float32)[None, :])
    cps = [np.asarray(inputs[k], dtype=np.float32)
           for k in ("L0_ffn_gate_cp", "L0_ffn_up_cp", "L0_ffn_down_cp")]
    tabs = recon_tables(cfg)
    tri = tri_masks(cfg)
    ones = np.ones((128, 1), dtype=BF)
    ones4 = np.zeros((128, 1), dtype=BF)
    for _r in range(min(cfg.NK, 4)):
        ones4[32 * _r, 0] = 1.0
    onesr = np.ones((1, 128), dtype=BF)
    cpp = [np.ascontiguousarray(np.concatenate(
        [np.full(cfg.PAD_LO, c[0]), c, np.full(cfg.PAD_HI, c[-1])]
        ).astype(np.float32)[None, :]) for c in cps]

    in_maps = []
    for core in range(8):
        b, r = divmod(core, 4)
        chunks = cfg.zig_chunks(r)
        xs = np.concatenate([x[b, c * cfg.G:(c + 1) * cfg.G, :] for c in chunks], axis=0)
        in_maps.append(dict(
            x_own=np.ascontiguousarray(xs, dtype=np.float32),
            w_qkv=wq, b_qkv=bq, w_out=wo, b_out=bo,
            cpp0=cpp[0], cpp1=cpp[1], cpp2=cpp[2],
            t_off=tabs["off"], t_atab=tabs["atab"], t_mask=tabs["mask"],
            t_basis=tabs["basis"], t_bias=attn_bias_table(cfg, r), t_tri=tri,
            t_ones=ones, t_onesr=onesr, t_ones4=ones4,
            t_eps=np.full((128, 1), 1e-5, dtype=np.float32),
        ))
    return in_maps


def assemble_output(cfg, results):
    y = np.zeros((cfg.B, cfg.S, cfg.D), dtype=np.float32)
    for core in range(8):
        b, r = divmod(core, 4)
        yo = results[core]["y_own"]
        for s, c in enumerate(cfg.zig_chunks(r)):
            y[b, c * cfg.G:(c + 1) * cfg.G, :] = yo[s * cfg.G:(s + 1) * cfg.G, :]
    return y


def kernel(L0_attn_x, L0_attn_w_qkv, L0_attn_b_qkv, L0_attn_w_out, L0_attn_b_out,
           L0_ffn_gate_cp, L0_ffn_up_cp, L0_ffn_down_cp):
    cfg = Cfg()
    inputs = dict(L0_attn_x=L0_attn_x, L0_attn_w_qkv=L0_attn_w_qkv,
                  L0_attn_b_qkv=L0_attn_b_qkv, L0_attn_w_out=L0_attn_w_out,
                  L0_attn_b_out=L0_attn_b_out, L0_ffn_gate_cp=L0_ffn_gate_cp,
                  L0_ffn_up_cp=L0_ffn_up_cp, L0_ffn_down_cp=L0_ffn_down_cp)
    in_maps = shard_inputs(cfg, inputs)
    nc = build_nc(cfg)
    res = run_bass_kernel_spmd(nc, in_maps, core_ids=list(range(8)))
    return assemble_output(cfg, list(res.results))


# version 48
# speedup vs baseline: 1.0071x; 1.0071x over previous
import numpy as np
import ml_dtypes
from math import comb

import concourse.bass as bass
import concourse.mybir as mybir
from concourse.bass_utils import run_bass_kernel_spmd
from concourse.bass_core_rust import IndirectOffsetOnAxis

F32 = mybir.dt.float32
BF16 = mybir.dt.bfloat16
I32 = mybir.dt.int32
AF = mybir.ActivationFunctionType
ALU = mybir.AluOpType
AXX = mybir.AxisListType.X
BF = ml_dtypes.bfloat16


class Cfg:
    def __init__(self, B=2, S=8192, D=2048, H=16, INTER=8192):
        self.B, self.S, self.D, self.H, self.INTER = B, S, D, H, INTER
        self.HD = D // H
        assert self.HD == 128 and D % 512 == 0 and INTER % 512 == 0
        self.E3 = 3 * D
        self.G = S // 16
        assert self.G % 128 == 0 and self.G <= 512
        self.NK = self.G // 128
        self.TOWN = 4 * self.G
        self.TT = self.TOWN // 128
        self.DC = D // 128
        self.PADS = [4, 8, 12, 16]
        self.C = max(16, int(INTER * D / 128.9))
        self.NS = INTER * D
        self.A = self.NS // 128
        self.NCH = self.A // 128
        self.NG = (self.NCH + 127) // 128
        self.WINW = 136
        self.WINR = 6
        self.PAD_LO, self.PAD_HI = 2, 16
        self.CPP = self.C + self.PAD_LO + self.PAD_HI
        self.dt = float(np.float64(self.C - 1.0) / np.float64(self.NS - 1))
        self.SG = D // 128
        self.SD = INTER // 128
        self.TB = min(2048, self.TOWN)
        self.NTB = self.TOWN // self.TB
        self.IC = 512
        self.NIC = INTER // self.IC
        self.HC = 256
        self.NHC = D // self.HC
        self.SPT = min(4, self.TT)
        self.NSP = self.TT // self.SPT
        self.KT_GRP = min(2, self.NK) if self.NK > 1 else 1
        while self.NK % self.KT_GRP:
            self.KT_GRP -= 1
        self.ESH = 10.0
        self.EC = self.E3 // 512
        self.MC = D // 512
        self.NPASS = 4 + sum(self.PADS)

    def zig_chunks(self, r):
        return [r, 7 - r, 8 + r, 15 - r]

    def pos_row(self):
        pos = {}
        for r in range(4):
            for s, cid in enumerate(self.zig_chunks(r)):
                pos[cid] = r * self.TOWN + s * self.G
        return pos


CRB = 0.5 * np.array([
    [0., 2., 0., 0.], [-1., 0., 1., 0.], [2., -5., 4., -1.], [-1., 3., -3., 1.],
], dtype=np.float64)

_TABLE_CACHE = {}

def recon_tables(cfg):
    key = (cfg.C, cfg.NS)
    if key in _TABLE_CACHE:
        return _TABLE_CACHE[key]
    C, n, A = cfg.C, cfg.NS, cfg.A
    t = np.linspace(0.0, C - 1.0, n, dtype=np.float64)
    i_raw = np.clip(np.floor(t).astype(np.int64), 0, C - 2)
    b_a = i_raw[::128].copy()
    jj = np.arange(n, dtype=np.int64)
    delta = (i_raw - b_a[jj // 128])
    assert delta.min() >= 0 and delta.max() <= 1
    mask = np.zeros((128, A), dtype=np.float32)
    mask[jj % 128, jj // 128] = delta
    phi = t[::128] - b_a
    off = b_a[::128].copy()
    assert off.max() + cfg.WINW <= cfg.CPP

    aa = np.arange(A)
    gch = aa // 128
    al = aa % 128
    atab = np.zeros((8, cfg.WINR, A), dtype=np.float64)
    for side in (0, 1):
        bb = b_a + side
        shift = phi - side
        for l in range(4):
            for r in range(4):
                padidx = (bb - 1 + r) + cfg.PAD_LO
                rp = padidx - off[gch] - al
                assert rp.min() >= 0 and rp.max() < cfg.WINR
                coef = np.zeros(A)
                for k in range(l, 4):
                    coef += comb(k, l) * (shift ** (k - l)) * CRB[k, r]
                np.add.at(atab, (4 * side + l, rp, aa), coef)
    atab[4:8] -= atab[0:4]

    basis = np.zeros((4, 128), dtype=np.float64)
    p = np.arange(128, dtype=np.float64)
    for l in range(4):
        basis[l] = (p * cfg.dt) ** l

    atab_f = np.zeros((8, cfg.WINR, 128, cfg.NG, 128), dtype=np.float32)
    a3 = atab.reshape(8, cfg.WINR, cfg.NCH, 128)
    for g in range(cfg.NCH):
        atab_f[:, :, g % 128, g // 128, :] = a3[:, :, g, :]
    off_f = np.zeros((128, cfg.NG), dtype=np.int32)
    for g in range(cfg.NCH):
        off_f[g % 128, g // 128] = off[g]

    out = dict(off=off_f, atab=atab_f, mask=mask.astype(BF),
               basis=basis.astype(BF))
    _TABLE_CACHE[key] = out
    return out


def attn_bias_table(cfg, r):
    cols = []
    own = cfg.zig_chunks(r)
    for s in range(4):
        cols.append(-cfg.ESH)
        for j in range(cfg.PADS[s]):
            cols.append(-cfg.ESH if j < own[s] else -1e30)
    v = np.array(cols, dtype=np.float32)
    return np.ascontiguousarray(np.broadcast_to(v[None, :], (128, len(cols))))


def tri_masks(cfg):
    m = np.zeros((128, cfg.NK * cfg.G), dtype=np.float32)
    k = np.arange(128)[:, None]
    q = np.arange(cfg.G)[None, :]
    for kt in range(cfg.NK):
        m[:, kt * cfg.G:(kt + 1) * cfg.G] = (kt * 128 + k <= q)
    return m.astype(BF)


ENGINES = ("tensor", "vector", "scalar", "sync", "gpsimd")

class Prog:
    def __init__(self):
        self.ops = {e: [] for e in ENGINES}
        self.cnt = {}
        self.marks = {}

    def wait(self, eng, sem, val):
        if val > 0:
            self.ops[eng].append(("wait", sem, val))

    def waitm(self, eng, label):
        if label in self.marks:
            sem, val = self.marks[label]
            self.wait(eng, sem, val)

    def op(self, eng, fn, sem=None, by=1, mark=None):
        if sem is not None:
            self.cnt[sem] = self.cnt.get(sem, 0) + (1 if by == "dflt" else by)
        self.ops[eng].append(("op", fn, sem, by))
        if mark is not None:
            assert sem is not None
            self.marks[mark] = (sem, self.cnt[sem])

    def setmark(self, mark, sem):
        self.marks[mark] = (sem, self.cnt.get(sem, 0))

    def n(self, sem):
        return self.cnt.get(sem, 0)


def build_nc(cfg):
    nc = bass.Bass("TRN2", target_bir_lowering=False)
    TOWN, D, G, NK, H, DC, TT = cfg.TOWN, cfg.D, cfg.G, cfg.NK, cfg.H, cfg.DC, cfg.TT
    E3, EC, MC, INTER = cfg.E3, cfg.EC, cfg.MC, cfg.INTER
    KTG = cfg.KT_GRP
    NGR = NK // KTG
    SCL = float(1.0 / np.sqrt(128.0))
    pos = cfg.pos_row()
    TB, NTB, IC, NIC, HC, NHC = cfg.TB, cfg.NTB, cfg.IC, cfg.NIC, cfg.HC, cfg.NHC
    SG, SD, SPT, NSP = cfg.SG, cfg.SD, cfg.SPT, cfg.NSP
    NII = INTER // 128
    NTTB = TB // 128

    DT = nc.dram_tensor
    x_own = DT("x_own", [TOWN, D], F32, kind="ExternalInput").ap()
    w_qkv = DT("w_qkv", [E3, D], F32, kind="ExternalInput").ap()
    b_qkv = DT("b_qkv", [1, E3], F32, kind="ExternalInput").ap()
    w_out = DT("w_out", [D, D], F32, kind="ExternalInput").ap()
    b_out = DT("b_out", [1, D], F32, kind="ExternalInput").ap()
    cpp = [DT(f"cpp{i}", [1, cfg.CPP], F32, kind="ExternalInput").ap() for i in range(3)]
    t_off = DT("t_off", [128, cfg.NG], I32, kind="ExternalInput").ap()
    t_atab = DT("t_atab", [8, cfg.WINR, 128, cfg.NG, 128], F32, kind="ExternalInput").ap()
    t_mask = DT("t_mask", [128, cfg.A], BF16, kind="ExternalInput").ap()
    t_basis = DT("t_basis", [4, 128], BF16, kind="ExternalInput").ap()
    t_bias = DT("t_bias", [128, cfg.NPASS], F32, kind="ExternalInput").ap()
    t_tri = DT("t_tri", [128, NK * G], BF16, kind="ExternalInput").ap()
    t_ones = DT("t_ones", [128, 1], BF16, kind="ExternalInput").ap()
    t_ones4 = DT("t_ones4", [128, 1], BF16, kind="ExternalInput").ap()
    t_eps = DT("t_eps", [128, 1], F32, kind="ExternalInput").ap()
    t_onesr = DT("t_onesr", [1, 128], BF16, kind="ExternalInput").ap()
    y_own = DT("y_own", [TOWN, D], F32, kind="ExternalOutput").ap()

    x_bf = DT("x_bf", [TOWN, D], BF16).ap()
    wq_bf = DT("wq_bf", [E3, D], BF16).ap()
    wo_bf = DT("wo_bf", [D, D], BF16).ap()
    woT_dr = DT("woT_dr", [D, D], BF16).ap()
    q_dr = DT("q_dr", [TOWN, D], BF16).ap()
    k_own = DT("k_own", [TOWN, D], BF16).ap()
    v_own = DT("v_own", [TOWN, D], BF16).ap()
    k_all = DT("k_all", [4 * TOWN, D], BF16).ap()
    v_all = DT("v_all", [4 * TOWN, D], BF16).ap()
    h32_dr = DT("h32_dr", [TOWN, D], F32).ap()
    hbf_dr = DT("hbf_dr", [TOWN, D], BF16).ap()
    inter_dr = DT("inter_dr", [TOWN, INTER], BF16).ap()
    r_dr = [DT(f"r_dr{i}", [8, cfg.A], F32).ap() for i in range(3)]
    w_dr = [DT(f"w_dr{i}", [128, cfg.A], BF16).ap() for i in range(3)]

    import contextlib
    stack = contextlib.ExitStack()

    SBn = lambda name, shape, dt=BF16: stack.enter_context(nc.sbuf_tensor(name, shape, dt))
    ones_sb = SBn("ones_sb", [128, 1])
    ones4_sb = SBn("ones4_sb", [128, 1])
    onesr_sb = SBn("onesr_sb", [1, 128])
    basis_sb = SBn("basis_sb", [4, 128])
    bias_sb = SBn("bias_sb", [128, cfg.NPASS], F32)

    off_sb = SBn("off_sb", [128, cfg.NG], I32)
    mean_sb = SBn("mean_sb", [128, 1], F32)
    var_sb = SBn("var_sb", [128, 1], F32)
    rstd_sb = SBn("rstd_sb", [128, 1], F32)
    eps_sb = SBn("eps_sb", [128, 1], F32)
    rec_sb = SBn("rec_sb", [1, G])
    bq_sl = [SBn(f"bq_sl{i}", [1, 512]) for i in range(2)]
    ev_sb = [SBn(f"ev_sb{i}", [128, 512]) for i in range(2)]
    msk_sb = [SBn(f"msk_sb{i}", [128, 512]) for i in range(2)]

    wsel_sb = [SBn(f"wsel_sb{i}", [128, 512]) for i in range(2)]
    xr_sb = [SBn(f"xr_sb{i}", [128, 512], F32) for i in range(2)]

    h32t_sb = [SBn(f"h32t_sb{i}", [128, HC], F32) for i in range(2)]

    watermark = nc.sbuf_base
    W0 = (watermark + 31) // 32 * 32

    SZ = 229376
    assert nc.SBUF_PARTITION_SIZE_BYTES >= SZ - 1

    def manual(name, shape, dtype, off):
        t = nc.alloc_sbuf_tensor_at(name, shape, dtype, offset=off)
        itemsz = {BF16: 2, F32: 4, I32: 4}[dtype]
        end = off + int(np.prod(shape[1:])) * itemsz
        assert end <= nc.SBUF_PARTITION_SIZE_BYTES, (name, end)
        return t

    xT_bytes = DC * TOWN * 2
    attno_bytes = H * TOWN * 2
    hT_bytes = DC * TB * 2
    wg_bytes = IC * SG * 2
    iT_bytes = SPT * NII * 128 * 2
    R0 = W0
    R0_size = max(xT_bytes, attno_bytes, hT_bytes, iT_bytes)
    xT_sb = manual("xT_sb", [128, DC * TOWN], BF16, R0)
    attno_sb = manual("attno_sb", [128, H * TOWN], BF16, R0)
    hT_sb = manual("hT_sb", [128, DC * TB], BF16, R0)
    iT_sb = [manual("iT_sb", [128, SPT * NII * 128], BF16, R0)] * 2

    R1 = R0 + ((R0_size + 31) // 32 * 32)
    wq_bytes = DC * 512 * 2
    wo_bytes = DC * 512 * 2
    slab_bytes = HC * SD * 2
    attn_small = (2 * (TOWN + G + NK * 128) + 4 * KTG * G) * 2
    R1_size = max(max(2 * wq_bytes, 2 * 4 * TOWN * 2) + attn_small,
                  DC * D * 2, 2 * slab_bytes, 2 * wg_bytes)
    wq_col = [manual(f"wq_col{i}", [128, DC * 512], BF16, R1 + i * wq_bytes)
              for i in range(2)]
    wg_blk = [manual(f"wg_blk{i}", [128, IC * SG], BF16, R1 + i * wg_bytes)
              for i in range(2)]
    wu_blk = [manual(f"wu_blk{i}", [128, IC * SG], BF16, R1 + (2 + i) * wg_bytes)
              for i in range(2)]
    kf_sb = [manual(f"kf_sb{i}", [128, 4 * TOWN], BF16, R1 + i * 4 * TOWN * 2)
             for i in range(2)]
    _ao = R1 + max(2 * wq_bytes, 2 * 4 * TOWN * 2)
    q_sb = [manual(f"q_sb{i}", [128, TOWN], BF16, _ao + i * TOWN * 2) for i in range(2)]
    _ao += 2 * TOWN * 2
    k_sb = [manual(f"k_sb{i}", [128, G], BF16, _ao + i * G * 2) for i in range(2)]
    _ao += 2 * G * 2
    v_sb = [manual(f"v_sb{i}", [128, NK * 128], BF16, _ao + i * NK * 128 * 2)
            for i in range(2)]
    _ao += 2 * NK * 128 * 2
    p_sb = [manual(f"p_sb{i}", [128, KTG * G], BF16, _ao + i * KTG * G * 2)
            for i in range(4)]
    _ao += 4 * KTG * G * 2
    wo_full = manual("wo_full", [128, DC * D], BF16, R1)
    slab_sb = [manual(f"slab_sb{i}", [128, HC * SD], BF16, R1 + i * slab_bytes)
               for i in range(2)]

    R2 = R1 + ((R1_size + 31) // 32 * 32)
    cpw_bytes = (cfg.NG * cfg.WINW * 4 + 31) // 32 * 32
    at_bytes = cfg.NG * 128 * 4
    r2_bytes = SPT * D * 4
    scratch_end = 3 * cpw_bytes + 3 * at_bytes + 4 * at_bytes
    bo_off = max(scratch_end, 4 * D * 4)
    y_off = max(r2_bytes + D * 4, bo_off)
    R2_core = max(bo_off + D * 2, r2_bytes + D * 4, y_off + D * 4)
    R2_size = R2_core
    cpw_sb = [manual(f"cpw{i}", [128, cfg.NG * cfg.WINW], F32, R2 + i * cpw_bytes)
              for i in range(3)]
    at_sb = [manual(f"at_sb{i}", [128, cfg.NG * 128], F32,
                    R2 + 3 * cpw_bytes + i * at_bytes) for i in range(3)]
    racc3_sb = [manual(f"racc3_{i}", [128, cfg.NG * 128], F32,
                       R2 + 3 * cpw_bytes + (3 + i) * at_bytes) for i in range(3)]
    rtmp_sb = manual("rtmp_sb", [128, cfg.NG * 128], F32,
                     R2 + 3 * cpw_bytes + 6 * at_bytes)
    rldl_sb = [manual(f"rldl_sb{i}", [4, 1024], BF16,
                      R2 + 3 * cpw_bytes + i * 2048) for i in range(4)]
    _rb = R2 + 3 * cpw_bytes + 3 * at_bytes
    wout_sb = [manual(f"wout_sb{i}", [128, 512], BF16, _rb + i * 1024)
               for i in range(6)]
    wpi_sb = [manual(f"wpi_sb{i}", [128, 512], BF16, _rb + 6144 + i * 1024)
              for i in range(6)]
    bias_q_bc = manual("bias_q_bc", [128, E3], BF16,
                       R1 + max(2 * wq_bytes, 2 * 4 * TOWN * 2))
    bias_o_bc = manual("bias_o_bc", [128, D], BF16, R2 + bo_off)
    r1_sb = manual("r1_sb", [128, D], F32, R2)
    sums4_sb = manual("sums4_sb", [128, G], BF16, R2)
    av_sb = manual("av_sb", [128, G], F32, R2 + G * 4)
    tri_sb = manual("tri_sb", [128, NK * G], BF16, R2 + 2 * G * 4)
    _p5o = R2 + 2 * G * 4 + NK * G * 2
    sg_sb = manual("sg_sb", [128, IC], F32, _p5o)
    t1_sb = manual("t1_sb", [128, IC], F32, _p5o + IC * 4)
    iv_sb = [manual(f"iv_sb{i}", [128, IC], BF16, _p5o + 2 * IC * 4 + i * IC * 2)
             for i in range(2)]
    sq_sb = manual("sq_sb", [128, D], F32, R2 + D * 4)
    hf_sb = [manual(f"hf_sb{i}", [128, D], F32, R2 + 2 * D * 4 + i * D * 4)
             for i in range(2)]
    r2_sb = manual("r2_sb", [128, SPT * D], F32, R2)
    sq2_sb = manual("sq2_sb", [128, D], F32, R2 + r2_bytes)

    y_off = max(r2_bytes + D * 4, bo_off)
    assert y_off + D * 4 <= R2_size
    y_sb = [manual("y_sb", [128, D], F32, R2 + y_off)]
    assert R2 + R2_size <= nc.SBUF_PARTITION_SIZE_BYTES, (R2 + R2_size)

    PSn = lambda name, shape: stack.enter_context(nc.psum_tensor(name, shape, F32))
    S2 = [PSn(f"S2_{i}", [128, 1024]) for i in range(2)]
    P2 = [PSn(f"P2_{i}", [128, 512]) for i in range(2)]
    Q2 = [PSn(f"Q2_{i}", [128, 512]) for i in range(2)]

    P = Prog()

    def gq(dst, src, mark=None):
        P.op("gpsimd", lambda s, e, dst=dst, src=src: e.dma_start(out=dst, in_=src),
             sem="gq", by=16, mark=mark)
    def sq(dst, src, transpose=False, mark=None):
        P.op("sync", lambda s, e, dst=dst, src=src, tr=transpose:
             e.dma_start(out=dst, in_=src, transpose=tr), sem="sq", by=16, mark=mark)

    gq(off_sb[:, :], t_off[:, :], mark="off_ld")
    P.waitm("gpsimd", "off_ld")
    for i in range(3):
        for grp in range(cfg.NG):
            P.op("gpsimd", lambda s, e, i=i, grp=grp: e.indirect_dma_start(
                out=cpw_sb[i][:, grp * cfg.WINW:(grp + 1) * cfg.WINW],
                out_offset=None,
                in_=cpp[i][0:1, :],
                in_offset=IndirectOffsetOnAxis(off_sb[:, grp:grp + 1], 1)),
                sem="cpq", by=16)
    P.setmark("cpw_done", "cpq")
    for _c in range(4):
        if _c:
            P.wait("gpsimd", "xcq", 16 * _c)
        P.op("gpsimd", lambda s, e, _c=_c: e.dma_start(
            out=x_bf[_c * 512:(_c + 1) * 512, :],
            in_=x_own[_c * 512:(_c + 1) * 512, :]),
            sem="xcq", by=16, mark=("x_cast", _c))
    P.op("gpsimd", lambda s, e: e.dma_start(
        out=wq_bf[0:512, :], in_=w_qkv[0:512, :]),
        sem="wcq", by=16, mark=("wq_cast", 0))
    sq(ones_sb[:, :], t_ones[:, :])
    sq(ones4_sb[:, :], t_ones4[:, :])
    sq(eps_sb[:, :], t_eps[:, :])
    sq(onesr_sb[:, :], t_onesr[:, :])
    sq(basis_sb[:, :], t_basis[:, :])
    sq(bias_sb[:, :], t_bias[:, :], mark="p0_sq")

    AT_SEMS = ("rla", "rlb", "rlc")

    def rprep_use_count(k):
        return (k // cfg.WINR) * (6 * cfg.WINR - 3) + 3 + 6 * (k % cfg.WINR)

    def emit_at_row(row):
        for rp in range(cfg.WINR):
            k = row * cfg.WINR + rp
            if k >= 3:
                P.wait("sync", "vc", rprep_use_count(k - 3))
            P.op("sync", lambda s, e, row=row, rp=rp, k=k: e.dma_start(
                out=at_sb[k % 3][:, :], in_=t_atab[row, rp, :, :, :]),
                sem=AT_SEMS[k % 3], by=16, mark=("at_ld", k))

    XT_SEMS = ("xtq0", "xtq1", "xtq2", "xtq3")
    xt_jobs = [(dc, t0) for t0 in range(0, TOWN, 512) for dc in range(DC)]
    for ji, (dc, t0) in enumerate(xt_jobs):
        c = ji // DC
        if ji % DC == 0:
            P.waitm("sync", ("x_cast", c))
        P.op("sync", lambda s, e, dc=dc, t0=t0: e.dma_start(
            out=xT_sb[:, dc * TOWN + t0: dc * TOWN + t0 + 512],
            in_=x_bf[t0:t0 + 512, dc * 128:(dc + 1) * 128], transpose=True),
            sem=XT_SEMS[c], by=16,
            mark=("xt_chunk", c) if ji % DC == DC - 1 else None)
        if ji == DC - 1:
            P.waitm("sync", ("wq_cast", 0))
            for dcw in range(DC):
                P.op("sync", lambda s, en, dcw=dcw: en.dma_start(
                    out=wq_col[0][:, dcw * 512:(dcw + 1) * 512],
                    in_=wq_bf[0:512, dcw * 128:(dcw + 1) * 128],
                    transpose=True), sem="wqa", by=16)
            P.setmark(("wq_loaded", 0), "wqa")
        if (ji + 1) % 8 == 0 and (ji + 1) // 8 <= 8:
            emit_at_row((ji + 1) // 8 - 1)


    nq = E3 // 512
    P.waitm("tensor", "p0_sq")
    for i in range(nq + D // 512):
        src = b_qkv[:, i * 512:(i + 1) * 512] if i < nq else \
              b_out[:, (i - nq) * 512:(i - nq + 1) * 512]
        P.waitm("gpsimd", ("bc_mm", i - 2))
        P.op("gpsimd", lambda s, e, src=src, i=i: e.dma_start(
            out=bq_sl[i % 2][:, :], in_=src),
            sem=("bqa" if i % 2 == 0 else "bqb"), by=16, mark=("bq_sl", i))
        P.waitm("tensor", ("bq_sl", i))
        P.waitm("tensor", ("bc_copy", i - 1))
        P.op("tensor", lambda s, e, i=i: e.matmul(
            P2[0][:, :], onesr_sb[:, :], bq_sl[i % 2][:, :], start=True, stop=True),
            sem="tc", mark=("bc_mm", i))
        P.waitm("scalar", ("bc_mm", i))
        dst = bias_q_bc[:, i * 512:(i + 1) * 512] if i < nq else \
              bias_o_bc[:, (i - nq) * 512:(i - nq + 1) * 512]
        P.op("scalar", lambda s, e, dst=dst: e.copy(dst, P2[0][:, :]),
             sem="ac", mark=("bc_copy", i))
    P.setmark("bias_bc_done", "ac")

    for _e in range(1, EC):
        P.wait("gpsimd", "wcq", 16 * _e)
        P.op("gpsimd", lambda s, e, _e=_e: e.dma_start(
            out=wq_bf[_e * 512:(_e + 1) * 512, :],
            in_=w_qkv[_e * 512:(_e + 1) * 512, :]),
            sem="wcq", by=16, mark=("wq_cast", _e))
    P.op("gpsimd", lambda s, e: e.dma_start(out=wo_bf[:, :], in_=w_out[:, :]),
         sem="wcq", by=16, mark="wo_cast")

    for row in range(8):
        for rp in range(cfg.WINR):
            k = row * cfg.WINR + rp
            P.waitm("vector", ("at_ld", k))
            if k == 0:
                P.waitm("vector", "cpw_done")
            atv = at_sb[k % 3].rearrange("p (g w) -> p g w", w=128)[:, :, :]
            for cpi in range(3):
                cpws = cpw_sb[cpi].rearrange(
                    "p (g w) -> p g w", w=cfg.WINW)[:, :, rp:rp + 128]
                if rp == 0:
                    P.waitm("vector", ("rrow_st", (row - 1) * 3 + 2))
                    P.op("vector", lambda s, e, cpws=cpws, atv=atv, cpi=cpi:
                         e.tensor_tensor(
                             racc3_sb[cpi].rearrange(
                                 "p (g w) -> p g w", w=128)[:, :, :],
                             atv, cpws, ALU.mult), sem="vc",
                         mark=("rprep_use", k) if cpi == 2 else None)
                else:
                    P.op("vector", lambda s, e, cpws=cpws, atv=atv: e.tensor_tensor(
                        rtmp_sb.rearrange("p (g w) -> p g w", w=128)[:, :, :],
                        atv, cpws, ALU.mult), sem="vc")
                    P.op("vector", lambda s, e, cpi=cpi: e.tensor_tensor(
                        racc3_sb[cpi][:, :], racc3_sb[cpi][:, :], rtmp_sb[:, :],
                        ALU.add), sem="vc",
                        mark=("rprep_use", k) if cpi == 2 else None)
        P.setmark(("rrow_vec", row), "vc")
        PPC = min(cfg.NCH, 128)
        P.waitm("scalar", ("rrow_vec", row))
        for cpi in range(3):
            dst = r_dr[cpi][row:row + 1, :].rearrange("o (g pp w) -> (o pp) g w",
                                                      pp=PPC, w=128)
            P.op("scalar", lambda s, e, dst=dst, cpi=cpi: e.dma_start(
                out=dst,
                in_=racc3_sb[cpi][0:PPC, :].rearrange(
                    "p (g w) -> p g w", w=128)[:, :, :]),
                sem="rwq", by=16, mark=("rrow_st", row * 3 + cpi))
    P.setmark("rprep_all", "vc")
    P.setmark("rprep_out_all", "rwq")

    kq = 0
    for e in range(EC):
        nm = "wqa" if e % 2 == 0 else "wqb"
        if e > 0:
            P.waitm("sync", ("qkv_mm_blk", e - 2))
            P.waitm("sync", ("wq_cast", e))
            for dc in range(DC):
                P.op("sync", lambda s, en, e=e, dc=dc: en.dma_start(
                    out=wq_col[e % 2][:, dc * 512:(dc + 1) * 512],
                    in_=wq_bf[e * 512:(e + 1) * 512, dc * 128:(dc + 1) * 128],
                    transpose=True), sem=nm, by=16)
            P.setmark(("wq_loaded", e), nm)
        P.waitm("tensor", ("wq_loaded", e))
        for tt in range(TT):
            if e == 0:
                P.waitm("tensor", ("xt_chunk", tt // 4))
            pb = Q2[kq % 2]
            P.waitm("tensor", ("qkv_ev", kq - 2))
            for dc in range(DC):
                last = dc == DC - 1
                P.op("tensor", lambda s, en, e=e, tt=tt, dc=dc, pb=pb:
                     en.matmul(pb[:, :],
                               xT_sb[:, dc * TOWN + tt * 128: dc * TOWN + (tt + 1) * 128],
                               wq_col[e % 2][:, dc * 512:(dc + 1) * 512],
                               start=(dc == 0), stop=(dc == DC - 1)),
                     sem=("tc" if last else None),
                     mark=(("qkv_mm", kq) if last else None))
            P.waitm("vector", ("qkv_mm", kq))
            P.waitm("vector", "bias_bc_done")
            P.waitm("vector", ("qkv_out", kq - 2))
            P.op("vector", lambda s, en, e=e, kq=kq, pb=pb: en.tensor_tensor(
                ev_sb[kq % 2][:, :], pb[:, :],
                bias_q_bc[:, e * 512:(e + 1) * 512], ALU.add),
                sem="vc", mark=("qkv_ev", kq))
            P.waitm("gpsimd", ("qkv_ev", kq))
            if e < D // 512:
                dst = q_dr[tt * 128:(tt + 1) * 128, e * 512:(e + 1) * 512]
            else:
                ek = e - D // 512
                dst = (k_own if ek < D // 512 else v_own)[
                    tt * 128:(tt + 1) * 128,
                    (ek % (D // 512)) * 512:(ek % (D // 512) + 1) * 512]
            is_k = (D // 512) <= e < 2 * (D // 512)
            nmo = ("eka" if kq % 2 == 0 else "ekb") if is_k else \
                  ("eva" if kq % 2 == 0 else "evb")
            P.op("gpsimd", lambda s, en, dst=dst, kq=kq: en.dma_start(
                out=dst, in_=ev_sb[kq % 2][:, :]), sem=nmo, by=16,
                mark=("qkv_out", kq))
            kq += 1
        P.setmark(("qkv_mm_blk", e), "tc")
    P.setmark("qkv_done", "tc")
    P.setmark("qkv_ev_all", "vc")

    P.wait("gpsimd", "eka", P.n("eka"))
    P.wait("gpsimd", "ekb", P.n("ekb"))
    P.op("gpsimd", lambda s, e: e.collective_compute(
        "AllGather", ALU.bypass,
        replica_groups=[[0, 1, 2, 3], [4, 5, 6, 7]],
        ins=[k_own[:, :].opt()], outs=[k_all[:, :].opt()]),
        sem="cc", by="dflt", mark="kv_all")
    P.wait("gpsimd", "eva", P.n("eva"))
    P.wait("gpsimd", "evb", P.n("evb"))
    P.op("gpsimd", lambda s, e: e.collective_compute(
        "AllGather", ALU.bypass,
        replica_groups=[[0, 1, 2, 3], [4, 5, 6, 7]],
        ins=[v_own[:, :].opt()], outs=[v_all[:, :].opt()]),
        sem="cc", by="dflt", mark="kv_allv")

    NBLK = cfg.A // 512
    P.waitm("gpsimd", "rprep_out_all")
    PSLOT = [(S2[0][:, 0:512], S2[0][:, 512:1024]),
             (S2[1][:, 0:512], S2[1][:, 512:1024]),
             (P2[0][:, :], P2[1][:, :]),
             (Q2[0][:, :], Q2[1][:, :])]
    WS_SEMS = ("ws0", "ws1", "ws2", "ws3", "ws4", "ws5")
    NK_RC = NBLK * 3

    def rc_emit_loads(k):
        blk, cpi = divmod(k, 3)
        if cpi == 0:
            nmm = "mka" if blk % 2 == 0 else "mkb"
            P.waitm("sync", ("rc_mult", (blk - 2) * 3 + 2))
            P.op("sync", lambda s, e, blk=blk: e.dma_start(
                out=msk_sb[blk % 2][:, :],
                in_=t_mask[:, blk * 512:(blk + 1) * 512]),
                sem=nmm, by=16, mark=("mk_ld", blk))
        nm = ("rwa", "rwb", "rwc", "rwd")[k % 4]
        P.waitm("gpsimd", ("rc_mm", k - 4))
        P.op("gpsimd", lambda s, e, cpi=cpi, blk=blk, k=k: e.dma_start(
            out=rldl_sb[k % 4].rearrange("p (h c) -> p h c", h=2)[:, :, :],
            in_=r_dr[cpi][:, blk * 512:(blk + 1) * 512]
            .rearrange("(h p) c -> p h c", p=4)), sem=nm, by=16,
            mark=("rdl_ld", k))

    def rc_emit_mm(k):
        lo_ps, hi_ps = PSLOT[k % 4]
        P.waitm("tensor", ("rdl_ld", k))
        if k == 0:
            P.waitm("tensor", "p0_sq")
            P.waitm("tensor", "qkv_ev_all")
            P.waitm("tensor", "bias_bc_done")
        P.waitm("tensor", ("rc_mult", k - 4))
        P.op("tensor", lambda s, e, hi_ps=hi_ps, k=k: e.matmul(
            hi_ps, basis_sb[:, :], rldl_sb[k % 4][:, 512:1024],
            start=True, stop=True), sem="tc", mark=("rc_hi", k))
        P.waitm("tensor", ("rc_add", k - 4))
        P.op("tensor", lambda s, e, lo_ps=lo_ps, k=k: e.matmul(
            lo_ps, basis_sb[:, :], rldl_sb[k % 4][:, 0:512],
            start=True, stop=True), sem="tc", mark=("rc_mm", k))

    def rc_emit_vec(k):
        blk = k // 3
        lo_ps, hi_ps = PSLOT[k % 4]
        P.waitm("vector", ("rc_hi", k))
        P.waitm("vector", ("mk_ld", blk))
        P.op("vector", lambda s, e, k=k, blk=blk, hi_ps=hi_ps: e.tensor_tensor(
            wsel_sb[k % 2][:, :], hi_ps, msk_sb[blk % 2][:, :],
            ALU.mult), sem="vc", mark=("rc_mult", k))
        P.waitm("vector", ("rc_mm", k))
        P.waitm("vector", ("rc_cpy", k - 6))
        P.op("vector", lambda s, e, k=k, lo_ps=lo_ps: e.tensor_tensor(
            wout_sb[k % 6][:, :], wsel_sb[k % 2][:, :], lo_ps,
            ALU.add), sem="vc", mark=("rc_add", k))

    def rc_emit_copy(k):
        cpi = k % 3
        stride = SG if cpi < 2 else SD
        il = 512 // stride
        pview = wpi_sb[k % 6].rearrange("p (c il) -> p il c", il=il)[:, :, :]
        eng = "gpsimd" if (k % 5) in (1, 3) else "scalar"
        P.waitm(eng, ("rc_add", k))
        P.waitm(eng, ("w_out", k - 6))
        if eng == "scalar":
            P.op("scalar", lambda s, e, k=k, pview=pview: e.copy(
                pview, wout_sb[k % 6][:, :]), sem="ac", mark=("rc_cpy", k))
        else:
            P.op("gpsimd", lambda s, e, k=k, pview=pview: e.tensor_copy(
                pview, wout_sb[k % 6][:, :]), sem="pc", mark=("rc_cpy", k))
        blk, cpi = divmod(k, 3)
        P.waitm(eng, ("rc_cpy", k))
        P.op(eng, lambda s, e, cpi=cpi, blk=blk, k=k: e.dma_start(
            out=w_dr[cpi][:, blk * 512:(blk + 1) * 512],
            in_=wpi_sb[k % 6][:, :]), sem=WS_SEMS[k % 6], by=16,
            mark=("w_out", k))

    rc_emit_loads(0)
    rc_emit_loads(1)
    for k in range(NK_RC + 2):
        if k + 2 < NK_RC:
            rc_emit_loads(k + 2)
        if k < NK_RC:
            rc_emit_mm(k)
            rc_emit_vec(k)
        if k >= 2:
            rc_emit_copy(k - 2)
    P.setmark("recon_sel_all", "vc")
    P.setmark("recon_mm_all", "tc")
    WS_TOT = [P.n(nm) for nm in WS_SEMS]

    def pass_list(sl):
        out = [("diag", (k_own, v_own), sl * G, sum(cfg.PADS[:sl]) + sl)]
        for j in range(cfg.PADS[sl]):
            out.append(("rect", (k_all, v_all), pos[j],
                        sum(cfg.PADS[:sl]) + sl + 1 + j))
        return out

    P.wait("sync", "eva", P.n("eva"))
    P.wait("sync", "evb", P.n("evb"))
    P.wait("sync", "eka", P.n("eka"))
    P.wait("sync", "ekb", P.n("ekb"))
    P.waitm("sync", "wo_cast")
    for j in range(MC * DC):
        mc, dc = divmod(j, DC)
        npar = "wta" if j % 2 == 0 else "wtb"
        P.waitm("sync", ("wo_st", j - 2))
        P.op("sync", lambda s, e, mc=mc, dc=dc, j=j: e.dma_start(
            out=ev_sb[j % 2][:, :],
            in_=wo_bf[mc * 512:(mc + 1) * 512, dc * 128:(dc + 1) * 128],
            transpose=True), sem=npar, by=16, mark=("wo_tld", j))
        P.waitm("gpsimd", ("wo_tld", j))
        P.op("gpsimd", lambda s, e, mc=mc, dc=dc, j=j: e.dma_start(
            out=woT_dr[dc * 128:(dc + 1) * 128, mc * 512:(mc + 1) * 512],
            in_=ev_sb[j % 2][:, :]), sem=("wqoa" if j % 2 == 0 else "wqob"),
            by=16, mark=("wo_st", j))

    P.waitm("sync", "rprep_all")
    sq(tri_sb[:, :], t_tri[:, :], mark="tri_ld")
    P.waitm("vector", "tri_ld")
    gidx = 0
    kidx = 0
    for h in range(H):
        nmq = "qla" if h % 2 == 0 else "qlb"
        nmk = "kfa" if h % 2 == 0 else "kfb"
        P.waitm("sync", ("attn_h_smm", h - 2))
        for t0 in range(0, TOWN, 512):
            P.op("sync", lambda s, e, h=h, t0=t0: e.dma_start(
                out=q_sb[h % 2][:, t0:t0 + 512],
                in_=q_dr[t0:t0 + 512, h * 128:(h + 1) * 128], transpose=True),
                sem=nmq, by=16)
        P.setmark(("q_ld", h), nmq)
        if h == 0:
            P.waitm("sync", "qkv_done")
            P.waitm("sync", "kv_all")
        for t0 in range(0, 4 * TOWN, 2048):
            P.op("sync", lambda s, e, h=h, t0=t0: e.dma_start(
                out=kf_sb[h % 2][:, t0:t0 + 2048],
                in_=k_all[t0:t0 + 2048, h * 128:(h + 1) * 128], transpose=True),
                sem=nmk, by=16)
        P.setmark(("kf_ld", h), nmk)
        P.waitm("tensor", ("q_ld", h))
        P.waitm("tensor", ("kf_ld", h))
        for sl in range(4):
            pend_av = [None]
            pl = pass_list(sl)
            npass = len(pl)
            for pi, (kind, src, row0, bidx) in enumerate(pl):
                nm = "kta" if kidx % 2 == 0 else "ktb"
                nmv = "vta" if kidx % 2 == 0 else "vtb"
                P.waitm("sync", ("kv_use", kidx - 2))
                if kind == "rect" and h == 0 and sl == 0 and pi == 1:
                    P.waitm("sync", "kv_allv")
                if kind == "diag":
                    P.op("sync", lambda s, e, src=src, row0=row0, h=h, kidx=kidx:
                         e.dma_start(out=k_sb[kidx % 2][:, :],
                                     in_=src[0][row0:row0 + G, h * 128:(h + 1) * 128],
                                     transpose=True), sem=nm, by=16, mark=("k_ld", kidx))
                P.op("sync", lambda s, e, src=src, row0=row0, h=h, kidx=kidx:
                     e.dma_start(
                         out=v_sb[kidx % 2].rearrange("p (kt d) -> p kt d", d=128)[:, :, :],
                         in_=src[1][row0:row0 + G, h * 128:(h + 1) * 128]
                         .rearrange("(kt p) d -> p kt d", p=128)),
                     sem=nmv, by=16, mark=("v_ld", kidx))
                if kind == "diag":
                    P.waitm("tensor", ("k_ld", kidx))
                for grp in range(NGR):
                    gpar = gidx % 2
                    ppar = gidx % 4
                    if gidx == 0:
                        P.waitm("tensor", "recon_sel_all")
                    P.waitm("tensor", ("exp", gidx - 2))
                    for ktl in range(KTG):
                        kt = grp * KTG + ktl
                        last = ktl == KTG - 1
                        if kind == "diag":
                            klhs = k_sb[kidx % 2][:, kt * 128:(kt + 1) * 128]
                        else:
                            klhs = kf_sb[h % 2][:, row0 + kt * 128: row0 + (kt + 1) * 128]
                        P.op("tensor", lambda s, e, klhs=klhs, gpar=gpar, h=h, sl=sl,
                             ktl=ktl:
                             e.matmul(S2[gpar][:, ktl * G:(ktl + 1) * G],
                                      klhs,
                                      q_sb[h % 2][:, sl * G:(sl + 1) * G],
                                      start=True, stop=True),
                             sem=("tc" if last else None),
                             mark=(("s_mm", gidx) if last else None))
                    P.waitm("scalar", ("s_mm", gidx))
                    P.waitm("scalar", ("av_grp", gidx - 4))
                    P.op("scalar", lambda s, e, gpar=gpar, ppar=ppar, bidx=bidx:
                         e.activation(
                             p_sb[ppar][:, :], S2[gpar][:, 0:KTG * G], AF.Exp,
                             bias=bias_sb[:, bidx:bidx + 1], scale=SCL),
                         sem="ac", mark=("exp", gidx))
                    if kind == "diag":
                        P.waitm("vector", ("exp", gidx))
                        P.op("vector", lambda s, e, ppar=ppar, grp=grp: e.tensor_tensor(
                            p_sb[ppar][:, :], p_sb[ppar][:, :],
                            tri_sb[:, grp * KTG * G:(grp + 1) * KTG * G], ALU.mult),
                            sem="vc", mark=("tri", gidx))
                    gidx += 1
                def emit_av(kind=kind, kidx=kidx, g_base=gidx - NGR, pi=pi,
                           npass=npass, sl=sl, h=h):
                    P.waitm("tensor", ("v_ld", kidx))
                    first = pi == 0
                    last_pass = pi == npass - 1
                    if first:
                        P.waitm("tensor", ("attno_mul", h * 4 + sl - 2))
                        P.waitm("tensor", ("recip", h * 4 + sl - 1))
                        if h == 0 and sl == 0:
                            P.waitm("tensor", "bias_bc_done")
                            P.waitm("tensor", "qkv_ev_all")
                    for grp_i in range(NGR):
                        g2 = g_base + grp_i
                        ppar = g2 % 4
                        P.waitm("tensor", ("exp", g2))
                        if kind == "diag":
                            P.waitm("tensor", ("tri", g2))
                        for ktl in range(KTG):
                            kt = grp_i * KTG + ktl
                            av_last = last_pass and kt == NK - 1
                            grp_last = ktl == KTG - 1
                            P.op("tensor", lambda s, e, sl=sl, kidx=kidx, ppar=ppar,
                                 kt=kt, ktl=ktl, first=first, av_last=av_last:
                                 e.matmul(P2[sl % 2][:, 0:G],
                                          v_sb[kidx % 2][:, kt * 128:(kt + 1) * 128],
                                          p_sb[ppar][:, ktl * G:(ktl + 1) * G],
                                          start=(first and kt == 0), stop=av_last))
                            mk2 = ("avs_mm", h * 4 + sl) if av_last else \
                                  (("av_grp", g2) if grp_last else None)
                            srow = 32 * (kt % 4)
                            P.op("tensor", lambda s, e, ppar=ppar, kt=kt, ktl=ktl,
                                 first=first, last_pass=last_pass, srow=srow:
                                 e.matmul(Q2[0][srow:srow + 1, 0:G], ones_sb[:, :],
                                          p_sb[ppar][:, ktl * G:(ktl + 1) * G],
                                          start=first, stop=last_pass,
                                          tile_position=(0, srow)),
                                 sem=("tc" if (av_last or grp_last) else None),
                                 mark=mk2)
                        if last_pass and grp_i == NGR - 1:
                            P.setmark(("av_grp", g2), "tc")
                    P.setmark(("kv_use", kidx), "tc")
                if pend_av[0] is not None:
                    pend_av[0]()
                pend_av[0] = emit_av
                kidx += 1
            if pend_av[0] is not None:
                pend_av[0]()
                pend_av[0] = None
            P.waitm("scalar", ("avs_mm", h * 4 + sl))
            P.op("scalar", lambda s, e: e.copy(sums4_sb[:, :], Q2[0][:, 0:G]),
                 sem="ac", mark=("sums_cp", h * 4 + sl))
            P.waitm("tensor", ("sums_cp", h * 4 + sl))
            P.op("tensor", lambda s, e: e.matmul(
                Q2[0][0:1, 0:G], ones4_sb[:, :], sums4_sb[:, :],
                start=True, stop=True), sem="tc", mark=("sums_red", h * 4 + sl))
            P.waitm("vector", ("sums_red", h * 4 + sl))
            def _recip(s, e):
                with nc.allow_low_precision(reason="recip feeds bf16 matmul"):
                    return e.reciprocal(rec_sb[:, :], Q2[0][0:1, 0:G])
            P.op("vector", _recip, sem="vc", mark=("recip", h * 4 + sl))
            P.waitm("tensor", ("recip", h * 4 + sl))
            P.waitm("tensor", ("attno_mul", h * 4 + sl - 1))
            P.op("tensor", lambda s, e: e.matmul(
                Q2[1][:, 0:G], onesr_sb[:, :], rec_sb[:, :], start=True, stop=True),
                sem="tc", mark=("rbc", h * 4 + sl))
            P.waitm("scalar", ("avs_mm", h * 4 + sl))
            P.waitm("scalar", ("attno_mul", h * 4 + sl - 1))
            P.op("scalar", lambda s, e, sl=sl: e.copy(av_sb[:, :], P2[sl % 2][:, 0:G]),
                 sem="ac", mark=("av_cp", h * 4 + sl))
            P.waitm("vector", ("av_cp", h * 4 + sl))
            P.waitm("vector", ("rbc", h * 4 + sl))
            P.op("vector", lambda s, e, h=h, sl=sl: e.tensor_tensor(
                attno_sb[:, h * TOWN + sl * G: h * TOWN + (sl + 1) * G],
                av_sb[:, :], Q2[1][:, 0:G], ALU.mult),
                sem="vc", mark=("attno_mul", h * 4 + sl))
        P.setmark(("attn_h_smm", h), "tc")
    P.setmark("attn_done", "vc")
    P.setmark("attn_ac_all", "ac")
    P.setmark("attn_tc_all", "tc")

    P.waitm("sync", "rprep_all")
    P.waitm("sync", "qkv_done")
    P.waitm("tensor", "attn_done")
    P.waitm("sync", "attn_tc_all")
    P.wait("sync", "wqoa", P.n("wqoa"))
    P.wait("sync", "wqob", P.n("wqob"))
    wo_fv = wo_full.rearrange("p (dc mcm) -> p dc mcm", mcm=MC * 512)
    for mc in range(MC):
        P.op("sync", lambda s, e, mc=mc: e.dma_start(
            out=wo_fv[:, :, mc * 512:(mc + 1) * 512],
            in_=woT_dr[:, mc * 512:(mc + 1) * 512]
            .rearrange("(dc p) m -> p dc m", p=128)), sem="wfq", by=16,
            mark="wo_full_ld" if mc == MC - 1 else None)
    for tt in range(TT):
        for mc in range(MC):
            k4 = tt * MC + mc
            pb = Q2[k4 % 2]
            if k4 == 0:
                P.waitm("tensor", "wo_full_ld")
            P.waitm("tensor", ("p4_ev", k4 - 2))
            for dc in range(DC):
                last = dc == DC - 1
                P.op("tensor", lambda s, e, tt=tt, mc=mc, dc=dc, pb=pb, k4=k4:
                     e.matmul(pb[:, :],
                              attno_sb[:, dc * TOWN + tt * 128: dc * TOWN + (tt + 1) * 128],
                              wo_full[:, dc * MC * 512 + mc * 512:
                                      dc * MC * 512 + (mc + 1) * 512],
                              start=(dc == 0), stop=(dc == DC - 1)),
                     sem=("tc" if last else None),
                     mark=(("p4_mm", k4) if last else None))
            nx = "xca" if k4 % 2 == 0 else "xcb"
            P.waitm("gpsimd", ("p4_ev", k4 - 2))
            P.op("gpsimd", lambda s, e, tt=tt, mc=mc, k4=k4: e.dma_start(
                out=xr_sb[k4 % 2][:, :],
                in_=x_own[tt * 128:(tt + 1) * 128, mc * 512:(mc + 1) * 512]),
                sem=nx, by=16, mark=("x_ld", k4))
            P.waitm("vector", ("p4_mm", k4))
            P.waitm("vector", ("x_ld", k4))
            P.op("vector", lambda s, e, mc=mc, pb=pb: e.tensor_tensor(
                r1_sb[:, mc * 512:(mc + 1) * 512], pb[:, :],
                bias_o_bc[:, mc * 512:(mc + 1) * 512], ALU.add), sem="vc")
            P.op("vector", lambda s, e, mc=mc, k4=k4: e.tensor_tensor(
                r1_sb[:, mc * 512:(mc + 1) * 512],
                r1_sb[:, mc * 512:(mc + 1) * 512], xr_sb[k4 % 2][:, :], ALU.add),
                sem="vc", mark=("p4_ev", k4))
        P.op("vector", lambda s, e: e.reduce_sum(mean_sb[:, :], r1_sb[:, :], axis=AXX),
             sem="vc")
        P.op("vector", lambda s, e: e.tensor_scalar(
            mean_sb[:, :], mean_sb[:, :], 1.0 / D, None, ALU.mult), sem="vc")
        P.op("vector", lambda s, e: e.tensor_scalar(
            r1_sb[:, :], r1_sb[:, :], mean_sb[:, 0:1], None, ALU.subtract), sem="vc")
        P.op("vector", lambda s, e: e.tensor_tensor(
            sq_sb[:, :], r1_sb[:, :], r1_sb[:, :], ALU.mult), sem="vc")
        P.op("vector", lambda s, e: e.reduce_sum(var_sb[:, :], sq_sb[:, :], axis=AXX),
             sem="vc", mark=("ln1_var", tt))
        P.waitm("scalar", ("ln1_var", tt))
        P.op("scalar", lambda s, e: e.activation(
            rstd_sb[:, :], var_sb[:, :], AF.Ln, bias=eps_sb[:, 0:1], scale=1.0 / D), sem="ac")
        P.op("scalar", lambda s, e: e.activation(
            rstd_sb[:, :], rstd_sb[:, :], AF.Exp, scale=-0.5),
            sem="ac", mark=("ln1_rstd", tt))
        P.waitm("vector", ("ln1_rstd", tt))
        P.waitm("vector", ("h_out", tt - 2))
        P.op("vector", lambda s, e, tt=tt: e.tensor_scalar(
            hf_sb[tt % 2][:, :], r1_sb[:, :], rstd_sb[:, 0:1], None, ALU.mult),
            sem="vc", mark=("ln1_h", tt))
        P.waitm("gpsimd", ("ln1_h", tt))
        nhq = "hqa" if tt % 2 == 0 else "hqb"
        P.op("gpsimd", lambda s, e, tt=tt: e.dma_start(
            out=h32_dr[tt * 128:(tt + 1) * 128, :], in_=hf_sb[tt % 2][:, :]),
            sem=nhq, by=16)
        P.op("gpsimd", lambda s, e, tt=tt: e.dma_start(
            out=hbf_dr[tt * 128:(tt + 1) * 128, :], in_=hf_sb[tt % 2][:, :]),
            sem=nhq, by=16, mark=("h_out", tt))
    P.setmark("p4_mm_all", "tc")
    P.setmark("p4_vec_all", "vc")

    P.wait("sync", "hqa", P.n("hqa"))
    P.wait("sync", "hqb", P.n("hqb"))
    P.waitm("sync", "p4_vec_all")
    kg = 0
    for tb in range(NTB):
        t_base = tb * TB
        P.waitm("sync", ("gu_blk", tb - 1))
        for dc in range(DC):
            for t0 in range(0, TB, 512):
                P.op("sync", lambda s, e, dc=dc, t0=t0, t_base=t_base: e.dma_start(
                    out=hT_sb[:, dc * TB + t0: dc * TB + t0 + 512],
                    in_=hbf_dr[t_base + t0: t_base + t0 + 512, dc * 128:(dc + 1) * 128],
                    transpose=True), sem="htq", by=16)
        P.setmark(("hT_ld", tb), "htq")
        P.waitm("tensor", ("hT_ld", tb))
        for ic in range(NIC):
            nm = "wga" if ic % 2 == 0 else "wgb"
            P.waitm("gpsimd", ("gu_mm", (tb * NIC + ic - 2 + 1) * NTTB - 1))
            if tb == 0 and ic == 0:
                for _nm, _v in zip(WS_SEMS, WS_TOT):
                    P.wait("gpsimd", _nm, _v)
                P.waitm("gpsimd", "p4_mm_all")
            P.op("gpsimd", lambda s, e, ic=ic: e.dma_start(
                out=wg_blk[ic % 2][:, :],
                in_=w_dr[0][:, ic * IC * SG:(ic + 1) * IC * SG]), sem=nm, by=16)
            P.op("gpsimd", lambda s, e, ic=ic: e.dma_start(
                out=wu_blk[ic % 2][:, :],
                in_=w_dr[1][:, ic * IC * SG:(ic + 1) * IC * SG]), sem=nm, by=16,
                mark=("wg_ld", tb * NIC + ic))
            P.waitm("tensor", ("wg_ld", tb * NIC + ic))
            for tt in range(NTTB):
                gpar = kg % 2
                gq4 = kg % 4
                gps = (S2[0][:, 0:512], S2[1][:, 0:512], P2[0][:, :], Q2[0][:, :])[gq4]
                ups = (S2[0][:, 512:1024], S2[1][:, 512:1024], P2[1][:, :],
                       Q2[1][:, :])[gq4]
                if kg == 0:
                    P.waitm("tensor", "attn_ac_all")
                    P.waitm("tensor", "attn_done")
                    P.waitm("tensor", ("p4_ev", TT * MC - 1))
                P.waitm("tensor", ("gu_ev", kg - 4))
                BIL = 512 // SG
                wgv = wg_blk[ic % 2].rearrange("p (blk x) -> p blk x", x=512)
                wuv = wu_blk[ic % 2].rearrange("p (blk x) -> p blk x", x=512)
                for dc in range(DC):
                    last = dc == DC - 1
                    P.op("tensor", lambda s, e, dc=dc, tt=tt, gps=gps, wgv=wgv, BIL=BIL:
                         e.matmul(gps,
                                  hT_sb[:, dc * TB + tt * 128: dc * TB + (tt + 1) * 128],
                                  wgv[:, :, dc * BIL:(dc + 1) * BIL],
                                  start=(dc == 0), stop=(dc == DC - 1)))
                    P.op("tensor", lambda s, e, dc=dc, tt=tt, ups=ups, wuv=wuv, BIL=BIL:
                         e.matmul(ups,
                                  hT_sb[:, dc * TB + tt * 128: dc * TB + (tt + 1) * 128],
                                  wuv[:, :, dc * BIL:(dc + 1) * BIL],
                                  start=(dc == 0), stop=(dc == DC - 1)),
                         sem=("tc" if last else None),
                         mark=(("gu_mm", kg) if last else None))
                P.waitm("scalar", ("gu_mm", kg))
                P.waitm("scalar", ("gu_t1", kg - 1))
                P.op("scalar", lambda s, e, gps=gps: e.activation(
                    sg_sb[:, :], gps, AF.Sigmoid),
                    sem="ac", mark=("gu_sig", kg))
                P.waitm("vector", ("gu_sig", kg))
                P.op("vector", lambda s, e, gps=gps: e.tensor_tensor(
                    t1_sb[:, :], gps, sg_sb[:, :], ALU.mult),
                    sem="vc", mark=("gu_t1", kg))
                P.waitm("vector", ("iv_out", kg - 2))
                P.op("vector", lambda s, e, ups=ups, kg=kg: e.tensor_tensor(
                    iv_sb[kg % 2][:, :], t1_sb[:, :], ups,
                    ALU.mult), sem="vc", mark=("gu_ev", kg))
                P.waitm("gpsimd", ("gu_ev", kg))
                nmi = "iva" if kg % 2 == 0 else "ivb"
                P.op("gpsimd", lambda s, e, ic=ic, tt=tt, t_base=t_base, kg=kg:
                     e.dma_start(out=inter_dr[t_base + tt * 128: t_base + (tt + 1) * 128,
                                              ic * IC:(ic + 1) * IC],
                                 in_=iv_sb[kg % 2][:, :]), sem=nmi, by=16,
                     mark=("iv_out", kg))
                kg += 1
        P.setmark(("gu_blk", tb), "tc")
    P.setmark("gu_all_tc", "tc")
    P.setmark("gu_all_ac", "ac")
    P.wait("gpsimd", "iva", P.n("iva"))
    P.wait("gpsimd", "ivb", P.n("ivb"))
    P.setmark("inter_all", "gq")

    IVA_ALL, IVB_ALL = P.n("iva"), P.n("ivb")
    kd = 0
    for sp in range(NSP):
        t0 = sp * SPT * 128
        P.waitm("sync", "p4_mm_all")
        P.wait("sync", "iva", IVA_ALL)
        P.wait("sync", "ivb", IVB_ALL)
        P.waitm("sync", ("dn_sp", sp - 1))
        nit = "itqa" if sp % 2 == 0 else "itqb"
        for ii in range(NII):
            P.op("sync", lambda s, e, ii=ii, t0=t0, sp=sp: e.dma_start(
                out=iT_sb[sp % 2][:, ii * SPT * 128:(ii + 1) * SPT * 128],
                in_=inter_dr[t0: t0 + SPT * 128, ii * 128:(ii + 1) * 128],
                transpose=True), sem=nit, by=16)
        P.setmark(("iT_ld", sp), nit)
        P.waitm("tensor", ("iT_ld", sp))
        for hc in range(NHC):
            nms = "sba" if hc % 2 == 0 else "sbb"
            P.waitm("gpsimd", ("dn_mm", kd - SPT - 1))
            if sp == 0 and hc == 0:
                P.waitm("gpsimd", "gu_all_tc")
            P.op("gpsimd", lambda s, e, hc=hc: e.dma_start(
                out=slab_sb[hc % 2][:, :],
                in_=w_dr[2][:, hc * HC * SD:(hc + 1) * HC * SD]), sem=nms, by=16,
                mark=("slab_ld", sp * NHC + hc))
            P.waitm("tensor", ("slab_ld", sp * NHC + hc))
            BHL = 512 // SD
            slv = slab_sb[hc % 2].rearrange("p (blk x) -> p blk x", x=512)
            for ttl in range(SPT):
                pdp = (P2[0], P2[1], Q2[0], Q2[1])[kd % 4]
                if kd == 0:
                    P.waitm("tensor", ("gu_ev", NTB * NIC * NTTB - 1))
                P.waitm("tensor", ("dn_ev", kd - 4))
                for ii in range(NII):
                    last = ii == NII - 1
                    P.op("tensor", lambda s, e, ii=ii, ttl=ttl, pdp=pdp, slv=slv, BHL=BHL,
                         sp=sp:
                         e.matmul(pdp[:, 0:HC],
                                  iT_sb[sp % 2][:, (ii * SPT + ttl) * 128:(ii * SPT + ttl + 1) * 128],
                                  slv[:, :, ii * BHL:(ii + 1) * BHL],
                                  start=(ii == 0), stop=(ii == NII - 1)),
                         sem=("tc" if last else None),
                         mark=(("dn_mm", kd) if last else None))
                nh3 = "h3a" if kd % 2 == 0 else "h3b"
                P.waitm("sync", ("dn_ev", kd - 2))
                P.op("sync", lambda s, e, ttl=ttl, hc=hc, t0=t0, kd=kd: e.dma_start(
                    out=h32t_sb[kd % 2][:, :],
                    in_=h32_dr[t0 + ttl * 128: t0 + (ttl + 1) * 128,
                               hc * HC:(hc + 1) * HC]), sem=nh3, by=16,
                    mark=("h3_ld", kd))
                P.waitm("vector", ("dn_mm", kd))
                P.waitm("vector", ("h3_ld", kd))
                P.op("vector", lambda s, e, ttl=ttl, hc=hc, pdp=pdp, kd=kd:
                     e.tensor_tensor(
                         r2_sb[:, ttl * D + hc * HC: ttl * D + (hc + 1) * HC],
                         pdp[:, 0:HC], h32t_sb[kd % 2][:, :], ALU.add),
                     sem="vc", mark=("dn_ev", kd))
                kd += 1
        for ttl in range(SPT):
            r2v = r2_sb[:, ttl * D:(ttl + 1) * D]
            P.op("vector", lambda s, e, r2v=r2v: e.reduce_sum(
                mean_sb[:, :], r2v, axis=AXX), sem="vc")
            P.op("vector", lambda s, e: e.tensor_scalar(
                mean_sb[:, :], mean_sb[:, :], 1.0 / D, None, ALU.mult), sem="vc")
            P.op("vector", lambda s, e, r2v=r2v: e.tensor_scalar(
                r2v, r2v, mean_sb[:, 0:1], None, ALU.subtract), sem="vc")
            P.op("vector", lambda s, e, r2v=r2v: e.tensor_tensor(
                sq2_sb[:, :], r2v, r2v, ALU.mult), sem="vc")
            P.op("vector", lambda s, e: e.reduce_sum(
                var_sb[:, :], sq2_sb[:, :], axis=AXX), sem="vc",
                mark=("ln2_var", sp * SPT + ttl))
            P.waitm("scalar", ("ln2_var", sp * SPT + ttl))
            P.op("scalar", lambda s, e: e.activation(
                rstd_sb[:, :], var_sb[:, :], AF.Ln, bias=eps_sb[:, 0:1], scale=1.0 / D), sem="ac")
            P.op("scalar", lambda s, e: e.activation(
                rstd_sb[:, :], rstd_sb[:, :], AF.Exp, scale=-0.5),
                sem="ac", mark=("ln2_rstd", sp * SPT + ttl))
            gy = sp * SPT + ttl
            P.waitm("vector", ("ln2_rstd", gy))
            P.waitm("vector", ("y_out", gy - 1))
            P.op("vector", lambda s, e, r2v=r2v, gy=gy: e.tensor_scalar(
                y_sb[0][:, :], r2v, rstd_sb[:, 0:1], None, ALU.mult),
                sem="vc", mark=("ln2_y", gy))
            P.waitm("gpsimd", ("ln2_y", gy))
            nmy = "yqa" if gy % 2 == 0 else "yqb"
            P.op("gpsimd", lambda s, e, gy=gy, ttl=ttl, t0=t0: e.dma_start(
                out=y_own[t0 + ttl * 128: t0 + (ttl + 1) * 128, :],
                in_=y_sb[0][:, :]), sem=nmy, by=16, mark=("y_out", gy))
        P.setmark(("dn_sp", sp), "tc")
    P.wait("gpsimd", "yqa", P.n("yqa"))
    P.wait("gpsimd", "yqb", P.n("yqb"))

    semnames = set(P.cnt.keys())
    for e in ENGINES:
        for item in P.ops[e]:
            if item[0] == "wait":
                semnames.add(item[1])
    sems = {name: stack.enter_context(nc.semaphore(name)) for name in sorted(semnames)}

    with nc.Block() as block:
        selfsem = {"vector": "vc", "scalar": "ac"}
        def mk(eng_name):
            def go(eng):
                own = selfsem.get(eng_name)
                nown = 0
                for item in P.ops[eng_name]:
                    if item[0] == "wait":
                        _, s, v = item
                        eng.wait_ge(sems[s], v)
                    else:
                        _, fn, s, by = item
                        if own is not None and nown > 0:
                            eng.wait_ge(sems[own], nown)
                        inst = fn(sems, eng)
                        if s is not None:
                            if by == "dflt":
                                inst.then_inc(sems[s])
                            else:
                                inst.then_inc(sems[s], by)
                                if s == own:
                                    nown += by
            return go
        block.tensor(mk("tensor"))
        block.vector(mk("vector"))
        block.scalar(mk("scalar"))
        block.sync(mk("sync"))
        block.gpsimd(mk("gpsimd"))

    stack.close()
    return nc


def shard_inputs(cfg, inputs):
    x = np.asarray(inputs["L0_attn_x"])
    wq = np.ascontiguousarray(np.asarray(inputs["L0_attn_w_qkv"], dtype=np.float32))
    bq = np.ascontiguousarray(np.asarray(inputs["L0_attn_b_qkv"], dtype=np.float32)[None, :])
    wo = np.ascontiguousarray(np.asarray(inputs["L0_attn_w_out"], dtype=np.float32))
    bo = np.ascontiguousarray(np.asarray(inputs["L0_attn_b_out"], dtype=np.float32)[None, :])
    cps = [np.asarray(inputs[k], dtype=np.float32)
           for k in ("L0_ffn_gate_cp", "L0_ffn_up_cp", "L0_ffn_down_cp")]
    tabs = recon_tables(cfg)
    tri = tri_masks(cfg)
    ones = np.ones((128, 1), dtype=BF)
    ones4 = np.zeros((128, 1), dtype=BF)
    for _r in range(min(cfg.NK, 4)):
        ones4[32 * _r, 0] = 1.0
    onesr = np.ones((1, 128), dtype=BF)
    cpp = [np.ascontiguousarray(np.concatenate(
        [np.full(cfg.PAD_LO, c[0]), c, np.full(cfg.PAD_HI, c[-1])]
        ).astype(np.float32)[None, :]) for c in cps]

    in_maps = []
    for core in range(8):
        b, r = divmod(core, 4)
        chunks = cfg.zig_chunks(r)
        xs = np.concatenate([x[b, c * cfg.G:(c + 1) * cfg.G, :] for c in chunks], axis=0)
        in_maps.append(dict(
            x_own=np.ascontiguousarray(xs, dtype=np.float32),
            w_qkv=wq, b_qkv=bq, w_out=wo, b_out=bo,
            cpp0=cpp[0], cpp1=cpp[1], cpp2=cpp[2],
            t_off=tabs["off"], t_atab=tabs["atab"], t_mask=tabs["mask"],
            t_basis=tabs["basis"], t_bias=attn_bias_table(cfg, r), t_tri=tri,
            t_ones=ones, t_onesr=onesr, t_ones4=ones4,
            t_eps=np.full((128, 1), 1e-5, dtype=np.float32),
        ))
    return in_maps


def assemble_output(cfg, results):
    y = np.zeros((cfg.B, cfg.S, cfg.D), dtype=np.float32)
    for core in range(8):
        b, r = divmod(core, 4)
        yo = results[core]["y_own"]
        for s, c in enumerate(cfg.zig_chunks(r)):
            y[b, c * cfg.G:(c + 1) * cfg.G, :] = yo[s * cfg.G:(s + 1) * cfg.G, :]
    return y


def kernel(L0_attn_x, L0_attn_w_qkv, L0_attn_b_qkv, L0_attn_w_out, L0_attn_b_out,
           L0_ffn_gate_cp, L0_ffn_up_cp, L0_ffn_down_cp):
    cfg = Cfg()
    inputs = dict(L0_attn_x=L0_attn_x, L0_attn_w_qkv=L0_attn_w_qkv,
                  L0_attn_b_qkv=L0_attn_b_qkv, L0_attn_w_out=L0_attn_w_out,
                  L0_attn_b_out=L0_attn_b_out, L0_ffn_gate_cp=L0_ffn_gate_cp,
                  L0_ffn_up_cp=L0_ffn_up_cp, L0_ffn_down_cp=L0_ffn_down_cp)
    in_maps = shard_inputs(cfg, inputs)
    nc = build_nc(cfg)
    res = run_bass_kernel_spmd(nc, in_maps, core_ids=list(range(8)))
    return assemble_output(cfg, list(res.results))


# version 49
# speedup vs baseline: 1.0315x; 1.0242x over previous
import numpy as np
import ml_dtypes
from math import comb

import concourse.bass as bass
import concourse.mybir as mybir
from concourse.bass_utils import run_bass_kernel_spmd
from concourse.bass_core_rust import IndirectOffsetOnAxis

F32 = mybir.dt.float32
BF16 = mybir.dt.bfloat16
I32 = mybir.dt.int32
AF = mybir.ActivationFunctionType
ALU = mybir.AluOpType
AXX = mybir.AxisListType.X
BF = ml_dtypes.bfloat16


class Cfg:
    def __init__(self, B=2, S=8192, D=2048, H=16, INTER=8192):
        self.B, self.S, self.D, self.H, self.INTER = B, S, D, H, INTER
        self.HD = D // H
        assert self.HD == 128 and D % 512 == 0 and INTER % 512 == 0
        self.E3 = 3 * D
        self.G = S // 16
        assert self.G % 128 == 0 and self.G <= 512
        self.NK = self.G // 128
        self.TOWN = 4 * self.G
        self.TT = self.TOWN // 128
        self.DC = D // 128
        self.PADS = [4, 8, 12, 16]
        self.C = max(16, int(INTER * D / 128.9))
        self.NS = INTER * D
        self.A = self.NS // 128
        self.NCH = self.A // 128
        self.NG = (self.NCH + 127) // 128
        self.WINW = 136
        self.WINR = 6
        self.PAD_LO, self.PAD_HI = 2, 16
        self.CPP = self.C + self.PAD_LO + self.PAD_HI
        self.dt = float(np.float64(self.C - 1.0) / np.float64(self.NS - 1))
        self.SG = D // 128
        self.SD = INTER // 128
        self.TB = min(2048, self.TOWN)
        self.NTB = self.TOWN // self.TB
        self.IC = 512
        self.NIC = INTER // self.IC
        self.HC = 256
        self.NHC = D // self.HC
        self.SPT = min(4, self.TT)
        self.NSP = self.TT // self.SPT
        self.KT_GRP = min(2, self.NK) if self.NK > 1 else 1
        while self.NK % self.KT_GRP:
            self.KT_GRP -= 1
        self.ESH = 10.0
        self.EC = self.E3 // 512
        self.MC = D // 512
        self.NPASS = 4 + sum(self.PADS)

    def zig_chunks(self, r):
        return [r, 7 - r, 8 + r, 15 - r]

    def pos_row(self):
        pos = {}
        for r in range(4):
            for s, cid in enumerate(self.zig_chunks(r)):
                pos[cid] = r * self.TOWN + s * self.G
        return pos


CRB = 0.5 * np.array([
    [0., 2., 0., 0.], [-1., 0., 1., 0.], [2., -5., 4., -1.], [-1., 3., -3., 1.],
], dtype=np.float64)

_TABLE_CACHE = {}

def recon_tables(cfg):
    key = (cfg.C, cfg.NS)
    if key in _TABLE_CACHE:
        return _TABLE_CACHE[key]
    C, n, A = cfg.C, cfg.NS, cfg.A
    t = np.linspace(0.0, C - 1.0, n, dtype=np.float64)
    i_raw = np.clip(np.floor(t).astype(np.int64), 0, C - 2)
    b_a = i_raw[::128].copy()
    jj = np.arange(n, dtype=np.int64)
    delta = (i_raw - b_a[jj // 128])
    assert delta.min() >= 0 and delta.max() <= 1
    mask = np.zeros((128, A), dtype=np.float32)
    mask[jj % 128, jj // 128] = delta
    phi = t[::128] - b_a
    off = b_a[::128].copy()
    assert off.max() + cfg.WINW <= cfg.CPP

    aa = np.arange(A)
    gch = aa // 128
    al = aa % 128
    atab = np.zeros((8, cfg.WINR, A), dtype=np.float64)
    for side in (0, 1):
        bb = b_a + side
        shift = phi - side
        for l in range(4):
            for r in range(4):
                padidx = (bb - 1 + r) + cfg.PAD_LO
                rp = padidx - off[gch] - al
                assert rp.min() >= 0 and rp.max() < cfg.WINR
                coef = np.zeros(A)
                for k in range(l, 4):
                    coef += comb(k, l) * (shift ** (k - l)) * CRB[k, r]
                np.add.at(atab, (4 * side + l, rp, aa), coef)
    atab[4:8] -= atab[0:4]

    basis = np.zeros((4, 128), dtype=np.float64)
    p = np.arange(128, dtype=np.float64)
    for l in range(4):
        basis[l] = (p * cfg.dt) ** l

    atab_f = np.zeros((8, cfg.WINR, 128, cfg.NG, 128), dtype=np.float32)
    a3 = atab.reshape(8, cfg.WINR, cfg.NCH, 128)
    for g in range(cfg.NCH):
        atab_f[:, :, g % 128, g // 128, :] = a3[:, :, g, :]
    off_f = np.zeros((128, cfg.NG), dtype=np.int32)
    for g in range(cfg.NCH):
        off_f[g % 128, g // 128] = off[g]

    out = dict(off=off_f, atab=atab_f, mask=mask.astype(BF),
               basis=basis.astype(BF))
    _TABLE_CACHE[key] = out
    return out


def attn_bias_table(cfg, r):
    cols = []
    own = cfg.zig_chunks(r)
    for s in range(4):
        cols.append(-cfg.ESH)
        for j in range(cfg.PADS[s]):
            cols.append(-cfg.ESH if j < own[s] else -1e30)
    v = np.array(cols, dtype=np.float32)
    return np.ascontiguousarray(np.broadcast_to(v[None, :], (128, len(cols))))


def tri_masks(cfg):
    m = np.zeros((128, cfg.NK * cfg.G), dtype=np.float32)
    k = np.arange(128)[:, None]
    q = np.arange(cfg.G)[None, :]
    for kt in range(cfg.NK):
        m[:, kt * cfg.G:(kt + 1) * cfg.G] = (kt * 128 + k <= q)
    return m.astype(BF)


ENGINES = ("tensor", "vector", "scalar", "sync", "gpsimd")

class Prog:
    def __init__(self):
        self.ops = {e: [] for e in ENGINES}
        self.cnt = {}
        self.marks = {}

    def wait(self, eng, sem, val):
        if val > 0:
            self.ops[eng].append(("wait", sem, val))

    def waitm(self, eng, label):
        if label in self.marks:
            sem, val = self.marks[label]
            self.wait(eng, sem, val)

    def op(self, eng, fn, sem=None, by=1, mark=None):
        if sem is not None:
            self.cnt[sem] = self.cnt.get(sem, 0) + (1 if by == "dflt" else by)
        self.ops[eng].append(("op", fn, sem, by))
        if mark is not None:
            assert sem is not None
            self.marks[mark] = (sem, self.cnt[sem])

    def setmark(self, mark, sem):
        self.marks[mark] = (sem, self.cnt.get(sem, 0))

    def n(self, sem):
        return self.cnt.get(sem, 0)


def build_nc(cfg):
    nc = bass.Bass("TRN2", target_bir_lowering=False)
    TOWN, D, G, NK, H, DC, TT = cfg.TOWN, cfg.D, cfg.G, cfg.NK, cfg.H, cfg.DC, cfg.TT
    E3, EC, MC, INTER = cfg.E3, cfg.EC, cfg.MC, cfg.INTER
    KTG = cfg.KT_GRP
    NGR = NK // KTG
    SCL = float(1.0 / np.sqrt(128.0))
    pos = cfg.pos_row()
    TB, NTB, IC, NIC, HC, NHC = cfg.TB, cfg.NTB, cfg.IC, cfg.NIC, cfg.HC, cfg.NHC
    SG, SD, SPT, NSP = cfg.SG, cfg.SD, cfg.SPT, cfg.NSP
    NII = INTER // 128
    NTTB = TB // 128

    DT = nc.dram_tensor
    x_own = DT("x_own", [TOWN, D], F32, kind="ExternalInput").ap()
    w_qkv = DT("w_qkv", [E3, D], F32, kind="ExternalInput").ap()
    b_qkv = DT("b_qkv", [1, E3], F32, kind="ExternalInput").ap()
    w_out = DT("w_out", [D, D], F32, kind="ExternalInput").ap()
    b_out = DT("b_out", [1, D], F32, kind="ExternalInput").ap()
    cpp = [DT(f"cpp{i}", [1, cfg.CPP], F32, kind="ExternalInput").ap() for i in range(3)]
    t_off = DT("t_off", [128, cfg.NG], I32, kind="ExternalInput").ap()
    t_atab = DT("t_atab", [8, cfg.WINR, 128, cfg.NG, 128], F32, kind="ExternalInput").ap()
    t_mask = DT("t_mask", [128, cfg.A], BF16, kind="ExternalInput").ap()
    t_basis = DT("t_basis", [4, 128], BF16, kind="ExternalInput").ap()
    t_bias = DT("t_bias", [128, cfg.NPASS], F32, kind="ExternalInput").ap()
    t_tri = DT("t_tri", [128, NK * G], BF16, kind="ExternalInput").ap()
    t_ones = DT("t_ones", [128, 1], BF16, kind="ExternalInput").ap()
    t_ones4 = DT("t_ones4", [128, 1], BF16, kind="ExternalInput").ap()
    t_eps = DT("t_eps", [128, 1], F32, kind="ExternalInput").ap()
    t_onesr = DT("t_onesr", [1, 128], BF16, kind="ExternalInput").ap()
    y_own = DT("y_own", [TOWN, D], F32, kind="ExternalOutput").ap()

    x_bf = DT("x_bf", [TOWN, D], BF16).ap()
    wq_bf = DT("wq_bf", [E3, D], BF16).ap()
    wo_bf = DT("wo_bf", [D, D], BF16).ap()
    woT_dr = DT("woT_dr", [D, D], BF16).ap()
    q_dr = DT("q_dr", [TOWN, D], BF16).ap()
    k_own = DT("k_own", [TOWN, D], BF16).ap()
    v_own = DT("v_own", [TOWN, D], BF16).ap()
    k_all = DT("k_all", [4 * TOWN, D], BF16).ap()
    v_all = DT("v_all", [4 * TOWN, D], BF16).ap()
    h32_dr = DT("h32_dr", [TOWN, D], F32).ap()
    hbf_dr = DT("hbf_dr", [TOWN, D], BF16).ap()
    inter_dr = DT("inter_dr", [TOWN, INTER], BF16).ap()
    r_dr = [DT(f"r_dr{i}", [8, cfg.A], F32).ap() for i in range(3)]
    w_dr = [DT(f"w_dr{i}", [128, cfg.A], BF16).ap() for i in range(3)]

    import contextlib
    stack = contextlib.ExitStack()

    SBn = lambda name, shape, dt=BF16: stack.enter_context(nc.sbuf_tensor(name, shape, dt))
    ones_sb = SBn("ones_sb", [128, 1])
    ones4_sb = SBn("ones4_sb", [128, 1])
    onesr_sb = SBn("onesr_sb", [1, 128])
    basis_sb = SBn("basis_sb", [4, 128])
    bias_sb = SBn("bias_sb", [128, cfg.NPASS], F32)

    off_sb = SBn("off_sb", [128, cfg.NG], I32)
    mean_sb = SBn("mean_sb", [128, 1], F32)
    var_sb = SBn("var_sb", [128, 1], F32)
    rstd_sb = SBn("rstd_sb", [128, 1], F32)
    eps_sb = SBn("eps_sb", [128, 1], F32)
    rec_sb = SBn("rec_sb", [1, G])
    bq_sl = [SBn(f"bq_sl{i}", [1, 512]) for i in range(2)]
    ev_sb = [SBn(f"ev_sb{i}", [128, 512]) for i in range(2)]
    msk_sb = [SBn(f"msk_sb{i}", [128, 512]) for i in range(2)]

    wsel_sb = [SBn(f"wsel_sb{i}", [128, 512]) for i in range(2)]
    xr_sb = [SBn(f"xr_sb{i}", [128, 512], F32) for i in range(2)]

    h32t_sb = [SBn(f"h32t_sb{i}", [128, HC], F32) for i in range(2)]

    watermark = nc.sbuf_base
    W0 = (watermark + 31) // 32 * 32

    SZ = 229376
    assert nc.SBUF_PARTITION_SIZE_BYTES >= SZ - 1

    def manual(name, shape, dtype, off):
        t = nc.alloc_sbuf_tensor_at(name, shape, dtype, offset=off)
        itemsz = {BF16: 2, F32: 4, I32: 4}[dtype]
        end = off + int(np.prod(shape[1:])) * itemsz
        assert end <= nc.SBUF_PARTITION_SIZE_BYTES, (name, end)
        return t

    xT_bytes = DC * TOWN * 2
    attno_bytes = H * TOWN * 2
    hT_bytes = DC * TB * 2
    wg_bytes = IC * SG * 2
    iT_bytes = SPT * NII * 128 * 2
    R0 = W0
    R0_size = max(xT_bytes, attno_bytes, hT_bytes, iT_bytes)
    xT_sb = manual("xT_sb", [128, DC * TOWN], BF16, R0)
    attno_sb = manual("attno_sb", [128, H * TOWN], BF16, R0)
    hT_sb = manual("hT_sb", [128, DC * TB], BF16, R0)
    iT_sb = [manual("iT_sb", [128, SPT * NII * 128], BF16, R0)] * 2

    R1 = R0 + ((R0_size + 31) // 32 * 32)
    wq_bytes = DC * 512 * 2
    wo_bytes = DC * 512 * 2
    slab_bytes = HC * SD * 2
    attn_small = (2 * (TOWN + G + NK * 128) + 4 * KTG * G) * 2
    R1_size = max(max(2 * wq_bytes, 2 * 4 * TOWN * 2) + attn_small,
                  DC * D * 2, 2 * slab_bytes, 2 * wg_bytes)
    wq_col = [manual(f"wq_col{i}", [128, DC * 512], BF16, R1 + i * wq_bytes)
              for i in range(2)]
    wg_blk = [manual(f"wg_blk{i}", [128, IC * SG], BF16, R1 + i * wg_bytes)
              for i in range(2)]
    wu_blk = [manual(f"wu_blk{i}", [128, IC * SG], BF16, R1 + (2 + i) * wg_bytes)
              for i in range(2)]
    kf_sb = [manual(f"kf_sb{i}", [128, 4 * TOWN], BF16, R1 + i * 4 * TOWN * 2)
             for i in range(2)]
    _ao = R1 + max(2 * wq_bytes, 2 * 4 * TOWN * 2)
    q_sb = [manual(f"q_sb{i}", [128, TOWN], BF16, _ao + i * TOWN * 2) for i in range(2)]
    _ao += 2 * TOWN * 2
    k_sb = [manual(f"k_sb{i}", [128, G], BF16, _ao + i * G * 2) for i in range(2)]
    _ao += 2 * G * 2
    v_sb = [manual(f"v_sb{i}", [128, NK * 128], BF16, _ao + i * NK * 128 * 2)
            for i in range(2)]
    _ao += 2 * NK * 128 * 2
    p_sb = [manual(f"p_sb{i}", [128, KTG * G], BF16, _ao + i * KTG * G * 2)
            for i in range(4)]
    _ao += 4 * KTG * G * 2
    wo_full = manual("wo_full", [128, DC * D], BF16, R1)
    slab_sb = [manual(f"slab_sb{i}", [128, HC * SD], BF16, R1 + i * slab_bytes)
               for i in range(2)]

    R2 = R1 + ((R1_size + 31) // 32 * 32)
    cpw_bytes = (cfg.NG * cfg.WINW * 4 + 31) // 32 * 32
    at_bytes = cfg.NG * 128 * 4
    r2_bytes = SPT * D * 4
    scratch_end = 3 * cpw_bytes + 3 * at_bytes + 4 * at_bytes
    bo_off = max(scratch_end, 4 * D * 4)
    y_off = max(r2_bytes + D * 4, bo_off)
    R2_core = max(bo_off + D * 2, r2_bytes + D * 4, y_off + D * 4)
    R2_size = R2_core
    cpw_sb = [manual(f"cpw{i}", [128, cfg.NG * cfg.WINW], F32, R2 + i * cpw_bytes)
              for i in range(3)]
    at_sb = [manual(f"at_sb{i}", [128, cfg.NG * 128], F32,
                    R2 + 3 * cpw_bytes + i * at_bytes) for i in range(3)]
    racc3_sb = [manual(f"racc3_{i}", [128, cfg.NG * 128], F32,
                       R2 + 3 * cpw_bytes + (3 + i) * at_bytes) for i in range(3)]
    rtmp_sb = manual("rtmp_sb", [128, cfg.NG * 128], F32,
                     R2 + 3 * cpw_bytes + 6 * at_bytes)
    rldl_sb = [manual(f"rldl_sb{i}", [4, 1024], BF16,
                      R2 + 3 * cpw_bytes + i * 2048) for i in range(4)]
    _rb = R2 + 3 * cpw_bytes + 3 * at_bytes
    wout_sb = [manual(f"wout_sb{i}", [128, 512], BF16, _rb + i * 1024)
               for i in range(6)]
    wpi_sb = [manual(f"wpi_sb{i}", [128, 512], BF16, _rb + 6144 + i * 1024)
              for i in range(6)]
    bias_q_bc = manual("bias_q_bc", [128, E3], BF16,
                       R1 + max(2 * wq_bytes, 2 * 4 * TOWN * 2))
    bias_o_bc = manual("bias_o_bc", [128, D], BF16, R2 + bo_off)
    r1_sb = manual("r1_sb", [128, D], F32, R2)
    sums4_sb = manual("sums4_sb", [128, G], BF16, R2)
    av_sb = manual("av_sb", [128, G], F32, R2 + G * 4)
    tri_sb = manual("tri_sb", [128, NK * G], BF16, R2 + 2 * G * 4)
    _p5o = R2 + 2 * G * 4 + NK * G * 2
    sg_sb = manual("sg_sb", [128, IC], F32, _p5o)
    t1_sb = manual("t1_sb", [128, IC], F32, _p5o + IC * 4)
    iv_sb = [manual(f"iv_sb{i}", [128, IC], BF16, _p5o + 2 * IC * 4 + i * IC * 2)
             for i in range(2)]
    sq_sb = manual("sq_sb", [128, D], F32, R2 + D * 4)
    hf_sb = [manual(f"hf_sb{i}", [128, D], F32, R2 + 2 * D * 4 + i * D * 4)
             for i in range(2)]
    r2_sb = manual("r2_sb", [128, SPT * D], F32, R2)
    sq2_sb = manual("sq2_sb", [128, D], F32, R2 + r2_bytes)

    y_off = max(r2_bytes + D * 4, bo_off)
    assert y_off + D * 4 <= R2_size
    y_sb = [manual("y_sb", [128, D], F32, R2 + y_off)]
    assert R2 + R2_size <= nc.SBUF_PARTITION_SIZE_BYTES, (R2 + R2_size)

    PSn = lambda name, shape: stack.enter_context(nc.psum_tensor(name, shape, F32))
    S2 = [PSn(f"S2_{i}", [128, 1024]) for i in range(2)]
    P2 = [PSn(f"P2_{i}", [128, 512]) for i in range(2)]
    Q2 = [PSn(f"Q2_{i}", [128, 512]) for i in range(2)]

    P = Prog()

    def gq(dst, src, mark=None):
        P.op("gpsimd", lambda s, e, dst=dst, src=src: e.dma_start(out=dst, in_=src),
             sem="gq", by=16, mark=mark)
    def sq(dst, src, transpose=False, mark=None):
        P.op("sync", lambda s, e, dst=dst, src=src, tr=transpose:
             e.dma_start(out=dst, in_=src, transpose=tr), sem="sq", by=16, mark=mark)

    gq(off_sb[:, :], t_off[:, :], mark="off_ld")
    P.waitm("gpsimd", "off_ld")
    for i in range(3):
        for grp in range(cfg.NG):
            P.op("gpsimd", lambda s, e, i=i, grp=grp: e.indirect_dma_start(
                out=cpw_sb[i][:, grp * cfg.WINW:(grp + 1) * cfg.WINW],
                out_offset=None,
                in_=cpp[i][0:1, :],
                in_offset=IndirectOffsetOnAxis(off_sb[:, grp:grp + 1], 1)),
                sem="cpq", by=16)
    P.setmark("cpw_done", "cpq")
    for _c in range(4):
        if _c:
            P.wait("gpsimd", "xcq", 16 * _c)
        P.op("gpsimd", lambda s, e, _c=_c: e.dma_start(
            out=x_bf[_c * 512:(_c + 1) * 512, :],
            in_=x_own[_c * 512:(_c + 1) * 512, :]),
            sem="xcq", by=16, mark=("x_cast", _c))
    P.op("gpsimd", lambda s, e: e.dma_start(
        out=wq_bf[0:512, :], in_=w_qkv[0:512, :]),
        sem="wcq", by=16, mark=("wq_cast", 0))
    sq(ones_sb[:, :], t_ones[:, :])
    sq(ones4_sb[:, :], t_ones4[:, :])
    sq(eps_sb[:, :], t_eps[:, :])
    sq(onesr_sb[:, :], t_onesr[:, :])
    sq(basis_sb[:, :], t_basis[:, :])
    sq(bias_sb[:, :], t_bias[:, :], mark="p0_sq")

    AT_SEMS = ("rla", "rlb", "rlc")

    def rprep_use_count(k):
        return (k // cfg.WINR) * (6 * cfg.WINR - 3) + 3 + 6 * (k % cfg.WINR)

    def emit_at_row(row):
        for rp in range(cfg.WINR):
            k = row * cfg.WINR + rp
            if k >= 3:
                P.wait("sync", "vc", rprep_use_count(k - 3))
            P.op("sync", lambda s, e, row=row, rp=rp, k=k: e.dma_start(
                out=at_sb[k % 3][:, :], in_=t_atab[row, rp, :, :, :]),
                sem=AT_SEMS[k % 3], by=16, mark=("at_ld", k))

    XT_SEMS = ("xtq0", "xtq1", "xtq2", "xtq3")
    xt_jobs = [(dc, t0) for t0 in range(0, TOWN, 512) for dc in range(DC)]
    for ji, (dc, t0) in enumerate(xt_jobs):
        c = ji // DC
        if ji % DC == 0:
            P.waitm("sync", ("x_cast", c))
        P.op("sync", lambda s, e, dc=dc, t0=t0: e.dma_start(
            out=xT_sb[:, dc * TOWN + t0: dc * TOWN + t0 + 512],
            in_=x_bf[t0:t0 + 512, dc * 128:(dc + 1) * 128], transpose=True),
            sem=XT_SEMS[c], by=16,
            mark=("xt_chunk", c) if ji % DC == DC - 1 else None)
        if ji == DC - 1:
            P.waitm("sync", ("wq_cast", 0))
            for dcw in range(DC):
                P.op("sync", lambda s, en, dcw=dcw: en.dma_start(
                    out=wq_col[0][:, dcw * 512:(dcw + 1) * 512],
                    in_=wq_bf[0:512, dcw * 128:(dcw + 1) * 128],
                    transpose=True), sem="wqa", by=16)
            P.setmark(("wq_loaded", 0), "wqa")
        if (ji + 1) % 8 == 0 and (ji + 1) // 8 <= 8:
            emit_at_row((ji + 1) // 8 - 1)


    nq = E3 // 512
    P.waitm("tensor", "p0_sq")
    for i in range(nq + D // 512):
        src = b_qkv[:, i * 512:(i + 1) * 512] if i < nq else \
              b_out[:, (i - nq) * 512:(i - nq + 1) * 512]
        P.waitm("gpsimd", ("bc_mm", i - 2))
        P.op("gpsimd", lambda s, e, src=src, i=i: e.dma_start(
            out=bq_sl[i % 2][:, :], in_=src),
            sem=("bqa" if i % 2 == 0 else "bqb"), by=16, mark=("bq_sl", i))
        P.waitm("tensor", ("bq_sl", i))
        P.waitm("tensor", ("bc_copy", i - 1))
        P.op("tensor", lambda s, e, i=i: e.matmul(
            P2[0][:, :], onesr_sb[:, :], bq_sl[i % 2][:, :], start=True, stop=True),
            sem="tc", mark=("bc_mm", i))
        P.waitm("scalar", ("bc_mm", i))
        dst = bias_q_bc[:, i * 512:(i + 1) * 512] if i < nq else \
              bias_o_bc[:, (i - nq) * 512:(i - nq + 1) * 512]
        P.op("scalar", lambda s, e, dst=dst: e.copy(dst, P2[0][:, :]),
             sem="ac", mark=("bc_copy", i))
    P.setmark("bias_bc_done", "ac")

    for _e in range(1, EC):
        P.wait("gpsimd", "wcq", 16 * _e)
        P.op("gpsimd", lambda s, e, _e=_e: e.dma_start(
            out=wq_bf[_e * 512:(_e + 1) * 512, :],
            in_=w_qkv[_e * 512:(_e + 1) * 512, :]),
            sem="wcq", by=16, mark=("wq_cast", _e))
    P.op("gpsimd", lambda s, e: e.dma_start(out=wo_bf[:, :], in_=w_out[:, :]),
         sem="wcq", by=16, mark="wo_cast")

    for row in range(8):
        for rp in range(cfg.WINR):
            k = row * cfg.WINR + rp
            P.waitm("vector", ("at_ld", k))
            if k == 0:
                P.waitm("vector", "cpw_done")
            atv = at_sb[k % 3].rearrange("p (g w) -> p g w", w=128)[:, :, :]
            for cpi in range(3):
                cpws = cpw_sb[cpi].rearrange(
                    "p (g w) -> p g w", w=cfg.WINW)[:, :, rp:rp + 128]
                if rp == 0:
                    P.waitm("vector", ("rrow_st", (row - 1) * 3 + 2))
                    P.op("vector", lambda s, e, cpws=cpws, atv=atv, cpi=cpi:
                         e.tensor_tensor(
                             racc3_sb[cpi].rearrange(
                                 "p (g w) -> p g w", w=128)[:, :, :],
                             atv, cpws, ALU.mult), sem="vc",
                         mark=("rprep_use", k) if cpi == 2 else None)
                else:
                    P.op("vector", lambda s, e, cpws=cpws, atv=atv: e.tensor_tensor(
                        rtmp_sb.rearrange("p (g w) -> p g w", w=128)[:, :, :],
                        atv, cpws, ALU.mult), sem="vc")
                    P.op("vector", lambda s, e, cpi=cpi: e.tensor_tensor(
                        racc3_sb[cpi][:, :], racc3_sb[cpi][:, :], rtmp_sb[:, :],
                        ALU.add), sem="vc",
                        mark=("rprep_use", k) if cpi == 2 else None)
        P.setmark(("rrow_vec", row), "vc")
        PPC = min(cfg.NCH, 128)
        P.waitm("scalar", ("rrow_vec", row))
        for cpi in range(3):
            dst = r_dr[cpi][row:row + 1, :].rearrange("o (g pp w) -> (o pp) g w",
                                                      pp=PPC, w=128)
            P.op("scalar", lambda s, e, dst=dst, cpi=cpi: e.dma_start(
                out=dst,
                in_=racc3_sb[cpi][0:PPC, :].rearrange(
                    "p (g w) -> p g w", w=128)[:, :, :]),
                sem="rwq", by=16, mark=("rrow_st", row * 3 + cpi))
    P.setmark("rprep_all", "vc")
    P.setmark("rprep_out_all", "rwq")

    kq = 0
    for e in range(EC):
        nm = "wqa" if e % 2 == 0 else "wqb"
        if e > 0:
            P.waitm("sync", ("qkv_mm_blk", e - 2))
            P.waitm("sync", ("wq_cast", e))
            for dc in range(DC):
                P.op("sync", lambda s, en, e=e, dc=dc: en.dma_start(
                    out=wq_col[e % 2][:, dc * 512:(dc + 1) * 512],
                    in_=wq_bf[e * 512:(e + 1) * 512, dc * 128:(dc + 1) * 128],
                    transpose=True), sem=nm, by=16)
            P.setmark(("wq_loaded", e), nm)
        P.waitm("tensor", ("wq_loaded", e))
        for tt in range(TT):
            if e == 0:
                P.waitm("tensor", ("xt_chunk", tt // 4))
            pb = Q2[kq % 2]
            P.waitm("tensor", ("qkv_ev", kq - 2))
            for dc in range(DC):
                last = dc == DC - 1
                P.op("tensor", lambda s, en, e=e, tt=tt, dc=dc, pb=pb:
                     en.matmul(pb[:, :],
                               xT_sb[:, dc * TOWN + tt * 128: dc * TOWN + (tt + 1) * 128],
                               wq_col[e % 2][:, dc * 512:(dc + 1) * 512],
                               start=(dc == 0), stop=(dc == DC - 1)),
                     sem=("tc" if last else None),
                     mark=(("qkv_mm", kq) if last else None))
            P.waitm("vector", ("qkv_mm", kq))
            P.waitm("vector", "bias_bc_done")
            P.waitm("vector", ("qkv_out", kq - 2))
            P.op("vector", lambda s, en, e=e, kq=kq, pb=pb: en.tensor_tensor(
                ev_sb[kq % 2][:, :], pb[:, :],
                bias_q_bc[:, e * 512:(e + 1) * 512], ALU.add),
                sem="vc", mark=("qkv_ev", kq))
            P.waitm("gpsimd", ("qkv_ev", kq))
            if e < D // 512:
                dst = q_dr[tt * 128:(tt + 1) * 128, e * 512:(e + 1) * 512]
            else:
                ek = e - D // 512
                dst = (k_own if ek < D // 512 else v_own)[
                    tt * 128:(tt + 1) * 128,
                    (ek % (D // 512)) * 512:(ek % (D // 512) + 1) * 512]
            is_k = (D // 512) <= e < 2 * (D // 512)
            nmo = ("eka" if kq % 2 == 0 else "ekb") if is_k else \
                  ("eva" if kq % 2 == 0 else "evb")
            P.op("gpsimd", lambda s, en, dst=dst, kq=kq: en.dma_start(
                out=dst, in_=ev_sb[kq % 2][:, :]), sem=nmo, by=16,
                mark=("qkv_out", kq))
            kq += 1
        P.setmark(("qkv_mm_blk", e), "tc")
    P.setmark("qkv_done", "tc")
    P.setmark("qkv_ev_all", "vc")

    P.wait("gpsimd", "eka", P.n("eka"))
    P.wait("gpsimd", "ekb", P.n("ekb"))
    P.op("gpsimd", lambda s, e: e.collective_compute(
        "AllGather", ALU.bypass,
        replica_groups=[[0, 1, 2, 3], [4, 5, 6, 7]],
        ins=[k_own[:, :].opt()], outs=[k_all[:, :].opt()]),
        sem="cc", by="dflt", mark="kv_all")
    P.wait("gpsimd", "eva", P.n("eva"))
    P.wait("gpsimd", "evb", P.n("evb"))
    P.op("gpsimd", lambda s, e: e.collective_compute(
        "AllGather", ALU.bypass,
        replica_groups=[[0, 1, 2, 3], [4, 5, 6, 7]],
        ins=[v_own[:, :].opt()], outs=[v_all[:, :].opt()]),
        sem="cc", by="dflt", mark="kv_allv")

    NBLK = cfg.A // 512
    P.waitm("gpsimd", "rprep_out_all")
    PSLOT = [(S2[0][:, 0:512], S2[0][:, 512:1024]),
             (S2[1][:, 0:512], S2[1][:, 512:1024]),
             (P2[0][:, :], P2[1][:, :]),
             (Q2[0][:, :], Q2[1][:, :])]
    WS_SEMS = ("ws0", "ws1", "ws2", "ws3", "ws4", "ws5")
    NK_RC = NBLK * 3

    def rc_emit_loads(k):
        blk, cpi = divmod(k, 3)
        if cpi == 0:
            nmm = "mka" if blk % 2 == 0 else "mkb"
            P.waitm("sync", ("rc_mult", (blk - 2) * 3 + 2))
            P.op("sync", lambda s, e, blk=blk: e.dma_start(
                out=msk_sb[blk % 2][:, :],
                in_=t_mask[:, blk * 512:(blk + 1) * 512]),
                sem=nmm, by=16, mark=("mk_ld", blk))
        nm = ("rwa", "rwb", "rwc", "rwd")[k % 4]
        P.waitm("gpsimd", ("rc_mm", k - 4))
        P.op("gpsimd", lambda s, e, cpi=cpi, blk=blk, k=k: e.dma_start(
            out=rldl_sb[k % 4].rearrange("p (h c) -> p h c", h=2)[:, :, :],
            in_=r_dr[cpi][:, blk * 512:(blk + 1) * 512]
            .rearrange("(h p) c -> p h c", p=4)), sem=nm, by=16,
            mark=("rdl_ld", k))

    def rc_emit_mm(k):
        lo_ps, hi_ps = PSLOT[k % 4]
        P.waitm("tensor", ("rdl_ld", k))
        if k == 0:
            P.waitm("tensor", "p0_sq")
            P.waitm("tensor", "qkv_ev_all")
            P.waitm("tensor", "bias_bc_done")
        P.waitm("tensor", ("rc_add", k - 4))
        P.op("tensor", lambda s, e, lo_ps=lo_ps, k=k: e.matmul(
            lo_ps, basis_sb[:, :], rldl_sb[k % 4][:, 0:512],
            start=True, stop=True), sem="tc")
        P.op("tensor", lambda s, e, hi_ps=hi_ps, k=k: e.matmul(
            hi_ps, basis_sb[:, :], rldl_sb[k % 4][:, 512:1024],
            start=True, stop=True), sem="tc", mark=("rc_mm", k))

    def rc_emit_vec(k):
        blk = k // 3
        lo_ps, hi_ps = PSLOT[k % 4]
        P.waitm("vector", ("rc_mm", k))
        P.waitm("vector", ("mk_ld", blk))
        P.op("vector", lambda s, e, k=k, blk=blk, hi_ps=hi_ps: e.tensor_tensor(
            wsel_sb[k % 2][:, :], hi_ps, msk_sb[blk % 2][:, :],
            ALU.mult), sem="vc", mark=("rc_mult", k))
        P.waitm("vector", ("rc_cpy", k - 6))
        P.op("vector", lambda s, e, k=k, lo_ps=lo_ps: e.tensor_tensor(
            wout_sb[k % 6][:, :], wsel_sb[k % 2][:, :], lo_ps,
            ALU.add), sem="vc", mark=("rc_add", k))

    def rc_emit_copy(k):
        cpi = k % 3
        stride = SG if cpi < 2 else SD
        il = 512 // stride
        pview = wpi_sb[k % 6].rearrange("p (c il) -> p il c", il=il)[:, :, :]
        eng = "gpsimd" if (k % 5) in (1, 3) else "scalar"
        P.waitm(eng, ("rc_add", k))
        P.waitm(eng, ("w_out", k - 6))
        if eng == "scalar":
            P.op("scalar", lambda s, e, k=k, pview=pview: e.copy(
                pview, wout_sb[k % 6][:, :]), sem="ac", mark=("rc_cpy", k))
        else:
            P.op("gpsimd", lambda s, e, k=k, pview=pview: e.tensor_copy(
                pview, wout_sb[k % 6][:, :]), sem="pc", mark=("rc_cpy", k))
        blk, cpi = divmod(k, 3)
        P.waitm(eng, ("rc_cpy", k))
        P.op(eng, lambda s, e, cpi=cpi, blk=blk, k=k: e.dma_start(
            out=w_dr[cpi][:, blk * 512:(blk + 1) * 512],
            in_=wpi_sb[k % 6][:, :]), sem=WS_SEMS[k % 6], by=16,
            mark=("w_out", k))

    for _j in range(4):
        rc_emit_loads(_j)
    for k in range(NK_RC + 2):
        if k < NK_RC:
            rc_emit_mm(k)
            rc_emit_vec(k)
        if k + 4 < NK_RC:
            rc_emit_loads(k + 4)
        if k >= 2:
            rc_emit_copy(k - 2)
    P.setmark("recon_sel_all", "vc")
    P.setmark("recon_mm_all", "tc")
    WS_TOT = [P.n(nm) for nm in WS_SEMS]

    def pass_list(sl):
        out = [("diag", (k_own, v_own), sl * G, sum(cfg.PADS[:sl]) + sl)]
        for j in range(cfg.PADS[sl]):
            out.append(("rect", (k_all, v_all), pos[j],
                        sum(cfg.PADS[:sl]) + sl + 1 + j))
        return out

    P.wait("sync", "eva", P.n("eva"))
    P.wait("sync", "evb", P.n("evb"))
    P.wait("sync", "eka", P.n("eka"))
    P.wait("sync", "ekb", P.n("ekb"))
    P.waitm("sync", "wo_cast")
    for j in range(MC * DC):
        mc, dc = divmod(j, DC)
        npar = "wta" if j % 2 == 0 else "wtb"
        P.waitm("sync", ("wo_st", j - 2))
        P.op("sync", lambda s, e, mc=mc, dc=dc, j=j: e.dma_start(
            out=ev_sb[j % 2][:, :],
            in_=wo_bf[mc * 512:(mc + 1) * 512, dc * 128:(dc + 1) * 128],
            transpose=True), sem=npar, by=16, mark=("wo_tld", j))
        P.waitm("gpsimd", ("wo_tld", j))
        P.op("gpsimd", lambda s, e, mc=mc, dc=dc, j=j: e.dma_start(
            out=woT_dr[dc * 128:(dc + 1) * 128, mc * 512:(mc + 1) * 512],
            in_=ev_sb[j % 2][:, :]), sem=("wqoa" if j % 2 == 0 else "wqob"),
            by=16, mark=("wo_st", j))

    P.waitm("sync", "rprep_all")
    sq(tri_sb[:, :], t_tri[:, :], mark="tri_ld")
    P.waitm("vector", "tri_ld")
    gidx = 0
    kidx = 0
    for h in range(H):
        nmq = "qla" if h % 2 == 0 else "qlb"
        nmk = "kfa" if h % 2 == 0 else "kfb"
        P.waitm("sync", ("attn_h_smm", h - 2))
        for t0 in range(0, TOWN, 512):
            P.op("sync", lambda s, e, h=h, t0=t0: e.dma_start(
                out=q_sb[h % 2][:, t0:t0 + 512],
                in_=q_dr[t0:t0 + 512, h * 128:(h + 1) * 128], transpose=True),
                sem=nmq, by=16)
        P.setmark(("q_ld", h), nmq)
        if h == 0:
            P.waitm("sync", "qkv_done")
            P.waitm("sync", "kv_all")
        for t0 in range(0, 4 * TOWN, 2048):
            P.op("sync", lambda s, e, h=h, t0=t0: e.dma_start(
                out=kf_sb[h % 2][:, t0:t0 + 2048],
                in_=k_all[t0:t0 + 2048, h * 128:(h + 1) * 128], transpose=True),
                sem=nmk, by=16)
        P.setmark(("kf_ld", h), nmk)
        P.waitm("tensor", ("q_ld", h))
        P.waitm("tensor", ("kf_ld", h))
        for sl in range(4):
            pend_av = [None]
            pl = pass_list(sl)
            npass = len(pl)
            for pi, (kind, src, row0, bidx) in enumerate(pl):
                nm = "kta" if kidx % 2 == 0 else "ktb"
                nmv = "vta" if kidx % 2 == 0 else "vtb"
                P.waitm("sync", ("kv_use", kidx - 2))
                if kind == "rect" and h == 0 and sl == 0 and pi == 1:
                    P.waitm("sync", "kv_allv")
                if kind == "diag":
                    P.op("sync", lambda s, e, src=src, row0=row0, h=h, kidx=kidx:
                         e.dma_start(out=k_sb[kidx % 2][:, :],
                                     in_=src[0][row0:row0 + G, h * 128:(h + 1) * 128],
                                     transpose=True), sem=nm, by=16, mark=("k_ld", kidx))
                P.op("sync", lambda s, e, src=src, row0=row0, h=h, kidx=kidx:
                     e.dma_start(
                         out=v_sb[kidx % 2].rearrange("p (kt d) -> p kt d", d=128)[:, :, :],
                         in_=src[1][row0:row0 + G, h * 128:(h + 1) * 128]
                         .rearrange("(kt p) d -> p kt d", p=128)),
                     sem=nmv, by=16, mark=("v_ld", kidx))
                if kind == "diag":
                    P.waitm("tensor", ("k_ld", kidx))
                for grp in range(NGR):
                    gpar = gidx % 2
                    ppar = gidx % 4
                    if gidx == 0:
                        P.waitm("tensor", "recon_sel_all")
                    P.waitm("tensor", ("exp", gidx - 2))
                    for ktl in range(KTG):
                        kt = grp * KTG + ktl
                        last = ktl == KTG - 1
                        if kind == "diag":
                            klhs = k_sb[kidx % 2][:, kt * 128:(kt + 1) * 128]
                        else:
                            klhs = kf_sb[h % 2][:, row0 + kt * 128: row0 + (kt + 1) * 128]
                        P.op("tensor", lambda s, e, klhs=klhs, gpar=gpar, h=h, sl=sl,
                             ktl=ktl:
                             e.matmul(S2[gpar][:, ktl * G:(ktl + 1) * G],
                                      klhs,
                                      q_sb[h % 2][:, sl * G:(sl + 1) * G],
                                      start=True, stop=True),
                             sem=("tc" if last else None),
                             mark=(("s_mm", gidx) if last else None))
                    P.waitm("scalar", ("s_mm", gidx))
                    P.waitm("scalar", ("av_grp", gidx - 4))
                    P.op("scalar", lambda s, e, gpar=gpar, ppar=ppar, bidx=bidx:
                         e.activation(
                             p_sb[ppar][:, :], S2[gpar][:, 0:KTG * G], AF.Exp,
                             bias=bias_sb[:, bidx:bidx + 1], scale=SCL),
                         sem="ac", mark=("exp", gidx))
                    if kind == "diag":
                        P.waitm("vector", ("exp", gidx))
                        P.op("vector", lambda s, e, ppar=ppar, grp=grp: e.tensor_tensor(
                            p_sb[ppar][:, :], p_sb[ppar][:, :],
                            tri_sb[:, grp * KTG * G:(grp + 1) * KTG * G], ALU.mult),
                            sem="vc", mark=("tri", gidx))
                    gidx += 1
                def emit_av(kind=kind, kidx=kidx, g_base=gidx - NGR, pi=pi,
                           npass=npass, sl=sl, h=h):
                    P.waitm("tensor", ("v_ld", kidx))
                    first = pi == 0
                    last_pass = pi == npass - 1
                    if first:
                        P.waitm("tensor", ("attno_mul", h * 4 + sl - 2))
                        P.waitm("tensor", ("recip", h * 4 + sl - 1))
                        if h == 0 and sl == 0:
                            P.waitm("tensor", "bias_bc_done")
                            P.waitm("tensor", "qkv_ev_all")
                    for grp_i in range(NGR):
                        g2 = g_base + grp_i
                        ppar = g2 % 4
                        P.waitm("tensor", ("exp", g2))
                        if kind == "diag":
                            P.waitm("tensor", ("tri", g2))
                        for ktl in range(KTG):
                            kt = grp_i * KTG + ktl
                            av_last = last_pass and kt == NK - 1
                            grp_last = ktl == KTG - 1
                            P.op("tensor", lambda s, e, sl=sl, kidx=kidx, ppar=ppar,
                                 kt=kt, ktl=ktl, first=first, av_last=av_last:
                                 e.matmul(P2[sl % 2][:, 0:G],
                                          v_sb[kidx % 2][:, kt * 128:(kt + 1) * 128],
                                          p_sb[ppar][:, ktl * G:(ktl + 1) * G],
                                          start=(first and kt == 0), stop=av_last))
                            mk2 = ("avs_mm", h * 4 + sl) if av_last else \
                                  (("av_grp", g2) if grp_last else None)
                            srow = 32 * (kt % 4)
                            P.op("tensor", lambda s, e, ppar=ppar, kt=kt, ktl=ktl,
                                 first=first, last_pass=last_pass, srow=srow:
                                 e.matmul(Q2[0][srow:srow + 1, 0:G], ones_sb[:, :],
                                          p_sb[ppar][:, ktl * G:(ktl + 1) * G],
                                          start=first, stop=last_pass,
                                          tile_position=(0, srow)),
                                 sem=("tc" if (av_last or grp_last) else None),
                                 mark=mk2)
                        if last_pass and grp_i == NGR - 1:
                            P.setmark(("av_grp", g2), "tc")
                    P.setmark(("kv_use", kidx), "tc")
                if pend_av[0] is not None:
                    pend_av[0]()
                pend_av[0] = emit_av
                kidx += 1
            if pend_av[0] is not None:
                pend_av[0]()
                pend_av[0] = None
            P.waitm("scalar", ("avs_mm", h * 4 + sl))
            P.op("scalar", lambda s, e: e.copy(sums4_sb[:, :], Q2[0][:, 0:G]),
                 sem="ac", mark=("sums_cp", h * 4 + sl))
            P.waitm("tensor", ("sums_cp", h * 4 + sl))
            P.op("tensor", lambda s, e: e.matmul(
                Q2[0][0:1, 0:G], ones4_sb[:, :], sums4_sb[:, :],
                start=True, stop=True), sem="tc", mark=("sums_red", h * 4 + sl))
            P.waitm("vector", ("sums_red", h * 4 + sl))
            def _recip(s, e):
                with nc.allow_low_precision(reason="recip feeds bf16 matmul"):
                    return e.reciprocal(rec_sb[:, :], Q2[0][0:1, 0:G])
            P.op("vector", _recip, sem="vc", mark=("recip", h * 4 + sl))
            P.waitm("tensor", ("recip", h * 4 + sl))
            P.waitm("tensor", ("attno_mul", h * 4 + sl - 1))
            P.op("tensor", lambda s, e: e.matmul(
                Q2[1][:, 0:G], onesr_sb[:, :], rec_sb[:, :], start=True, stop=True),
                sem="tc", mark=("rbc", h * 4 + sl))
            P.waitm("scalar", ("avs_mm", h * 4 + sl))
            P.waitm("scalar", ("attno_mul", h * 4 + sl - 1))
            P.op("scalar", lambda s, e, sl=sl: e.copy(av_sb[:, :], P2[sl % 2][:, 0:G]),
                 sem="ac", mark=("av_cp", h * 4 + sl))
            P.waitm("vector", ("av_cp", h * 4 + sl))
            P.waitm("vector", ("rbc", h * 4 + sl))
            P.op("vector", lambda s, e, h=h, sl=sl: e.tensor_tensor(
                attno_sb[:, h * TOWN + sl * G: h * TOWN + (sl + 1) * G],
                av_sb[:, :], Q2[1][:, 0:G], ALU.mult),
                sem="vc", mark=("attno_mul", h * 4 + sl))
        P.setmark(("attn_h_smm", h), "tc")
    P.setmark("attn_done", "vc")
    P.setmark("attn_ac_all", "ac")
    P.setmark("attn_tc_all", "tc")

    P.waitm("sync", "rprep_all")
    P.waitm("sync", "qkv_done")
    P.waitm("tensor", "attn_done")
    P.waitm("sync", "attn_tc_all")
    P.wait("sync", "wqoa", P.n("wqoa"))
    P.wait("sync", "wqob", P.n("wqob"))
    wo_fv = wo_full.rearrange("p (dc mcm) -> p dc mcm", mcm=MC * 512)
    for mc in range(MC):
        P.op("sync", lambda s, e, mc=mc: e.dma_start(
            out=wo_fv[:, :, mc * 512:(mc + 1) * 512],
            in_=woT_dr[:, mc * 512:(mc + 1) * 512]
            .rearrange("(dc p) m -> p dc m", p=128)), sem="wfq", by=16,
            mark="wo_full_ld" if mc == MC - 1 else None)
    for tt in range(TT):
        for mc in range(MC):
            k4 = tt * MC + mc
            pb = Q2[k4 % 2]
            if k4 == 0:
                P.waitm("tensor", "wo_full_ld")
            P.waitm("tensor", ("p4_ev", k4 - 2))
            for dc in range(DC):
                last = dc == DC - 1
                P.op("tensor", lambda s, e, tt=tt, mc=mc, dc=dc, pb=pb, k4=k4:
                     e.matmul(pb[:, :],
                              attno_sb[:, dc * TOWN + tt * 128: dc * TOWN + (tt + 1) * 128],
                              wo_full[:, dc * MC * 512 + mc * 512:
                                      dc * MC * 512 + (mc + 1) * 512],
                              start=(dc == 0), stop=(dc == DC - 1)),
                     sem=("tc" if last else None),
                     mark=(("p4_mm", k4) if last else None))
            nx = "xca" if k4 % 2 == 0 else "xcb"
            P.waitm("gpsimd", ("p4_ev", k4 - 2))
            P.op("gpsimd", lambda s, e, tt=tt, mc=mc, k4=k4: e.dma_start(
                out=xr_sb[k4 % 2][:, :],
                in_=x_own[tt * 128:(tt + 1) * 128, mc * 512:(mc + 1) * 512]),
                sem=nx, by=16, mark=("x_ld", k4))
            P.waitm("vector", ("p4_mm", k4))
            P.waitm("vector", ("x_ld", k4))
            P.op("vector", lambda s, e, mc=mc, pb=pb: e.tensor_tensor(
                r1_sb[:, mc * 512:(mc + 1) * 512], pb[:, :],
                bias_o_bc[:, mc * 512:(mc + 1) * 512], ALU.add), sem="vc")
            P.op("vector", lambda s, e, mc=mc, k4=k4: e.tensor_tensor(
                r1_sb[:, mc * 512:(mc + 1) * 512],
                r1_sb[:, mc * 512:(mc + 1) * 512], xr_sb[k4 % 2][:, :], ALU.add),
                sem="vc", mark=("p4_ev", k4))
        P.op("vector", lambda s, e: e.reduce_sum(mean_sb[:, :], r1_sb[:, :], axis=AXX),
             sem="vc")
        P.op("vector", lambda s, e: e.tensor_scalar(
            mean_sb[:, :], mean_sb[:, :], 1.0 / D, None, ALU.mult), sem="vc")
        P.op("vector", lambda s, e: e.tensor_scalar(
            r1_sb[:, :], r1_sb[:, :], mean_sb[:, 0:1], None, ALU.subtract), sem="vc")
        P.op("vector", lambda s, e: e.tensor_tensor(
            sq_sb[:, :], r1_sb[:, :], r1_sb[:, :], ALU.mult), sem="vc")
        P.op("vector", lambda s, e: e.reduce_sum(var_sb[:, :], sq_sb[:, :], axis=AXX),
             sem="vc", mark=("ln1_var", tt))
        P.waitm("scalar", ("ln1_var", tt))
        P.op("scalar", lambda s, e: e.activation(
            rstd_sb[:, :], var_sb[:, :], AF.Ln, bias=eps_sb[:, 0:1], scale=1.0 / D), sem="ac")
        P.op("scalar", lambda s, e: e.activation(
            rstd_sb[:, :], rstd_sb[:, :], AF.Exp, scale=-0.5),
            sem="ac", mark=("ln1_rstd", tt))
        P.waitm("vector", ("ln1_rstd", tt))
        P.waitm("vector", ("h_out", tt - 2))
        P.op("vector", lambda s, e, tt=tt: e.tensor_scalar(
            hf_sb[tt % 2][:, :], r1_sb[:, :], rstd_sb[:, 0:1], None, ALU.mult),
            sem="vc", mark=("ln1_h", tt))
        P.waitm("gpsimd", ("ln1_h", tt))
        nhq = "hqa" if tt % 2 == 0 else "hqb"
        P.op("gpsimd", lambda s, e, tt=tt: e.dma_start(
            out=h32_dr[tt * 128:(tt + 1) * 128, :], in_=hf_sb[tt % 2][:, :]),
            sem=nhq, by=16)
        P.op("gpsimd", lambda s, e, tt=tt: e.dma_start(
            out=hbf_dr[tt * 128:(tt + 1) * 128, :], in_=hf_sb[tt % 2][:, :]),
            sem=nhq, by=16, mark=("h_out", tt))
    P.setmark("p4_mm_all", "tc")
    P.setmark("p4_vec_all", "vc")

    P.wait("sync", "hqa", P.n("hqa"))
    P.wait("sync", "hqb", P.n("hqb"))
    P.waitm("sync", "p4_vec_all")
    kg = 0
    for tb in range(NTB):
        t_base = tb * TB
        P.waitm("sync", ("gu_blk", tb - 1))
        for dc in range(DC):
            for t0 in range(0, TB, 512):
                P.op("sync", lambda s, e, dc=dc, t0=t0, t_base=t_base: e.dma_start(
                    out=hT_sb[:, dc * TB + t0: dc * TB + t0 + 512],
                    in_=hbf_dr[t_base + t0: t_base + t0 + 512, dc * 128:(dc + 1) * 128],
                    transpose=True), sem="htq", by=16)
        P.setmark(("hT_ld", tb), "htq")
        P.waitm("tensor", ("hT_ld", tb))
        for ic in range(NIC):
            nm = "wga" if ic % 2 == 0 else "wgb"
            P.waitm("gpsimd", ("gu_mm", (tb * NIC + ic - 2 + 1) * NTTB - 1))
            if tb == 0 and ic == 0:
                for _nm, _v in zip(WS_SEMS, WS_TOT):
                    P.wait("gpsimd", _nm, _v)
                P.waitm("gpsimd", "p4_mm_all")
            P.op("gpsimd", lambda s, e, ic=ic: e.dma_start(
                out=wg_blk[ic % 2][:, :],
                in_=w_dr[0][:, ic * IC * SG:(ic + 1) * IC * SG]), sem=nm, by=16)
            P.op("gpsimd", lambda s, e, ic=ic: e.dma_start(
                out=wu_blk[ic % 2][:, :],
                in_=w_dr[1][:, ic * IC * SG:(ic + 1) * IC * SG]), sem=nm, by=16,
                mark=("wg_ld", tb * NIC + ic))
            P.waitm("tensor", ("wg_ld", tb * NIC + ic))
            for tt in range(NTTB):
                gpar = kg % 2
                gq4 = kg % 4
                gps = (S2[0][:, 0:512], S2[1][:, 0:512], P2[0][:, :], Q2[0][:, :])[gq4]
                ups = (S2[0][:, 512:1024], S2[1][:, 512:1024], P2[1][:, :],
                       Q2[1][:, :])[gq4]
                if kg == 0:
                    P.waitm("tensor", "attn_ac_all")
                    P.waitm("tensor", "attn_done")
                    P.waitm("tensor", ("p4_ev", TT * MC - 1))
                P.waitm("tensor", ("gu_ev", kg - 4))
                BIL = 512 // SG
                wgv = wg_blk[ic % 2].rearrange("p (blk x) -> p blk x", x=512)
                wuv = wu_blk[ic % 2].rearrange("p (blk x) -> p blk x", x=512)
                for dc in range(DC):
                    last = dc == DC - 1
                    P.op("tensor", lambda s, e, dc=dc, tt=tt, gps=gps, wgv=wgv, BIL=BIL:
                         e.matmul(gps,
                                  hT_sb[:, dc * TB + tt * 128: dc * TB + (tt + 1) * 128],
                                  wgv[:, :, dc * BIL:(dc + 1) * BIL],
                                  start=(dc == 0), stop=(dc == DC - 1)))
                    P.op("tensor", lambda s, e, dc=dc, tt=tt, ups=ups, wuv=wuv, BIL=BIL:
                         e.matmul(ups,
                                  hT_sb[:, dc * TB + tt * 128: dc * TB + (tt + 1) * 128],
                                  wuv[:, :, dc * BIL:(dc + 1) * BIL],
                                  start=(dc == 0), stop=(dc == DC - 1)),
                         sem=("tc" if last else None),
                         mark=(("gu_mm", kg) if last else None))
                P.waitm("scalar", ("gu_mm", kg))
                P.waitm("scalar", ("gu_t1", kg - 1))
                P.op("scalar", lambda s, e, gps=gps: e.activation(
                    sg_sb[:, :], gps, AF.Sigmoid),
                    sem="ac", mark=("gu_sig", kg))
                P.waitm("vector", ("gu_sig", kg))
                P.op("vector", lambda s, e, gps=gps: e.tensor_tensor(
                    t1_sb[:, :], gps, sg_sb[:, :], ALU.mult),
                    sem="vc", mark=("gu_t1", kg))
                P.waitm("vector", ("iv_out", kg - 2))
                P.op("vector", lambda s, e, ups=ups, kg=kg: e.tensor_tensor(
                    iv_sb[kg % 2][:, :], t1_sb[:, :], ups,
                    ALU.mult), sem="vc", mark=("gu_ev", kg))
                P.waitm("gpsimd", ("gu_ev", kg))
                nmi = "iva" if kg % 2 == 0 else "ivb"
                P.op("gpsimd", lambda s, e, ic=ic, tt=tt, t_base=t_base, kg=kg:
                     e.dma_start(out=inter_dr[t_base + tt * 128: t_base + (tt + 1) * 128,
                                              ic * IC:(ic + 1) * IC],
                                 in_=iv_sb[kg % 2][:, :]), sem=nmi, by=16,
                     mark=("iv_out", kg))
                kg += 1
        P.setmark(("gu_blk", tb), "tc")
    P.setmark("gu_all_tc", "tc")
    P.setmark("gu_all_ac", "ac")
    P.wait("gpsimd", "iva", P.n("iva"))
    P.wait("gpsimd", "ivb", P.n("ivb"))
    P.setmark("inter_all", "gq")

    IVA_ALL, IVB_ALL = P.n("iva"), P.n("ivb")
    kd = 0
    for sp in range(NSP):
        t0 = sp * SPT * 128
        P.waitm("sync", "p4_mm_all")
        P.wait("sync", "iva", IVA_ALL)
        P.wait("sync", "ivb", IVB_ALL)
        P.waitm("sync", ("dn_sp", sp - 1))
        nit = "itqa" if sp % 2 == 0 else "itqb"
        for ii in range(NII):
            P.op("sync", lambda s, e, ii=ii, t0=t0, sp=sp: e.dma_start(
                out=iT_sb[sp % 2][:, ii * SPT * 128:(ii + 1) * SPT * 128],
                in_=inter_dr[t0: t0 + SPT * 128, ii * 128:(ii + 1) * 128],
                transpose=True), sem=nit, by=16)
        P.setmark(("iT_ld", sp), nit)
        P.waitm("tensor", ("iT_ld", sp))
        for hc in range(NHC):
            nms = "sba" if hc % 2 == 0 else "sbb"
            P.waitm("gpsimd", ("dn_mm", kd - SPT - 1))
            if sp == 0 and hc == 0:
                P.waitm("gpsimd", "gu_all_tc")
            P.op("gpsimd", lambda s, e, hc=hc: e.dma_start(
                out=slab_sb[hc % 2][:, :],
                in_=w_dr[2][:, hc * HC * SD:(hc + 1) * HC * SD]), sem=nms, by=16,
                mark=("slab_ld", sp * NHC + hc))
            P.waitm("tensor", ("slab_ld", sp * NHC + hc))
            BHL = 512 // SD
            slv = slab_sb[hc % 2].rearrange("p (blk x) -> p blk x", x=512)
            for ttl in range(SPT):
                pdp = (P2[0], P2[1], Q2[0], Q2[1])[kd % 4]
                if kd == 0:
                    P.waitm("tensor", ("gu_ev", NTB * NIC * NTTB - 1))
                P.waitm("tensor", ("dn_ev", kd - 4))
                for ii in range(NII):
                    last = ii == NII - 1
                    P.op("tensor", lambda s, e, ii=ii, ttl=ttl, pdp=pdp, slv=slv, BHL=BHL,
                         sp=sp:
                         e.matmul(pdp[:, 0:HC],
                                  iT_sb[sp % 2][:, (ii * SPT + ttl) * 128:(ii * SPT + ttl + 1) * 128],
                                  slv[:, :, ii * BHL:(ii + 1) * BHL],
                                  start=(ii == 0), stop=(ii == NII - 1)),
                         sem=("tc" if last else None),
                         mark=(("dn_mm", kd) if last else None))
                nh3 = "h3a" if kd % 2 == 0 else "h3b"
                P.waitm("sync", ("dn_ev", kd - 2))
                P.op("sync", lambda s, e, ttl=ttl, hc=hc, t0=t0, kd=kd: e.dma_start(
                    out=h32t_sb[kd % 2][:, :],
                    in_=h32_dr[t0 + ttl * 128: t0 + (ttl + 1) * 128,
                               hc * HC:(hc + 1) * HC]), sem=nh3, by=16,
                    mark=("h3_ld", kd))
                P.waitm("vector", ("dn_mm", kd))
                P.waitm("vector", ("h3_ld", kd))
                P.op("vector", lambda s, e, ttl=ttl, hc=hc, pdp=pdp, kd=kd:
                     e.tensor_tensor(
                         r2_sb[:, ttl * D + hc * HC: ttl * D + (hc + 1) * HC],
                         pdp[:, 0:HC], h32t_sb[kd % 2][:, :], ALU.add),
                     sem="vc", mark=("dn_ev", kd))
                kd += 1
        for ttl in range(SPT):
            r2v = r2_sb[:, ttl * D:(ttl + 1) * D]
            P.op("vector", lambda s, e, r2v=r2v: e.reduce_sum(
                mean_sb[:, :], r2v, axis=AXX), sem="vc")
            P.op("vector", lambda s, e: e.tensor_scalar(
                mean_sb[:, :], mean_sb[:, :], 1.0 / D, None, ALU.mult), sem="vc")
            P.op("vector", lambda s, e, r2v=r2v: e.tensor_scalar(
                r2v, r2v, mean_sb[:, 0:1], None, ALU.subtract), sem="vc")
            P.op("vector", lambda s, e, r2v=r2v: e.tensor_tensor(
                sq2_sb[:, :], r2v, r2v, ALU.mult), sem="vc")
            P.op("vector", lambda s, e: e.reduce_sum(
                var_sb[:, :], sq2_sb[:, :], axis=AXX), sem="vc",
                mark=("ln2_var", sp * SPT + ttl))
            P.waitm("scalar", ("ln2_var", sp * SPT + ttl))
            P.op("scalar", lambda s, e: e.activation(
                rstd_sb[:, :], var_sb[:, :], AF.Ln, bias=eps_sb[:, 0:1], scale=1.0 / D), sem="ac")
            P.op("scalar", lambda s, e: e.activation(
                rstd_sb[:, :], rstd_sb[:, :], AF.Exp, scale=-0.5),
                sem="ac", mark=("ln2_rstd", sp * SPT + ttl))
            gy = sp * SPT + ttl
            P.waitm("vector", ("ln2_rstd", gy))
            P.waitm("vector", ("y_out", gy - 1))
            P.op("vector", lambda s, e, r2v=r2v, gy=gy: e.tensor_scalar(
                y_sb[0][:, :], r2v, rstd_sb[:, 0:1], None, ALU.mult),
                sem="vc", mark=("ln2_y", gy))
            P.waitm("gpsimd", ("ln2_y", gy))
            nmy = "yqa" if gy % 2 == 0 else "yqb"
            P.op("gpsimd", lambda s, e, gy=gy, ttl=ttl, t0=t0: e.dma_start(
                out=y_own[t0 + ttl * 128: t0 + (ttl + 1) * 128, :],
                in_=y_sb[0][:, :]), sem=nmy, by=16, mark=("y_out", gy))
        P.setmark(("dn_sp", sp), "tc")
    P.wait("gpsimd", "yqa", P.n("yqa"))
    P.wait("gpsimd", "yqb", P.n("yqb"))

    semnames = set(P.cnt.keys())
    for e in ENGINES:
        for item in P.ops[e]:
            if item[0] == "wait":
                semnames.add(item[1])
    sems = {name: stack.enter_context(nc.semaphore(name)) for name in sorted(semnames)}

    with nc.Block() as block:
        selfsem = {"vector": "vc", "scalar": "ac"}
        def mk(eng_name):
            def go(eng):
                own = selfsem.get(eng_name)
                nown = 0
                for item in P.ops[eng_name]:
                    if item[0] == "wait":
                        _, s, v = item
                        eng.wait_ge(sems[s], v)
                    else:
                        _, fn, s, by = item
                        if own is not None and nown > 0:
                            eng.wait_ge(sems[own], nown)
                        inst = fn(sems, eng)
                        if s is not None:
                            if by == "dflt":
                                inst.then_inc(sems[s])
                            else:
                                inst.then_inc(sems[s], by)
                                if s == own:
                                    nown += by
            return go
        block.tensor(mk("tensor"))
        block.vector(mk("vector"))
        block.scalar(mk("scalar"))
        block.sync(mk("sync"))
        block.gpsimd(mk("gpsimd"))

    stack.close()
    return nc


def shard_inputs(cfg, inputs):
    x = np.asarray(inputs["L0_attn_x"])
    wq = np.ascontiguousarray(np.asarray(inputs["L0_attn_w_qkv"], dtype=np.float32))
    bq = np.ascontiguousarray(np.asarray(inputs["L0_attn_b_qkv"], dtype=np.float32)[None, :])
    wo = np.ascontiguousarray(np.asarray(inputs["L0_attn_w_out"], dtype=np.float32))
    bo = np.ascontiguousarray(np.asarray(inputs["L0_attn_b_out"], dtype=np.float32)[None, :])
    cps = [np.asarray(inputs[k], dtype=np.float32)
           for k in ("L0_ffn_gate_cp", "L0_ffn_up_cp", "L0_ffn_down_cp")]
    tabs = recon_tables(cfg)
    tri = tri_masks(cfg)
    ones = np.ones((128, 1), dtype=BF)
    ones4 = np.zeros((128, 1), dtype=BF)
    for _r in range(min(cfg.NK, 4)):
        ones4[32 * _r, 0] = 1.0
    onesr = np.ones((1, 128), dtype=BF)
    cpp = [np.ascontiguousarray(np.concatenate(
        [np.full(cfg.PAD_LO, c[0]), c, np.full(cfg.PAD_HI, c[-1])]
        ).astype(np.float32)[None, :]) for c in cps]

    in_maps = []
    for core in range(8):
        b, r = divmod(core, 4)
        chunks = cfg.zig_chunks(r)
        xs = np.concatenate([x[b, c * cfg.G:(c + 1) * cfg.G, :] for c in chunks], axis=0)
        in_maps.append(dict(
            x_own=np.ascontiguousarray(xs, dtype=np.float32),
            w_qkv=wq, b_qkv=bq, w_out=wo, b_out=bo,
            cpp0=cpp[0], cpp1=cpp[1], cpp2=cpp[2],
            t_off=tabs["off"], t_atab=tabs["atab"], t_mask=tabs["mask"],
            t_basis=tabs["basis"], t_bias=attn_bias_table(cfg, r), t_tri=tri,
            t_ones=ones, t_onesr=onesr, t_ones4=ones4,
            t_eps=np.full((128, 1), 1e-5, dtype=np.float32),
        ))
    return in_maps


def assemble_output(cfg, results):
    y = np.zeros((cfg.B, cfg.S, cfg.D), dtype=np.float32)
    for core in range(8):
        b, r = divmod(core, 4)
        yo = results[core]["y_own"]
        for s, c in enumerate(cfg.zig_chunks(r)):
            y[b, c * cfg.G:(c + 1) * cfg.G, :] = yo[s * cfg.G:(s + 1) * cfg.G, :]
    return y


def kernel(L0_attn_x, L0_attn_w_qkv, L0_attn_b_qkv, L0_attn_w_out, L0_attn_b_out,
           L0_ffn_gate_cp, L0_ffn_up_cp, L0_ffn_down_cp):
    cfg = Cfg()
    inputs = dict(L0_attn_x=L0_attn_x, L0_attn_w_qkv=L0_attn_w_qkv,
                  L0_attn_b_qkv=L0_attn_b_qkv, L0_attn_w_out=L0_attn_w_out,
                  L0_attn_b_out=L0_attn_b_out, L0_ffn_gate_cp=L0_ffn_gate_cp,
                  L0_ffn_up_cp=L0_ffn_up_cp, L0_ffn_down_cp=L0_ffn_down_cp)
    in_maps = shard_inputs(cfg, inputs)
    nc = build_nc(cfg)
    res = run_bass_kernel_spmd(nc, in_maps, core_ids=list(range(8)))
    return assemble_output(cfg, list(res.results))
